# version 12
# speedup vs baseline: 1.0626x; 1.0080x over previous
from contextlib import ExitStack

import numpy as np

import concourse.bass as bass
import concourse.mybir as mybir
from concourse.bass_utils import run_bass_kernel_spmd

F32 = mybir.dt.float32
BF16 = mybir.dt.bfloat16
AF = mybir.ActivationFunctionType
ALU = mybir.AluOpType

D = 1024
DPROJ = 4256
DIN = 2048
DSTATE = 64
NH = 32
HD = 64
CONV = 2176
MLP = 4096
FH = 4096
QC = 128
EPS = 1e-5
MAGIC = float(np.float32(1.5 * 2 ** 23))
NEG = -1.0e30
N_CORES = 8
S_FULL = 16384

STAGE = "full"


def ap(t, off, dims):
    return bass.AP(t, off, [list(d) for d in dims])


_SEM_SEQ = [0]
_OPEN_CTRS = []


def _close_semctrs():
    while _OPEN_CTRS:
        _OPEN_CTRS.pop().close()


class SemCtr:

    def __init__(self, nc, stack=None):
        self.nc = nc
        self.cms = []
        self.sems = {}
        self.cnt = {}
        _SEM_SEQ[0] += 1
        self.pfx = f"b{_SEM_SEQ[0]}_"
        _OPEN_CTRS.append(self)

    def s(self, name):
        if name not in self.sems:
            cm = self.nc.semaphore(self.pfx + name)
            self.sems[name] = cm.__enter__()
            self.cms.append(cm)
            self.cnt[name] = 0
        return self.sems[name]

    def inc(self, ins, name, k=1):
        ins.then_inc(self.s(name), k)
        self.cnt[name] += k
        return self.cnt[name]

    def n(self, name):
        return self.cnt.get(name, 0)

    def close(self):
        if self.sems:
            nc = self.nc
            items = [(self.sems[n], self.cnt[n]) for n in self.sems]
            with nc.Block() as wb:
                @wb.gpsimd
                def _(e):
                    for h, cnt in items:
                        if cnt > 0:
                            e.wait_ge(h, cnt)
            with nc.Block() as cb:
                @cb.gpsimd
                def _(e):
                    for h, cnt in items:
                        e.dma_reset(range(h.num, h.num + 1))
                        e.sem_clear(h)
        for cm in reversed(self.cms):
            cm.__exit__(None, None, None)


def build_program(S, n_cores, stage=STAGE, dbg=False, rd=False):
    _SEM_SEQ[0] = 0
    _OPEN_CTRS.clear()
    NC = S // QC
    P = S + 3
    PF = S + 2
    NT = S // 512
    tiles = []
    c = 0
    while c < P:
        w = min(512, P - c)
        tiles.append((c, w))
        c += w

    nc = bass.Bass("TRN2", target_bir_lowering=False,
                   detect_race_conditions=rd)
    di = {}

    def din(name, shape, dt=F32):
        di[name] = nc.dram_tensor(name, shape, dt, kind="ExternalInput").ap()

    din("x0", [D, P])
    din("w_in", [D, DPROJ])
    din("conv_w", [CONV, 4])
    din("conv_b", [CONV, 1])
    din("dt_bias", [NH, 1])
    din("a_log", [NH, 1])
    din("d_rep", [128, 16])
    din("norm_w", [DIN, 1])
    din("w_out", [DIN, D])
    din("rms_w", [D, 1])
    din("mlp_w1", [D, MLP])
    din("mlp_b1", [MLP, 1])
    din("mlp_w2", [MLP, D])
    din("mlp_b2", [D, 1])
    din("w1t", [D, FH])
    din("fconv_w", [FH, 3])
    din("fconv_b", [FH, 1])
    din("w2t", [FH, D])
    din("identity", [128, 128])
    din("tri01", [128, 128])
    din("kvec", [n_cores, 1])
    din("maskcol", [n_cores, 1])
    din("selL", [2 * n_cores, 1])
    din("selR", [2 * n_cores, 1])
    din("halo_mask", [1, 2])

    y_out = nc.dram_tensor("y", [D, S], F32, kind="ExternalOutput").ap()

    dk = dict(kind="ExternalOutput") if dbg else {}
    x1s = nc.dram_tensor("x1s", [D, PF], F32, **dk).ap()
    h_sc = nc.dram_tensor("h_sc", [D, S], F32, **dk).ap()
    st_sc = nc.dram_tensor("st_sc", [128, 3 * S], F32, **dk).ap()
    fl_sc = nc.dram_tensor("fl_sc", [1, 8192], F32).ap()
    dkr = dk if dbg == "w" else {}
    r1s = nc.dram_tensor("r1s", [D, FH], BF16, **dkr).ap()
    r2s = nc.dram_tensor("r2s", [FH, D], BF16, **dkr).ap()
    CCW = NH * HD + 32
    cc1i = nc.dram_tensor("cc1i", [DSTATE, CCW], F32)
    cc1o = nc.dram_tensor("cc1o", [DSTATE * n_cores, CCW], F32)
    cc2i = nc.dram_tensor("cc2i", [2, D], F32)
    cc2o = nc.dram_tensor("cc2o", [2 * n_cores, D], F32)

    es = ExitStack()

    def mksb(stk):
        def f(name, shape, dt=F32):
            return stk.enter_context(nc.sbuf_tensor(name, shape, dt))
        return f

    sb = mksb(es)

    c_convw = sb("c_convw", [128, 17 * 4])
    c_convb = sb("c_convb", [128, 17])
    c_dtb = sb("c_dtb", [NH, 1])
    c_nA = sb("c_nA", [NH, 1])
    c_drep = sb("c_drep", [128, 16])
    c_nrmw = sb("c_nrmw", [128, 16])
    c_rmsw = sb("c_rmsw", [128, 8])
    c_b1 = sb("c_b1", [128, 32])
    c_b2 = sb("c_b2", [128, 8])
    c_fcw = sb("c_fcw", [128, 32 * 3])
    c_fcb = sb("c_fcb", [128, 32])
    c_id32 = sb("c_id32", [128, 128])
    c_idb = sb("c_idb", [128, 128], BF16)
    c_tri = sb("c_tri", [128, 128])
    c_ones = sb("c_ones", [128, 128], BF16)
    c_kv = sb("c_kv", [n_cores, 1])
    c_mc = sb("c_mc", [n_cores, 1])
    c_selL = sb("c_selL", [2 * n_cores, 1])
    c_selR = sb("c_selR", [2 * n_cores, 1])
    c_hm = sb("c_hm", [128, 2])
    c_zero = sb("c_zero", [64, max(QC, n_cores)])
    c_eps1 = sb("c_eps1", [128, 2])
    c_magic = sb("c_magic", [128, 1])
    c_wsq = [sb(f"c_wsq{i}", [1, 1]) for i in range(4)]
    c_w1m = sb("c_w1m", [128, 1])
    c_w2m = sb("c_w2m", [128, 1])
    c_w1s = sb("c_w1s", [128, 1])
    c_w2s = sb("c_w2s", [128, 1])

    _close_semctrs()
    b = SemCtr(nc)
    with nc.Block() as blk:

        @blk.sync
        def _(e):
            def d(o, i):
                b.inc(e.dma_start(out=o, in_=i,
                                  allow_slow_non_contiguous=True), "h", 16)
            d(c_convw[:, :], ap(di["conv_w"].tensor, 0, [[4, 128], [512, 17], [1, 4]]))
            d(c_convb[:, :], ap(di["conv_b"].tensor, 0, [[1, 128], [128, 17]]))
            d(c_dtb[:, :], di["dt_bias"])
            d(c_drep[:, :], di["d_rep"])
            d(c_nrmw[:, :], ap(di["norm_w"].tensor, 0, [[1, 128], [128, 16]]))
            d(c_rmsw[:, :], ap(di["rms_w"].tensor, 0, [[1, 128], [128, 8]]))
            d(c_b1[:, :], ap(di["mlp_b1"].tensor, 0, [[1, 128], [128, 32]]))
            d(c_b2[:, :], ap(di["mlp_b2"].tensor, 0, [[1, 128], [128, 8]]))
            d(c_fcw[:, :], ap(di["fconv_w"].tensor, 0, [[3, 128], [384, 32], [1, 3]]))
            d(c_fcb[:, :], ap(di["fconv_b"].tensor, 0, [[1, 128], [128, 32]]))
            d(c_id32[:, :], di["identity"])
            d(c_tri[:, :], di["tri01"])
            d(c_kv[:, :], di["kvec"])
            d(c_mc[:, :], di["maskcol"])
            d(c_selL[:, :], di["selL"])
            d(c_selR[:, :], di["selR"])
            d(c_hm[:, :], ap(di["halo_mask"].tensor, 0, [[1, 1], [0, 128], [1, 2]]))

        @blk.gpsimd
        def _(e):
            b.inc(e.dma_start(out=c_idb[:, :], in_=di["identity"]), "g", 16)
            b.inc(e.dma_start(out=c_nA[:, :], in_=di["a_log"]), "g", 16)

        @blk.vector
        def _(e):
            e.memset(c_ones[:, :], 1.0)
            e.memset(c_zero[:, :], 0.0)
            e.memset(c_eps1[:, 0:1], EPS)
            e.memset(c_eps1[:, 1:2], 1.0)
            e.memset(c_magic[:, :], MAGIC)

        @blk.scalar
        def _(e):
            e.wait_ge(b.s("g"), 32)
            e.activation(c_nA[:, :], c_nA[:, :], AF.Exp)

    sa = ExitStack()
    sba = mksb(sa)
    a_vf = sba("a_vf", [128, 16 * S], BF16)
    a_xs = sba("a_xs", [128, 16 * S], BF16)
    a_B = sba("a_B", [64, S], BF16)
    a_C = sba("a_C", [64, S], BF16)
    a_dt = sba("a_dt", [NH, S])
    a_lc = sba("a_lc", [NH, 80])

    s1 = ExitStack()
    sb1 = mksb(s1)
    t_st1 = sb1("t_st1", [128, 512])
    t_st2 = sb1("t_st2", [128, 512])
    t_x0b = sb1("t_x0b", [128, 8 * 512], BF16)
    t_wA = sb1("t_wA", [128, 8 * 128], BF16)
    t_wB = sb1("t_wB", [128, 8 * 128], BF16)
    t_pre = sb1("t_pre", [128, 17 * 515], BF16)
    t_cv = sb1("t_cv", [128, 512])
    t_cv2 = sb1("t_cv2", [128, 512])
    t_bcs = sb1("t_bcs", [128, 512], BF16)

    _close_semctrs()
    b = SemCtr(nc)
    with nc.Block() as blk, ExitStack() as ps:
        p_mm = [ps.enter_context(nc.psum_tensor(f"p_mm{i}", [128, 512], F32))
                for i in range(2)]
        p_s1 = ps.enter_context(nc.psum_tensor("p_s1", [128, 512], F32))
        p_s2 = ps.enter_context(nc.psum_tensor("p_s2", [128, 512], F32))
        p_dt = ps.enter_context(nc.psum_tensor("p_dt", [NH, 512], F32))
        ntl = len(tiles)

        @blk.gpsimd
        def _(e):
            for i, (c0, w) in enumerate(tiles):
                if i >= 1:
                    e.wait_ge(b.s("tdn"), i)
                b.inc(e.dma_start(
                    out=t_x0b[:, :8 * w],
                    in_=ap(di["x0"].tensor, c0, [[P, 128], [128 * P, 8], [1, w]])),
                    "gx", 16)
                for dc in range(34):
                    wdc = 128 if dc < 33 else 32
                    idx = i * 34 + dc
                    if idx >= 2:
                        e.wait_ge(b.s("mmW"), idx - 1)
                    buf = t_wA if dc % 2 == 0 else t_wB
                    b.inc(e.dma_start(
                        out=buf[:, :8 * wdc],
                        in_=ap(di["w_in"].tensor, dc * 128,
                               [[DPROJ, 128], [128 * DPROJ, 8], [1, wdc]])),
                        "gwA" if idx % 2 == 0 else "gwB", 16)
                r0 = max(c0 - 3, 0)
                wc = (c0 + w - 3) - r0
                e.wait_ge(b.s("cvS"), (i + 1) * 17)
                b.inc(e.dma_start(out=a_B[:, r0:r0 + wc], in_=t_bcs[0:64, :wc]),
                      "gb", 16)
                b.inc(e.dma_start(out=a_C[:, r0:r0 + wc], in_=t_bcs[64:128, :wc]),
                      "gb", 16)
                e.wait_ge(b.s("gb"), b.n("gb"))
                b.inc(e.engine_nop(), "bcD", 1)

        @blk.sync
        def _(e):
            for i, (c0, w) in enumerate(tiles):
                r0 = max(c0 - 3, 0)
                o = r0 - (c0 - 3)
                wr = w - o
                e.wait_ge(b.s("stR"), i + 1)
                b.inc(e.dma_start(out=st_sc[:, r0:r0 + wr],
                                  in_=t_st1[:, o:o + wr]), "sp", 16)
                b.inc(e.dma_start(out=st_sc[:, S + r0:S + r0 + wr],
                                  in_=t_st2[:, o:o + wr]), "sp", 16)

        @blk.vector
        def _(e):
            for i, (c0, w) in enumerate(tiles):
                e.wait_ge(b.s("gx"), (i + 1) * 16)
                if i >= 1:
                    e.wait_ge(b.s("bcD"), i)
                    e.wait_ge(b.s("sp"), i * 32)
                for k in range(8):
                    if k == 0:
                        if i >= 1:
                            e.wait_ge(b.s("mmS"), i)
                    else:
                        e.wait_ge(b.s("mmQ"), i * 7 + k)
                    e.tensor_tensor(t_bcs[:, :w], t_x0b[:, k * w:(k + 1) * w],
                                    t_x0b[:, k * w:(k + 1) * w], ALU.mult)
                    b.inc(e.engine_nop(), "sqk", 1)
                e.wait_ge(b.s("mmS"), i + 1)
                e.tensor_scalar(t_st1[:, :], p_s1[:, :], 1.0 / D, None, ALU.mult)
                nfn = b.inc(
                    e.tensor_scalar(t_st2[:, :], p_s2[:, :], 1.0 / D, None,
                                    ALU.mult), "stE", 1)
                e.wait_ge(b.s("stE"), nfn)
                nfn = b.inc(e.tensor_tensor(t_cv[:, :], t_st1[:, :],
                                            t_st1[:, :], ALU.mult), "fn", 1)
                e.wait_ge(b.s("fn"), nfn)
                e.tensor_tensor(t_st2[:, :], t_st2[:, :], t_cv[:, :],
                                ALU.subtract)
                b.inc(e.engine_nop(), "vok", 1)
                e.wait_ge(b.s("sqt"), i + 1)
                e.reciprocal(t_st2[:, :], t_st2[:, :])
                b.inc(e.engine_nop(), "stR", 1)
                e.wait_ge(b.s("mmX"), i + 1)
                nfn = b.inc(
                    e.tensor_tensor(t_x0b[:, :8 * w], t_x0b[:, :8 * w],
                                    ap(t_st1, 0, [[512, 128], [0, 8], [1, w]]),
                                    ALU.subtract), "fn", 1)
                e.wait_ge(b.s("fn"), nfn)
                e.tensor_tensor(t_x0b[:, :8 * w], t_x0b[:, :8 * w],
                                ap(t_st2, 0, [[512, 128], [0, 8], [1, w]]),
                                ALU.mult)
                b.inc(e.engine_nop(), "lnR", 1)
                r0 = max(c0 - 3, 0)
                wcv = (c0 + w - 3) - r0
                wpd = wcv if wcv >= 128 else 509
                for cc in range(17):
                    e.wait_ge(b.s("evD"), i * 34 + 17 + cc)
                    base = cc * 515
                    j0 = r0 - (c0 - 3)
                    cvb = t_cv if cc % 2 == 0 else t_cv2
                    if cc >= 2:
                        e.wait_ge(b.s("cvS"), i * 17 + cc - 1)
                    e.tensor_scalar(cvb[:, :wpd],
                                    t_pre[:, base + j0:base + j0 + wpd],
                                    c_convw[:, 4 * cc:4 * cc + 1],
                                    None, ALU.mult)
                    for tp in range(1, 4):
                        e.scalar_tensor_tensor(
                            cvb[:, :wpd],
                            t_pre[:, base + j0 + tp:base + j0 + tp + wpd],
                            c_convw[:, 4 * cc + tp:4 * cc + tp + 1],
                            cvb[:, :wpd], ALU.mult, ALU.add)
                    b.inc(e.engine_nop(), "cv", 1)
                if i + 1 < ntl:
                    e.wait_ge(b.s("cvS"), (i + 1) * 17)
                    for cc in range(17):
                        base = cc * 515
                        e.tensor_copy(t_pre[:, base:base + 3],
                                      t_pre[:, base + w:base + w + 3])
                e.wait_ge(b.s("evD"), (i + 1) * 34)
                b.inc(e.engine_nop(), "tdn", 1)

        @blk.tensor
        def _(e):
            for i, (c0, w) in enumerate(tiles):
                e.wait_ge(b.s("gx"), (i + 1) * 16)
                if i >= 1:
                    e.wait_ge(b.s("stE"), i)
                for k in range(8):
                    ins = e.matmul(p_s1[:, :w], c_ones[:, :],
                                   t_x0b[:, k * w:(k + 1) * w],
                                   start=(k == 0), stop=(k == 7))
                b.inc(ins, "mmX", 1)
                for k in range(8):
                    e.wait_ge(b.s("sqk"), i * 8 + k + 1)
                    ins = e.matmul(p_s2[:, :w], c_ones[:, :], t_bcs[:, :w],
                                   start=(k == 0), stop=(k == 7))
                    if k == 7:
                        b.inc(ins, "mmS", 1)
                    else:
                        b.inc(ins, "mmQ", 1)
                e.wait_ge(b.s("lnR"), i + 1)
                for dc in range(34):
                    wdc = 128 if dc < 33 else 32
                    idx = i * 34 + dc
                    e.wait_ge(b.s("gwA" if idx % 2 == 0 else "gwB"),
                              (idx // 2 + 1) * 16)
                    if idx >= 2:
                        e.wait_ge(b.s("evD"), idx - 1)
                    buf = t_wA if dc % 2 == 0 else t_wB
                    pm = p_dt if dc == 33 else p_mm[dc % 2]
                    for k in range(8):
                        ins = e.matmul(pm[:wdc, :w],
                                       buf[:, k * wdc:(k + 1) * wdc],
                                       t_x0b[:, k * w:(k + 1) * w],
                                       start=(k == 0), stop=(k == 7))
                    b.inc(ins, "mmW", 1)

        @blk.scalar
        def _(e):
            for i, (c0, w) in enumerate(tiles):
                e.wait_ge(b.s("vok"), i + 1)
                ins = e.activation(t_st2[:, :w], t_st2[:, :w], AF.Sqrt,
                                   bias=c_eps1[:, 0:1])
                b.inc(ins, "sqt", 1)
                r0 = max(c0 - 3, 0)
                o = r0 - (c0 - 3)
                wr = w - o
                for dc in range(34):
                    e.wait_ge(b.s("mmW"), i * 34 + dc + 1)
                    pm = p_dt if dc == 33 else p_mm[dc % 2]
                    if dc < 16:
                        ins = e.activation(
                            a_vf[:, dc * S + r0:dc * S + r0 + wr],
                            pm[:, o:o + wr], AF.Silu)
                    elif dc < 33:
                        cc = dc - 16
                        ins = e.copy(t_pre[:, cc * 515 + 3:cc * 515 + 3 + w],
                                     pm[:, :w])
                    else:
                        ins = e.copy(a_dt[:, r0:r0 + wr], pm[:NH, o:o + wr])
                    b.inc(ins, "evD", 1)
                wcv = (c0 + w - 3) - r0
                for cc in range(17):
                    e.wait_ge(b.s("cv"), i * 17 + cc + 1)
                    cvb = t_cv if cc % 2 == 0 else t_cv2
                    if cc < 16:
                        ins = e.activation(
                            a_xs[:, cc * S + r0:cc * S + r0 + wcv],
                            cvb[:, :wcv], AF.Silu, bias=c_convb[:, cc:cc + 1])
                    else:
                        if i >= 1:
                            e.wait_ge(b.s("bcD"), i)
                        ins = e.activation(t_bcs[:, :wcv], cvb[:, :wcv],
                                           AF.Silu, bias=c_convb[:, cc:cc + 1])
                    b.inc(ins, "cvS", 1)

    s1.close()

    s1b = ExitStack()
    sb1b = mksb(s1b)
    a_s = sb1b("a_s", [NH, S])
    a_st = sb1b("a_st", [64, NH * HD])
    a_dc = sb1b("a_dc", [64, NC * NH])
    t_xtm = sb1b("t_xtm", [128, NH * HD], BF16)
    t_bt = sb1b("t_bt", [128, DSTATE], BF16)
    t_bw = sb1b("t_bw", [128, NH * DSTATE], BF16)
    t_dwT = sb1b("t_dwT", [128, NH])
    s1w = ExitStack()
    a_w = mksb(s1w)("a_w", [NH, S])
    s1s = ExitStack()
    t_sp = mksb(s1s)("t_sp", [NH, S])

    _close_semctrs()
    b = SemCtr(nc)
    with nc.Block() as blk, ExitStack() as ps:
        p_f = ps.enter_context(nc.psum_tensor("p_f", [128, 128], F32))

        @blk.scalar
        def _(e):
            nfn = b.inc(e.activation(t_sp[:, 0:S], a_dt[:, 0:S], AF.Abs,
                                     bias=c_dtb[:, :]), "sf", 1)
            e.wait_ge(b.s("sf"), nfn)
            nfn = b.inc(e.activation(t_sp[:, 0:S], t_sp[:, 0:S], AF.Exp,
                                     scale=-1.0), "sf", 1)
            e.wait_ge(b.s("sf"), nfn)
            ins = e.activation(t_sp[:, 0:S], t_sp[:, 0:S], AF.Ln,
                               bias=c_eps1[:NH, 1:2])
            b.inc(ins, "sp", 1)
            e.wait_ge(b.s("cs"), 1)
            ins = e.activation(a_lc[:, 48:48 + NC], a_lc[:, 0:NC], AF.Exp)
            b.inc(ins, "elc", 1)
            e.wait_ge(b.s("dcol"), 1)
            for cc in range(NC):
                ins = e.activation(a_w[:, cc * QC:(cc + 1) * QC],
                                   a_s[:, cc * QC:(cc + 1) * QC],
                                   AF.Exp, scale=-1.0,
                                   bias=a_lc[:, 32 + cc:33 + cc])
            b.inc(ins, "wx", 1)

        @blk.vector
        def _(e):
            def vfence(ins):
                nfn = b.inc(ins, "fn", 1)
                e.wait_ge(b.s("fn"), nfn)
            e.wait_ge(b.s("sp"), 1)
            vfence(e.tensor_scalar(a_dt[:, 0:S], a_dt[:, 0:S], c_dtb[:, :],
                                   0.0, ALU.add, ALU.max))
            vfence(e.tensor_tensor(a_dt[:, 0:S], a_dt[:, 0:S], t_sp[:, 0:S],
                                   ALU.add))
            vfence(e.tensor_scalar(t_sp[:, 0:S], a_dt[:, 0:S], c_nA[:, :],
                                   -1.0, ALU.mult, ALU.mult))
            for cc in range(NC):
                ins = e.tensor_tensor_scan(
                    a_s[:, cc * QC:(cc + 1) * QC],
                    t_sp[:, cc * QC:(cc + 1) * QC],
                    c_zero[0:NH, :QC], 0.0, ALU.add, ALU.add)
            vfence(ins)
            vfence(e.tensor_copy(a_lc[:, 0:NC],
                                 ap(a_s, QC - 1, [[S, NH], [QC, NC]])))
            vfence(e.tensor_tensor_scan(a_lc[:, 16:16 + NC], a_lc[:, 0:NC],
                                        c_zero[0:NH, :NC], 0.0,
                                        ALU.add, ALU.add))
            b.inc(e.engine_nop(), "cs", 1)
            e.memset(a_lc[:, 32:33], 0.0)
            if NC > 1:
                ins = e.tensor_copy(a_lc[:, 33:32 + NC], a_lc[:, 16:15 + NC])
            else:
                ins = e.engine_nop()
            vfence(ins)
            e.tensor_scalar(a_lc[:, 32:32 + NC], a_lc[:, 32:32 + NC],
                            a_lc[:, 15 + NC:16 + NC], -1.0,
                            ALU.subtract, ALU.mult)
            b.inc(e.engine_nop(), "dcol", 1)
            e.wait_ge(b.s("wx"), 1)
            e.tensor_tensor(a_w[:, :], a_w[:, :], a_dt[:, 0:S], ALU.mult)
            b.inc(e.engine_nop(), "dtw", 1)
            e.wait_ge(b.s("tlc"), 1)
            e.tensor_copy(t_sp[:NC, :NH], p_f[:NC, :NH])
            b.inc(e.engine_nop(), "tlcE", 1)

        @blk.tensor
        def _(e):
            e.wait_ge(b.s("elc"), 1)
            ins = e.transpose(p_f[:NC, :NH], a_lc[:, 48:48 + NC],
                              c_id32[:NH, :NH])
            b.inc(ins, "tlc", 1)

        @blk.sync
        def _(e):
            e.wait_ge(b.s("tlcE"), 1)
            b.inc(e.dma_start(out=fl_sc[0:1, :NC * NH], in_=t_sp[:NC, :NH]),
                  "fl", 16)
            e.wait_ge(b.s("fl"), 16)
            b.inc(e.dma_start(
                out=a_dc[:, :],
                in_=ap(fl_sc.tensor, 0, [[1, 1], [0, 64], [1, NC * NH]])),
                "fl", 16)
            e.wait_ge(b.s("fl"), 32)

    if dbg and dbg != "d3":
        dbg1t = nc.dram_tensor("dbg1", [128, S], F32, kind="ExternalOutput")
        dbg1 = dbg1t.ap()
        _close_semctrs()
        b = SemCtr(nc)
        with nc.Block() as blk:

            @blk.sync
            def _(e):
                b.inc(e.dma_start(out=dbg1t.ap()[0:NH, :], in_=a_dt[:, :]),
                      "d", 16)
                b.inc(e.dma_start(out=dbg1t.ap()[NH:2 * NH, :], in_=a_s[:, :]),
                      "d", 16)
                b.inc(e.dma_start(out=dbg1t.ap()[2 * NH:3 * NH, :],
                                  in_=a_w[:, :]), "d", 16)
                for gi in range(4):
                    b.inc(e.dma_start(
                        out=dbg1t.ap()[3 * NH:4 * NH, gi * 16:gi * 16 + NC],
                        in_=a_lc[:, gi * 16:gi * 16 + NC]), "d", 16)
                e.wait_ge(b.s("d"), 112)

    s1s.close()

    s2 = ExitStack()
    sb2 = mksb(s2)
    t_sg = sb2("t_sg", [64, NH * HD])
    t_p8 = sb2("t_p8", [NH, n_cores])
    t_p8b = sb2("t_p8b", [n_cores, NH])
    t_cb = sb2("t_cb", [n_cores, NH])
    t_wbx = sb2("t_wbx", [64, n_cores * NH])

    _close_semctrs()
    b = SemCtr(nc)
    with nc.Block() as blk, ExitStack() as ps:
        p_tr = ps.enter_context(nc.psum_tensor("p_tr", [128, NH * HD], BF16))
        p_tb = ps.enter_context(nc.psum_tensor("p_tb", [128, DSTATE], BF16))
        p_tw = ps.enter_context(nc.psum_tensor("p_tw", [128, NH], F32))
        p_st = ps.enter_context(nc.psum_tensor("p_st", [64, NH * HD], F32))

        @blk.tensor
        def _(e):
            for cc in range(NC):
                if cc >= 1:
                    e.wait_ge(b.s("bw"), cc)
                for fc in range(16):
                    e.transpose(p_tr[:, fc * QC:(fc + 1) * QC],
                                a_xs[:, fc * S + cc * QC:fc * S + (cc + 1) * QC],
                                c_idb[:, :])
                e.transpose(p_tb[:, :], a_B[:, cc * QC:(cc + 1) * QC],
                            c_idb[:64, :64])
                ins = e.transpose(p_tw[:, :NH], a_w[:, cc * QC:(cc + 1) * QC],
                                  c_id32[:NH, :NH])
                b.inc(ins, "tA", 1)
                e.wait_ge(b.s("bw"), cc + 1)
                for h in range(NH):
                    ins = e.matmul(p_st[:, h * HD:(h + 1) * HD],
                                   t_bw[:, h * DSTATE:(h + 1) * DSTATE],
                                   t_xtm[:, h * HD:(h + 1) * HD],
                                   start=(cc == 0 and h % 8 == 0),
                                   stop=(cc == NC - 1 and h % 8 == 7))
                b.inc(ins, "smm", 1)
            e.wait_ge(b.s("p8s"), 1)
            ins = e.transpose(p_tw[:n_cores, :NH], t_p8[:, :n_cores],
                              c_id32[:NH, :NH])
            b.inc(ins, "tp8", 1)
            e.wait_ge(b.s("p8T"), 1)
            ins = e.matmul(p_tw[0:1, :NH], c_kv[:, :], t_p8b[:, :],
                           start=True, stop=True)
            b.inc(ins, "cmm", 1)

        @blk.vector
        def _(e):
            for cc in range(NC):
                e.wait_ge(b.s("tA"), cc + 1)
                e.tensor_copy(t_xtm[:, :], p_tr[:, :])
                e.tensor_copy(t_bt[:, :], p_tb[:, :])
                nfn = b.inc(e.tensor_copy(t_dwT[:, :NH], p_tw[:, :NH]),
                            "fn", 1)
                e.wait_ge(b.s("fn"), nfn)
                e.tensor_tensor(
                    t_bw[:, :],
                    ap(t_bt, 0, [[DSTATE, 128], [0, NH], [1, DSTATE]]),
                    ap(t_dwT, 0, [[NH, 128], [1, NH], [0, DSTATE]]),
                    ALU.mult)
                b.inc(e.engine_nop(), "bw", 1)
            e.wait_ge(b.s("lt"), 16)
            e.tensor_tensor_scan(t_p8[:, :], t_p8[:, :],
                                 c_zero[0:NH, :n_cores], 0.0, ALU.add, ALU.add)
            b.inc(e.engine_nop(), "p8s", 1)
            e.wait_ge(b.s("tp8"), 1)
            e.tensor_copy(t_p8b[:, :], p_tw[:n_cores, :NH])
            b.inc(e.engine_nop(), "p8T", 1)
            e.wait_ge(b.s("cmm"), 1)
            e.tensor_copy(t_cb[0:1, :], p_tw[0:1, :NH])
            b.inc(e.engine_nop(), "crow", 1)
            e.wait_ge(b.s("cb"), 16)
            e.scalar_tensor_tensor(t_cb[:, :], t_cb[:, :], c_mc[:, :],
                                   t_p8b[:, :], ALU.add, ALU.subtract)
            b.inc(e.engine_nop(), "warg", 1)
            nfn = b.inc(e.memset(a_st[:, :], 0.0), "fn", 1)
            e.wait_ge(b.s("fn"), nfn)
            e.wait_ge(b.s("wf"), 32)
            for j in range(n_cores):
                e.wait_ge(b.s("sgl"), (j + 1) * 16)
                nfn = b.inc(e.tensor_tensor(
                    t_sg[:, :], t_sg[:, :],
                    ap(t_wbx, j * NH, [[n_cores * NH, 64], [1, NH], [0, HD]]),
                    ALU.mult), "fn", 1)
                e.wait_ge(b.s("fn"), nfn)
                e.tensor_tensor(a_st[:, :], a_st[:, :], t_sg[:, :], ALU.add)
                b.inc(e.engine_nop(), "acc", 1)

        @blk.scalar
        def _(e):
            e.wait_ge(b.s("smm"), NC)
            ins = e.copy(t_sg[:, :], p_st[:, :])
            b.inc(ins, "stEv", 1)
            e.wait_ge(b.s("warg"), 1)
            ins = e.activation(t_cb[:, :], t_cb[:, :], AF.Exp)
            b.inc(ins, "wexp", 1)

        @blk.gpsimd
        def _(e):
            e.wait_ge(b.s("ci"), 32)
            ins = e.collective_compute(
                "AllGather", ALU.bypass,
                replica_groups=[list(range(n_cores))],
                ins=[cc1i.ap().opt()], outs=[cc1o.ap().opt()])
            b.inc(ins, "cco", 1)

        @blk.sync
        def _(e):
            b.inc(e.dma_start(out=cc1i.ap()[:, NH * HD:NH * HD + 32],
                              in_=c_zero[0:64, 0:32]), "cz", 16)
            e.wait_ge(b.s("cz"), 16)
            e.wait_ge(b.s("stEv"), 1)
            b.inc(e.dma_start(out=cc1i.ap()[:, 0:NH * HD], in_=t_sg[:, :]),
                  "ci", 16)
            b.inc(e.dma_start(out=cc1i.ap()[0:NH, NH * HD:NH * HD + 1],
                              in_=a_lc[:, 15 + NC:16 + NC],
                              allow_slow_non_contiguous=True), "ci", 16)
            e.wait_ge(b.s("cco"), 1)
            b.inc(e.dma_start(
                out=t_p8[:, :],
                in_=ap(cc1o.ap().tensor, NH * HD,
                       [[CCW, NH], [64 * CCW, n_cores]]),
                allow_slow_non_contiguous=True), "lt", 16)
            e.wait_ge(b.s("crow"), 1)
            b.inc(e.dma_start(
                out=t_cb[:, :],
                in_=ap(t_cb, 0, [[1, 1], [0, n_cores], [1, NH]])), "cb", 16)
            e.wait_ge(b.s("wexp"), 1)
            b.inc(e.dma_start(out=fl_sc[0:1, :n_cores * NH], in_=t_cb[:, :]),
                  "wf", 16)
            e.wait_ge(b.s("wf"), 16)
            b.inc(e.dma_start(
                out=t_wbx[:, :],
                in_=ap(fl_sc.tensor, 0, [[1, 1], [0, 64], [1, n_cores * NH]])),
                "wf", 16)
            for j in range(n_cores):
                if j >= 1:
                    e.wait_ge(b.s("acc"), j)
                b.inc(e.dma_start(
                    out=t_sg[:, :],
                    in_=cc1o.ap()[j * 64:(j + 1) * 64, 0:NH * HD]), "sgl", 16)

    if dbg == "d2":
        dbg2 = nc.dram_tensor("dbg2", [128, 2048 + 512], F32,
                              kind="ExternalOutput")
        _close_semctrs()
        b = SemCtr(nc)
        with nc.Block() as blk:

            @blk.sync
            def _(e):
                b.inc(e.dma_start(out=dbg2.ap()[0:64, 0:2048],
                                  in_=a_st[:, :]), "d", 16)
                b.inc(e.dma_start(out=dbg2.ap()[64:128, 2048:2048 + 256],
                                  in_=t_wbx[:, :]), "d", 16)
                b.inc(e.dma_start(out=dbg2.ap()[0:8, 2048 + 256:2048 + 288],
                                  in_=t_cb[:, :]), "d", 16)
                b.inc(e.dma_start(out=dbg2.ap()[0:32, 2048 + 288:2048 + 296],
                                  in_=t_p8[:, :]), "d", 16)
                b.inc(e.dma_start(out=dbg2.ap()[0:8, 2048 + 296:2048 + 328],
                                  in_=t_p8b[:, :]), "d", 16)
                e.wait_ge(b.s("d"), 80)

    s2.close()
    s1w.close()

    s3 = ExitStack()
    sb3 = mksb(s3)
    t_shb = sb3("t_shb", [128, 16 * QC])
    t_eA = sb3("t_eA", [128, 16 * QC], BF16)
    t_M = sb3("t_M", [128, 16 * QC], BF16)
    t_gm = sb3("t_gm", [128, QC], BF16)
    t_nsh = sb3("t_nsh", [NH, QC])
    t_se = sb3("t_se", [NH, QC])
    t_nshT = sb3("t_nshT", [128, NH])
    t_eST = sb3("t_eST", [128, NH])
    t_ye = sb3("t_ye", [128, NH * HD // 4])
    t_ytm = sb3("t_ytm", [128, NH * HD], BF16)
    t_q = sb3("t_q", [128, 16 * QC], BF16)
    t_stb = sb3("t_stb", [64, NH * HD], BF16)
    t_r1c = sb3("t_r1c", [128, QC])
    t_wc = sb3("t_wc", [NH, QC])

    _close_semctrs()
    b = SemCtr(nc)
    with nc.Block() as blk, ExitStack() as ps:
        p_tr = ps.enter_context(nc.psum_tensor("q_tr", [128, NH * HD + DSTATE], BF16))
        p_ms = ps.enter_context(nc.psum_tensor("q_ms", [128, 512], F32))
        p_sh = ps.enter_context(nc.psum_tensor("q_sh", [64, NH * HD // 2], F32))
        p_ya = ps.enter_context(nc.psum_tensor("q_ya", [128, NH * HD // 4], F32))
        p_yb = ps.enter_context(nc.psum_tensor("q_yb", [128, NH * HD // 4], F32))
        NQ = 4
        HQ = NH // NQ

        @blk.scalar
        def _(e):
            for cc in range(NC):
                if cc >= 1:
                    e.wait_ge(b.s("nsh"), cc)
                ins = e.activation(t_nsh[:, :], a_dt[:, cc * QC:(cc + 1) * QC],
                                   AF.Ln)
                b.inc(ins, "lnc", 1)
                if cc >= 1:
                    e.wait_ge(b.s("tf"), cc)
                ins = e.activation(t_se[:, :], a_s[:, cc * QC:(cc + 1) * QC],
                                   AF.Exp)
                b.inc(ins, "sec", 1)
                ins = e.activation(t_wc[:, :], a_s[:, cc * QC:(cc + 1) * QC],
                                   AF.Exp, scale=-1.0, bias=a_lc[:, cc:cc + 1])
                b.inc(ins, "wxc", 1)
                for hf in range(2):
                    e.wait_ge(b.s("t1"), 2 * cc + hf + 1)
                    if 2 * cc + hf >= 1:
                        e.wait_ge(b.s("mM"), 2 * cc + hf)
                    ins = e.activation(t_eA[:, :], t_shb[:, :], AF.Exp)
                    b.inc(ins, "eA", 1)
                e.wait_ge(b.s("rs"), cc + 1)
                if cc >= 1:
                    e.wait_ge(b.s("vfD"), cc)
                    e.wait_ge(b.s("r1o"), cc * 16)
                ins = e.activation(t_r1c[:, :], p_ms[:, 224:224 + QC], AF.Sqrt,
                                   scale=1.0 / DIN, bias=c_eps1[:, 0:1])
                b.inc(ins, "rsq", 1)

        @blk.vector
        def _(e):
            e.memset(p_ms[:, 0:96], 0.0)
            for cc in range(NC):
                q0 = cc * QC
                if cc >= 1:
                    e.wait_ge(b.s("ymm"), cc * NQ)
                e.tensor_copy(t_stb[:, :], a_st[:, :])
                b.inc(e.engine_nop(), "stb", 1)
                e.wait_ge(b.s("lnc"), cc + 1)
                e.tensor_tensor(t_nsh[:, :], t_nsh[:, :],
                                a_s[:, q0:q0 + QC], ALU.subtract)
                b.inc(e.engine_nop(), "nsh", 1)
                e.wait_ge(b.s("wxc"), cc + 1)
                e.tensor_tensor(t_wc[:, :], t_wc[:, :],
                                a_dt[:, q0:q0 + QC], ALU.mult)
                b.inc(e.engine_nop(), "dtwc", 1)
                e.wait_ge(b.s("tx"), cc + 1)
                if cc >= 1:
                    e.wait_ge(b.s("smm"), 2 * cc)
                e.tensor_copy(t_xtm[:, :], p_tr[:, :NH * HD])
                nfn = b.inc(
                    e.tensor_copy(t_bt[:, :],
                                  p_tr[:, NH * HD:NH * HD + DSTATE]),
                    "fn", 1)
                e.wait_ge(b.s("fn"), nfn)
                b.inc(e.engine_nop(), "trE", 1)
                e.wait_ge(b.s("tf"), cc + 1)
                e.tensor_copy(t_nshT[:, :], p_ms[:, 0:NH])
                e.tensor_copy(t_dwT[:, :NH], p_ms[:, 32:32 + NH])
                nfn = b.inc(e.tensor_copy(t_eST[:, :], p_ms[:, 64:64 + NH]),
                            "fn", 1)
                e.wait_ge(b.s("fn"), nfn)
                b.inc(e.engine_nop(), "tfE", 1)
                e.tensor_tensor(
                    t_bw[:, :],
                    ap(t_bt, 0, [[DSTATE, 128], [0, NH], [1, DSTATE]]),
                    ap(t_dwT, 0, [[NH, 128], [1, NH], [0, DSTATE]]),
                    ALU.mult)
                b.inc(e.engine_nop(), "bw", 1)
                e.wait_ge(b.s("g"), cc + 1)
                if cc >= 1:
                    e.wait_ge(b.s("mM"), 2 * cc)
                e.scalar_tensor_tensor(t_gm[:, :], p_ms[:, 96:96 + QC], 1.0,
                                       c_tri[:, :], ALU.mult, ALU.mult)
                b.inc(e.engine_nop(), "gmE", 1)
                for hf in range(2):
                    e.wait_ge(b.s("f2"), (2 * cc + hf + 1) * 16)
                    nfn = b.inc(e.tensor_tensor(
                        t_shb[:, :], t_shb[:, :],
                        ap(t_nshT, hf * 16, [[NH, 128], [1, 16], [0, QC]]),
                        ALU.add), "fn", 1)
                    e.wait_ge(b.s("fn"), nfn)
                    e.tensor_scalar(t_shb[:, :], t_shb[:, :], 16.0, None,
                                    ALU.min)
                    b.inc(e.engine_nop(), "t1", 1)
                    e.wait_ge(b.s("eA"), 2 * cc + hf + 1)
                    b.inc(e.engine_nop(), "shU", 1)
                for qg in range(NQ):
                    h0 = qg * HQ
                    e.wait_ge(b.s("ymm"), cc * NQ + qg + 1)
                    nfn = b.inc(e.tensor_tensor(
                        t_ye[:, :HQ * HD], p_yb[:, :],
                        ap(t_eST, h0, [[NH, 128], [1, HQ], [0, HD]]),
                        ALU.mult), "fn", 1)
                    e.wait_ge(b.s("fn"), nfn)
                    e.tensor_tensor(t_ytm[:, h0 * HD:(h0 + HQ) * HD],
                                    t_ye[:, :HQ * HD], p_ya[:, :], ALU.add)
                    b.inc(e.engine_nop(), "yev", 1)
                b.inc(e.engine_nop(), "ytm", 1)
                nfn = b.inc(e.tensor_tensor(
                    a_st[:, :], a_st[:, :],
                    ap(a_dc, cc * NH, [[NC * NH, 64], [1, NH], [0, HD]]),
                    ALU.mult), "fn", 1)
                e.wait_ge(b.s("fn"), nfn)
                e.wait_ge(b.s("smm"), 2 * cc + 1)
                nfn = b.inc(e.tensor_tensor(
                    a_st[:, 0:NH * HD // 2], a_st[:, 0:NH * HD // 2],
                    p_sh[:, :], ALU.add), "stU", 1)
                e.wait_ge(b.s("stU"), nfn)
                e.wait_ge(b.s("smm"), 2 * cc + 2)
                e.tensor_tensor(a_st[:, NH * HD // 2:], a_st[:, NH * HD // 2:],
                                p_sh[:, :], ALU.add)
                b.inc(e.engine_nop(), "stU", 1)
                e.wait_ge(b.s("ty"), cc + 1)
                if cc >= 1:
                    e.wait_ge(b.s("rs"), cc)
                vf_ap = ap(a_vf, q0, [[16 * S, 128], [S, 16], [1, QC]])
                nfn = b.inc(e.tensor_tensor(
                    t_q[:, :],
                    ap(a_xs, q0, [[16 * S, 128], [S, 16], [1, QC]]),
                    ap(c_drep, 0, [[16, 128], [1, 16], [0, QC]]), ALU.mult),
                    "fn", 1)
                e.wait_ge(b.s("fn"), nfn)
                nfn = b.inc(e.tensor_tensor(t_q[:, :], t_q[:, :],
                                            p_tr[:, :NH * HD], ALU.add),
                            "vE", 1)
                e.wait_ge(b.s("vE"), nfn)
                nfn = b.inc(e.tensor_tensor(vf_ap, vf_ap, t_q[:, :],
                                            ALU.mult), "fn", 1)
                e.wait_ge(b.s("fn"), nfn)
                e.tensor_tensor(t_q[:, :], vf_ap, vf_ap, ALU.mult)
                b.inc(e.engine_nop(), "sq", 1)
                e.wait_ge(b.s("rsq"), cc + 1)
                e.reciprocal(t_r1c[:, :], t_r1c[:, :])
                b.inc(e.engine_nop(), "r1D", 1)
                nfn = b.inc(e.engine_nop(), "fn", 1)
                e.wait_ge(b.s("fn"), nfn)
                e.tensor_tensor(
                    vf_ap, vf_ap,
                    ap(c_nrmw, 0, [[16, 128], [1, 16], [0, QC]]), ALU.mult)
                b.inc(e.engine_nop(), "vfD", 1)

        @blk.tensor
        def _(e):
            for cc in range(NC):
                q0 = cc * QC
                e.wait_ge(b.s("nsh"), cc + 1)
                e.wait_ge(b.s("dtwc"), cc + 1)
                e.wait_ge(b.s("sec"), cc + 1)
                if cc >= 1:
                    e.wait_ge(b.s("tfE"), cc)
                e.transpose(p_ms[:, 0:NH], t_nsh[:, :], c_id32[:NH, :NH])
                e.transpose(p_ms[:, 32:32 + NH], t_wc[:, :],
                            c_id32[:NH, :NH])
                ins = e.transpose(p_ms[:, 64:64 + NH], t_se[:, :],
                                  c_id32[:NH, :NH])
                b.inc(ins, "tf", 1)
                if cc >= 1:
                    e.wait_ge(b.s("gmE"), cc)
                ins = e.matmul(p_ms[:, 96:96 + QC], a_B[:, q0:q0 + QC],
                               a_C[:, q0:q0 + QC], start=True, stop=True)
                b.inc(ins, "g", 1)
                e.wait_ge(b.s("vE"), cc)
                for fc in range(16):
                    e.transpose(p_tr[:, fc * QC:(fc + 1) * QC],
                                a_xs[:, fc * S + q0:fc * S + q0 + QC],
                                c_idb[:, :])
                ins = e.transpose(p_tr[:, NH * HD:NH * HD + DSTATE],
                                  a_B[:, q0:q0 + QC], c_idb[:64, :64])
                b.inc(ins, "tx", 1)
                e.wait_ge(b.s("stb"), cc + 1)
                for qg in range(NQ):
                    h0 = qg * HQ
                    hf = qg // 2
                    e.wait_ge(b.s("mM"), 2 * cc + hf + 1)
                    if cc * NQ + qg >= 1:
                        e.wait_ge(b.s("yev"), cc * NQ + qg)
                    for h in range(h0, h0 + HQ):
                        e.matmul(p_ya[:, (h - h0) * HD:(h - h0 + 1) * HD],
                                 t_M[:, (h - hf * 16) * QC:(h - hf * 16 + 1) * QC],
                                 t_xtm[:, h * HD:(h + 1) * HD],
                                 start=(h == h0), stop=(h == h0 + HQ - 1))
                    for h in range(h0, h0 + HQ):
                        ins = e.matmul(p_yb[:, (h - h0) * HD:(h - h0 + 1) * HD],
                                       a_C[:, q0:q0 + QC],
                                       t_stb[:, h * HD:(h + 1) * HD],
                                       start=(h == h0), stop=(h == h0 + HQ - 1))
                    b.inc(ins, "ymm", 1)
                e.wait_ge(b.s("ytm"), cc + 1)
                for fc in range(16):
                    ins = e.transpose(p_tr[:, fc * QC:(fc + 1) * QC],
                                      t_ytm[:, fc * QC:(fc + 1) * QC],
                                      c_idb[:, :])
                b.inc(ins, "ty", 1)
                e.wait_ge(b.s("bw"), cc + 1)
                for hf in range(2):
                    h0 = hf * (NH // 2)
                    e.wait_ge(b.s("stU"), 2 * cc + hf)
                    for h in range(h0, h0 + NH // 2):
                        ins = e.matmul(
                            p_sh[:, (h - h0) * HD:(h - h0 + 1) * HD],
                            t_bw[:, h * DSTATE:(h + 1) * DSTATE],
                            t_xtm[:, h * HD:(h + 1) * HD],
                            start=((h - h0) % 8 == 0),
                            stop=((h - h0) % 8 == 7))
                    b.inc(ins, "smm", 1)
                e.wait_ge(b.s("sq"), cc + 1)
                if cc >= 1:
                    e.wait_ge(b.s("rsq"), cc)
                for fc in range(16):
                    ins = e.matmul(p_ms[:, 224:224 + QC], c_ones[:, :],
                                   t_q[:, fc * QC:(fc + 1) * QC],
                                   start=(fc == 0), stop=(fc == 15))
                b.inc(ins, "rs", 1)

        @blk.sync
        def _(e):
            for cc in range(NC):
                for hf in range(2):
                    ix = 2 * cc + hf
                    if ix >= 1:
                        e.wait_ge(b.s("shU"), ix)
                        if dbg == "d5" and ix == 1:
                            e.wait_ge(b.s("dd"), 32)
                    b.inc(e.dma_start(
                        out=fl_sc[0:1, :16 * QC],
                        in_=a_s[hf * 16:hf * 16 + 16, cc * QC:(cc + 1) * QC]),
                        "f1", 16)
                    e.wait_ge(b.s("f1"), (ix + 1) * 16)
                    b.inc(e.dma_start(
                        out=t_shb[:, :],
                        in_=ap(fl_sc.tensor, 0, [[1, 1], [0, 128], [1, 16 * QC]])),
                        "f2", 16)

        if dbg == "d5":
            dbg5 = nc.dram_tensor("dbg5", [128, 6 * 2048], F32,
                                  kind="ExternalOutput")

        @blk.gpsimd
        def _(e):
            if dbg == "d5":
                e.wait_ge(b.s("f1"), 16)
                b.inc(e.dma_start(
                    out=dbg5.ap()[0:1, 4 * 2048 + 128:4 * 2048 + 128 + 2048],
                    in_=fl_sc[0:1, 0:2048]), "dd", 16)
                e.wait_ge(b.s("f2"), 16)
                e.wait_ge(b.s("tfE"), 1)
                b.inc(e.dma_start(
                    out=dbg5.ap()[:, 4 * 2048 + 128 + 2048:
                                  4 * 2048 + 128 + 2048 + 32],
                    in_=t_nshT[:, :]), "dd", 16)
            for cc in range(NC):
                e.wait_ge(b.s("gmE"), cc + 1)
                for hf in range(2):
                    ix = 2 * cc + hf
                    e.wait_ge(b.s("eA"), ix + 1)
                    if 4 * cc + 2 * hf >= 1:
                        e.wait_ge(b.s("ymm"), 4 * cc + 2 * hf)
                    e.tensor_tensor(
                        t_M[:, :], t_eA[:, :],
                        ap(t_gm, 0, [[QC, 128], [0, 16], [1, QC]]),
                        ALU.mult)
                    b.inc(e.engine_nop(), "mM", 1)
                    if dbg == "d5" and cc == 0:
                        b.inc(e.dma_start(
                            out=dbg5.ap()[:, hf * 2048:(hf + 1) * 2048],
                            in_=t_M[:, :]), "dd", 16)
                        b.inc(e.dma_start(
                            out=dbg5.ap()[:, (2 + hf) * 2048:(3 + hf) * 2048],
                            in_=t_eA[:, :]), "dd", 16)
                        if hf == 0:
                            b.inc(e.dma_start(
                                out=dbg5.ap()[:, 4 * 2048:4 * 2048 + 128],
                                in_=t_gm[:, :]), "dd", 16)
                        e.wait_ge(b.s("dd"), b.n("dd"))
                e.wait_ge(b.s("r1D"), cc + 1)
                b.inc(e.dma_start(
                    out=st_sc[:, 2 * S + cc * QC:2 * S + (cc + 1) * QC],
                    in_=t_r1c[:, :]), "r1o", 16)
            e.wait_ge(b.s("r1o"), NC * 16)

    if dbg and dbg != "d1":
        dbg3 = nc.dram_tensor("dbg3", [128, 16 * S + 1024], F32,
                              kind="ExternalOutput")
        _close_semctrs()
        b = SemCtr(nc)
        with nc.Block() as blk:

            @blk.gpsimd
            def _(e):
                for fc in range(16):
                    b.inc(e.dma_start(out=dbg3.ap()[:, fc * S:(fc + 1) * S],
                                      in_=a_vf[:, fc * S:(fc + 1) * S]),
                          "d", 16)
                b.inc(e.dma_start(out=dbg3.ap()[0:64, 16 * S:16 * S + 1024],
                                  in_=a_st[:, 0:1024]), "d", 16)
                e.wait_ge(b.s("d"), 17 * 16)

    s3.close()
    s1b.close()

    s4 = ExitStack()
    sb4 = mksb(s4)
    t_x0 = sb4("m_x0", [128, 8 * 512])
    t_wA4 = sb4("m_wA", [128, 16 * 128], BF16)
    t_wB4 = sb4("m_wB", [128, 16 * 128], BF16)
    t_mu = sb4("m_mu", [128, 512])
    t_r0 = sb4("m_r0", [128, 512])
    t_r1t = sb4("m_r1", [128, 512])
    t_h = sb4("m_h", [128, 8 * 512])
    t_t1 = sb4("m_t1", [128, 512])

    _close_semctrs()
    b = SemCtr(nc)
    with nc.Block() as blk, ExitStack() as ps:
        p_a = ps.enter_context(nc.psum_tensor("m_a", [128, 512], F32))
        p_b = ps.enter_context(nc.psum_tensor("m_b", [128, 512], F32))

        @blk.gpsimd
        def _(e):
            for tt in range(NT):
                for dc in range(8):
                    idx = tt * 8 + dc
                    if idx >= 2:
                        e.wait_ge(b.s("m2"), idx - 1)
                    buf = t_wA4 if dc % 2 == 0 else t_wB4
                    b.inc(e.dma_start(
                        out=buf[:, :],
                        in_=ap(di["w_out"].tensor, dc * 128,
                               [[D, 128], [128 * D, 16], [1, 128]])),
                        "gwA" if idx % 2 == 0 else "gwB", 16)

        @blk.sync
        def _(e):
            for tt in range(NT):
                if tt >= 1:
                    e.wait_ge(b.s("x0u"), tt)
                b.inc(e.dma_start(
                    out=t_x0[:, :],
                    in_=ap(di["x0"].tensor, 3 + tt * 512,
                           [[P, 128], [128 * P, 8], [1, 512]])), "hx", 16)
                b.inc(e.dma_start(out=t_mu[:, :],
                                  in_=st_sc[:, tt * 512:tt * 512 + 512]),
                      "hx", 16)
                b.inc(e.dma_start(out=t_r0[:, :],
                                  in_=st_sc[:, S + tt * 512:S + tt * 512 + 512]),
                      "hx", 16)
                b.inc(e.dma_start(
                    out=t_r1t[:, :],
                    in_=st_sc[:, 2 * S + tt * 512:2 * S + tt * 512 + 512]),
                    "hx", 16)
                for dc in range(8):
                    e.wait_ge(b.s("hE"), tt * 8 + dc + 1)
                    b.inc(e.dma_start(
                        out=ap(h_sc.tensor, dc * 128 * S + tt * 512,
                               [[S, 128], [1, 512]]),
                        in_=t_h[:, dc * 512:(dc + 1) * 512]), "ho", 16)

        @blk.tensor
        def _(e):
            for tt in range(NT):
                for dc in range(8):
                    idx = tt * 8 + dc
                    e.wait_ge(b.s("gwA" if idx % 2 == 0 else "gwB"),
                              (idx // 2 + 1) * 16)
                    if idx >= 2:
                        e.wait_ge(b.s("m2e"), idx - 1)
                    buf = t_wA4 if dc % 2 == 0 else t_wB4
                    pm = p_a if dc % 2 == 0 else p_b
                    for k in range(16):
                        ins = e.matmul(pm[:, :], buf[:, k * 128:(k + 1) * 128],
                                       a_vf[:, k * S + tt * 512:k * S + tt * 512 + 512],
                                       start=(k == 0), stop=(k == 15))
                    b.inc(ins, "m2", 1)

        @blk.vector
        def _(e):
            for tt in range(NT):
                e.wait_ge(b.s("hx"), (tt + 1) * 64)
                if tt >= 1:
                    e.wait_ge(b.s("ho"), tt * 128)
                for dc in range(8):
                    idx = tt * 8 + dc
                    e.wait_ge(b.s("m2"), idx + 1)
                    if idx >= 1:
                        e.wait_ge(b.s("hE"), idx)
                    pm = p_a if dc % 2 == 0 else p_b
                    nfn = b.inc(e.tensor_tensor(t_t1[:, :], pm[:, :],
                                                t_r1t[:, :], ALU.mult),
                                "m2e", 1)
                    sl = slice(dc * 512, (dc + 1) * 512)
                    e.tensor_tensor(t_h[:, sl], t_x0[:, sl],
                                    t_mu[:, :], ALU.subtract)
                    e.tensor_tensor(t_h[:, sl], t_h[:, sl],
                                    t_r0[:, :], ALU.mult)
                    b.inc(e.tensor_tensor(t_h[:, sl], t_h[:, sl], t_t1[:, :],
                                          ALU.add), "hE", 1)
                b.inc(e.engine_nop(), "x0u", 1)

        @blk.scalar
        def _(e):
            e.wait_ge(b.s("ho"), NT * 128)

    s4.close()
    sa.close()

    if stage == "p2":
        _close_semctrs()
        b = SemCtr(nc)
        with nc.Block() as blk:

            @blk.gpsimd
            def _(e):
                b.inc(e.dma_start(out=y_out, in_=h_sc), "o", 16)
                e.wait_ge(b.s("o"), 16)
        _close_semctrs()
        es.close()
        return nc

    sB = ExitStack()
    sbB = mksb(sB)
    t_h5 = sbB("n_h", [128, 8 * 512])
    t_x05 = sbB("n_x0", [128, 8 * 512])
    t_hn = sbB("n_hn", [128, 8 * 512], BF16)
    t_sq5 = sbB("n_sq", [128, 8 * 512], BF16)
    t_w5A = sbB("n_w5A", [128, 8 * 128], BF16)
    t_w5B = sbB("n_w5B", [128, 8 * 128], BF16)
    t_w6A = sbB("n_w6A", [128, 32 * 128], BF16)
    t_w6B = sbB("n_w6B", [128, 32 * 128], BF16)
    t_mid = sbB("n_mid", [128, 32 * 512], BF16)
    t_st5 = sbB("n_st5", [128, 512])
    t_x1 = sbB("n_x1", [128, 8 * 512])

    _close_semctrs()
    b = SemCtr(nc)
    with nc.Block() as blk, ExitStack() as ps:
        p_a = ps.enter_context(nc.psum_tensor("n_a", [128, 512], F32))
        p_b = ps.enter_context(nc.psum_tensor("n_b", [128, 512], F32))
        p_s = ps.enter_context(nc.psum_tensor("n_s", [128, 512], F32))
        p_c = ps.enter_context(nc.psum_tensor("n_c", [128, 512], F32))
        p_d = ps.enter_context(nc.psum_tensor("n_d", [128, 512], F32))

        @blk.sync
        def _(e):
            for tt in range(NT):
                if tt >= 1:
                    e.wait_ge(b.s("ttd"), tt)
                b.inc(e.dma_start(
                    out=t_h5[:, :],
                    in_=ap(h_sc.tensor, tt * 512,
                           [[S, 128], [128 * S, 8], [1, 512]])), "hx", 16)
                b.inc(e.dma_start(
                    out=t_x05[:, :],
                    in_=ap(di["x0"].tensor, 3 + tt * 512,
                           [[P, 128], [128 * P, 8], [1, 512]])), "hx", 16)
                for dc in range(8):
                    e.wait_ge(b.s("x1e"), tt * 8 + dc + 1)
                    b.inc(e.dma_start(
                        out=ap(x1s.tensor, 1 + dc * 128 * PF + tt * 512,
                               [[PF, 128], [1, 512]]),
                        in_=t_x1[:, dc * 512:(dc + 1) * 512]), "xo", 16)
                if tt == 0:
                    e.wait_ge(b.s("x1e"), 8)
                    b.inc(e.dma_start(
                        out=ap(cc2i.ap().tensor, 0, [[1, 1], [1, 128], [128, 8]]),
                        in_=ap(t_x1, 0, [[8 * 512, 128], [512, 8]]),
                        allow_slow_non_contiguous=True), "c2", 16)
                if tt == NT - 1:
                    e.wait_ge(b.s("x1e"), NT * 8)
                    b.inc(e.dma_start(
                        out=ap(cc2i.ap().tensor, D, [[1, 1], [1, 128], [128, 8]]),
                        in_=ap(t_x1, 511, [[8 * 512, 128], [512, 8]]),
                        allow_slow_non_contiguous=True), "c2", 16)

        @blk.gpsimd
        def _(e):
            for tt in range(NT):
                for k2 in range(32):
                    idx = tt * 32 + k2
                    if idx >= 2:
                        e.wait_ge(b.s("m3"), idx - 1)
                    buf = t_w5A if k2 % 2 == 0 else t_w5B
                    b.inc(e.dma_start(
                        out=buf[:, :],
                        in_=ap(di["mlp_w1"].tensor, k2 * 128,
                               [[MLP, 128], [128 * MLP, 8], [1, 128]])),
                        "g1A" if idx % 2 == 0 else "g1B", 16)
                for dc in range(8):
                    idx = tt * 8 + dc
                    if idx >= 2:
                        e.wait_ge(b.s("m4"), idx - 1)
                    buf = t_w6A if dc % 2 == 0 else t_w6B
                    b.inc(e.dma_start(
                        out=buf[:, :],
                        in_=ap(di["mlp_w2"].tensor, dc * 128,
                               [[D, 128], [128 * D, 32], [1, 128]])),
                        "g2A" if idx % 2 == 0 else "g2B", 16)

        @blk.vector
        def _(e):
            def vfence(ins, tag="fn"):
                nfn = b.inc(ins, tag, 1)
                e.wait_ge(b.s(tag), nfn)
            for tt in range(NT):
                e.wait_ge(b.s("hx"), (tt + 1) * 32)
                vfence(e.tensor_copy(t_hn[:, :], t_h5[:, :]))
                e.tensor_tensor(t_sq5[:, :], t_hn[:, :], t_hn[:, :], ALU.mult)
                b.inc(e.engine_nop(), "hsq", 1)
                e.wait_ge(b.s("sq2"), tt + 1)
                vfence(e.reciprocal(t_st5[:, :], t_st5[:, :]))
                for dc in range(8):
                    sl = slice(dc * 512, (dc + 1) * 512)
                    ins = e.tensor_tensor(t_hn[:, sl], t_hn[:, sl],
                                          t_st5[:, :], ALU.mult)
                vfence(ins)
                for dc in range(8):
                    sl = slice(dc * 512, (dc + 1) * 512)
                    e.tensor_tensor(
                        t_hn[:, sl], t_hn[:, sl],
                        ap(c_rmsw, dc, [[8, 128], [0, 512]]), ALU.mult)
                b.inc(e.engine_nop(), "hnD", 1)
                for dc in range(8):
                    e.wait_ge(b.s("m4"), tt * 8 + dc + 1)
                    pm = p_c if dc % 2 == 0 else p_d
                    sl = slice(dc * 512, (dc + 1) * 512)
                    vfence(e.scalar_tensor_tensor(
                        t_x1[:, sl], pm[:, :], c_b2[:, dc:dc + 1],
                        t_h5[:, sl], ALU.add, ALU.add), "m4e")
                    e.tensor_tensor(t_x1[:, sl], t_x1[:, sl],
                                    t_x05[:, sl], ALU.add)
                    b.inc(e.engine_nop(), "x1e", 1)
                e.wait_ge(b.s("xo"), (tt + 1) * 128)
                if tt == 0:
                    e.wait_ge(b.s("c2"), 16)
                b.inc(e.engine_nop(), "ttd", 1)

        @blk.tensor
        def _(e):
            for tt in range(NT):
                e.wait_ge(b.s("hsq"), tt + 1)
                if tt >= 1:
                    e.wait_ge(b.s("sq2"), tt)
                for k in range(8):
                    ins = e.matmul(p_s[:, :], c_ones[:, :],
                                   t_sq5[:, k * 512:(k + 1) * 512],
                                   start=(k == 0), stop=(k == 7))
                b.inc(ins, "mmS", 1)
                e.wait_ge(b.s("hnD"), tt + 1)
                for k2 in range(32):
                    idx = tt * 32 + k2
                    e.wait_ge(b.s("g1A" if idx % 2 == 0 else "g1B"),
                              (idx // 2 + 1) * 16)
                    if idx >= 2:
                        e.wait_ge(b.s("midE"), idx - 1)
                    buf = t_w5A if k2 % 2 == 0 else t_w5B
                    pm = p_a if k2 % 2 == 0 else p_b
                    for k in range(8):
                        ins = e.matmul(pm[:, :], buf[:, k * 128:(k + 1) * 128],
                                       t_hn[:, k * 512:(k + 1) * 512],
                                       start=(k == 0), stop=(k == 7))
                    b.inc(ins, "m3", 1)
                for dc in range(8):
                    idx = tt * 8 + dc
                    e.wait_ge(b.s("g2A" if idx % 2 == 0 else "g2B"),
                              (idx // 2 + 1) * 16)
                    pm = p_c if dc % 2 == 0 else p_d
                    if dc >= 2 or tt >= 1:
                        e.wait_ge(b.s("m4e"), idx - 1)
                    buf = t_w6A if dc % 2 == 0 else t_w6B
                    for k2 in range(32):
                        if dc == 0:
                            e.wait_ge(b.s("midE"), tt * 32 + k2 + 1)
                        ins = e.matmul(pm[:, :], buf[:, k2 * 128:(k2 + 1) * 128],
                                       t_mid[:, k2 * 512:(k2 + 1) * 512],
                                       start=(k2 == 0), stop=(k2 == 31))
                    b.inc(ins, "m4", 1)

        @blk.scalar
        def _(e):
            for tt in range(NT):
                e.wait_ge(b.s("mmS"), tt + 1)
                ins = e.activation(t_st5[:, :], p_s[:, :], AF.Sqrt,
                                   scale=1.0 / D, bias=c_eps1[:, 0:1])
                b.inc(ins, "sq2", 1)
                if tt >= 1:
                    e.wait_ge(b.s("m4"), tt * 8)
                for k2 in range(32):
                    e.wait_ge(b.s("m3"), tt * 32 + k2 + 1)
                    pm = p_a if k2 % 2 == 0 else p_b
                    ins = e.activation(t_mid[:, k2 * 512:(k2 + 1) * 512],
                                       pm[:, :], AF.Silu,
                                       bias=c_b1[:, k2:k2 + 1])
                    b.inc(ins, "midE", 1)

    sB.close()

    sE = ExitStack()
    sbE = mksb(sE)
    t_sel = sbE("e_sel", [2 * n_cores, D])
    t_rL = sbE("e_rL", [1, D])
    t_rR = sbE("e_rR", [1, D])
    t_hc = sbE("e_hc", [128, 16])

    _close_semctrs()
    b = SemCtr(nc)
    with nc.Block() as blk, ExitStack() as ps:
        p_hr = ps.enter_context(nc.psum_tensor("e_hr", [1, 512], F32))
        p_hc = ps.enter_context(nc.psum_tensor("e_phc", [128, 16], F32))

        @blk.gpsimd
        def _(e):
            ins = e.collective_compute(
                "AllGather", ALU.bypass,
                replica_groups=[list(range(n_cores))],
                ins=[cc2i.ap().opt()], outs=[cc2o.ap().opt()])
            b.inc(ins, "cc", 1)

        @blk.sync
        def _(e):
            e.wait_ge(b.s("cc"), 1)
            b.inc(e.dma_start(out=t_sel[:, :], in_=cc2o.ap()[:, :]), "ld", 16)
            e.wait_ge(b.s("hcE"), 1)
            b.inc(e.dma_start(
                out=ap(x1s.tensor, 0, [[1, 1], [PF, 128], [128 * PF, 8]]),
                in_=t_hc[:, 0:8], allow_slow_non_contiguous=True), "hw", 16)
            b.inc(e.dma_start(
                out=ap(x1s.tensor, PF - 1, [[1, 1], [PF, 128], [128 * PF, 8]]),
                in_=t_hc[:, 8:16], allow_slow_non_contiguous=True), "hw", 16)
            e.wait_ge(b.s("hw"), 32)

        @blk.tensor
        def _(e):
            e.wait_ge(b.s("ld"), 16)
            for side, sel in ((0, c_selL), (1, c_selR)):
                for half in range(2):
                    if side + half >= 1:
                        e.wait_ge(b.s("rw"), 2 * side + half)
                    ins = e.matmul(p_hr[:, :], sel[:, :],
                                   t_sel[:, half * 512:(half + 1) * 512],
                                   start=True, stop=True)
                    b.inc(ins, "rmm", 1)
            e.wait_ge(b.s("rw"), 4)
            for fc in range(8):
                e.transpose(p_hc[:, fc:fc + 1],
                            t_rL[0:1, fc * 128:(fc + 1) * 128],
                            c_id32[0:1, 0:1])
            for fc in range(8):
                ins = e.transpose(p_hc[:, 8 + fc:9 + fc],
                                  t_rR[0:1, fc * 128:(fc + 1) * 128],
                                  c_id32[0:1, 0:1])
            b.inc(ins, "tr", 1)

        @blk.vector
        def _(e):
            for side in range(2):
                for half in range(2):
                    e.wait_ge(b.s("rmm"), 2 * side + half + 1)
                    e.tensor_scalar(t_sel[0:1, :], t_sel[0:1, :], 1.0,
                                    None, ALU.mult)
                    dst = t_rL if side == 0 else t_rR
                    e.tensor_copy(dst[0:1, half * 512:(half + 1) * 512],
                                  p_hr[:, :])
                    b.inc(e.engine_nop(), "rw", 1)
            e.wait_ge(b.s("tr"), 1)
            e.tensor_scalar(t_sel[0:1, :], t_sel[0:1, :], 1.0, None,
                            ALU.mult)
            nfn = b.inc(e.tensor_copy(t_hc[:, :], p_hc[:, :]), "fn", 1)
            e.wait_ge(b.s("fn"), nfn)
            e.tensor_scalar(t_hc[:, 0:8], t_hc[:, 0:8], c_hm[:, 0:1], None,
                            ALU.mult)
            e.tensor_scalar(t_hc[:, 8:16], t_hc[:, 8:16], c_hm[:, 1:2], None,
                            ALU.mult)
            b.inc(e.engine_nop(), "hcE", 1)

    sE.close()

    if stage == "x1":
        _close_semctrs()
        b = SemCtr(nc)
        with nc.Block() as blk:

            @blk.gpsimd
            def _(e):
                b.inc(e.dma_start(out=y_out, in_=x1s.tensor.ap()[:, 1:S + 1]),
                      "o", 16)
                e.wait_ge(b.s("o"), 16)
        _close_semctrs()
        es.close()
        return nc

    sC = ExitStack()
    sbC = mksb(sC)
    t_wq4 = [sbC("w_q%d" % i, [128, 4096]) for i in range(4)]
    t_wb2 = [sbC("w_b%d" % i, [128, 4096], BF16) for i in range(2)]
    t_q1A = sbC("w_1A", [128, 4096], BF16)
    t_q1B = sbC("w_1B", [128, 4096], BF16)
    t_acc = sbC("w_acc", [128, 1])
    t_ac2 = sbC("w_ac2", [128, 1])

    _close_semctrs()
    b = SemCtr(nc)
    with nc.Block() as blk, ExitStack() as ps:
        p_m = ps.enter_context(nc.psum_tensor("w_m", [1, 4], F32))
        wspecs = [(di["w1t"], r1s, D // 128, FH, c_w1m, c_w1s),
                  (di["w2t"], r2s, FH // 128, D, c_w2m, c_w2s)]
        NCH = sum(sp[2] for sp in wspecs)

        chunks = []
        for wi, (src_, dst_, nch, wid, cm, cs_) in enumerate(wspecs):
            for ch in range(nch):
                chunks.append((src_, dst_, ch, wid))

        @blk.sync
        def _(e):
            for qi, (src_, dst_, ch, wid) in enumerate(chunks):
                if qi >= 4:
                    e.wait_ge(b.s("qd"), qi - 3)
                b.inc(e.dma_start(
                    out=t_wq4[qi % 4][:, :wid],
                    in_=src_.tensor.ap()[ch * 128:(ch + 1) * 128, :]),
                    "l2%d" % (qi % 4), 16)
                if qi >= 1:
                    sj = qi - 1
                    _, dstj, chj, widj = chunks[sj]
                    e.wait_ge(b.s("qd"), sj + 1)
                    b.inc(e.dma_start(
                        out=dstj.tensor.ap()[chj * 128:(chj + 1) * 128, :],
                        in_=t_wb2[sj % 2][:, :widj]),
                        "qo%d" % (sj % 2), 16)
            sj = len(chunks) - 1
            _, dstj, chj, widj = chunks[sj]
            e.wait_ge(b.s("qd"), sj + 1)
            b.inc(e.dma_start(
                out=dstj.tensor.ap()[chj * 128:(chj + 1) * 128, :],
                in_=t_wb2[sj % 2][:, :widj]),
                "qo%d" % (sj % 2), 16)

        @blk.vector
        def _(e):
            li = 0
            for wi, (src_, dst_, nch, wid, cm, cs_) in enumerate(wspecs):
                e.memset(t_acc[:, :], 0.0)
                for ch in range(nch):
                    e.wait_ge(b.s("l1A" if li % 2 == 0 else "l1B"),
                              (li // 2 + 1) * 16)
                    buf = t_q1A if li % 2 == 0 else t_q1B
                    nfn = b.inc(
                        e.scalar_tensor_tensor(buf[:, :wid], buf[:, :wid],
                                               -1.0, buf[:, :wid],
                                               ALU.mult, ALU.max), "fn", 1)
                    e.wait_ge(b.s("fn"), nfn)
                    nfn = b.inc(e.reduce_sum(t_ac2[:, :], buf[:, :wid],
                                             mybir.AxisListType.X), "fn", 1)
                    e.wait_ge(b.s("fn"), nfn)
                    e.tensor_tensor(t_acc[:, :], t_acc[:, :], t_ac2[:, :],
                                    ALU.add)
                    b.inc(e.engine_nop(), "rd", 1)
                    li += 1
                b.inc(e.engine_nop(), "accD", 1)
                e.wait_ge(b.s("msum"), wi + 1)
                nfn = b.inc(e.tensor_copy(c_wsq[2 * wi][0:1, 0:1],
                                          p_m[0:1, 0:1]), "fn", 1)
                e.wait_ge(b.s("fn"), nfn)
                nfn = b.inc(
                    e.tensor_scalar(c_wsq[2 * wi][0:1, 0:1],
                                    c_wsq[2 * wi][0:1, 0:1],
                                    1.0 / (nch * 128 * wid), EPS,
                                    ALU.mult, ALU.max), "fn", 1)
                e.wait_ge(b.s("fn"), nfn)
                e.reciprocal(c_wsq[2 * wi + 1][0:1, 0:1], c_wsq[2 * wi][0:1, 0:1])
                b.inc(e.engine_nop(), "scD", 1)
            for qi, (src_, dst_, ch, wid) in enumerate(chunks):
                e.wait_ge(b.s("qi"), qi + 1)
                if qi >= 2:
                    e.wait_ge(b.s("qo%d" % (qi % 2)), (qi // 2) * 16)
                buf = t_wq4[qi % 4]
                nfn = b.inc(
                    e.tensor_scalar(buf[:, :wid], buf[:, :wid], MAGIC,
                                    -1.0, ALU.subtract, ALU.max), "fn", 1)
                e.wait_ge(b.s("fn"), nfn)
                e.tensor_scalar(t_wb2[qi % 2][:, :wid], buf[:, :wid], 1.0,
                                None, ALU.min)
                b.inc(e.engine_nop(), "qd", 1)

        @blk.tensor
        def _(e):
            for wi in range(2):
                e.wait_ge(b.s("accD"), wi + 1)
                if wi >= 1:
                    e.wait_ge(b.s("scD"), wi)
                ins = e.matmul(p_m[0:1, 0:1], t_acc[:, :], c_eps1[:, 1:2],
                               start=True, stop=True)
                b.inc(ins, "msum", 1)

        @blk.gpsimd
        def _(e):
            li = 0
            for wi, (src_, dst_, nch, wid, cm, cs_) in enumerate(wspecs):
                for ch in range(nch):
                    if li >= 2:
                        e.wait_ge(b.s("rd"), li - 1)
                    b.inc(e.dma_start(
                        out=(t_q1A if li % 2 == 0 else t_q1B)[:, :wid],
                        in_=src_.tensor.ap()[ch * 128:(ch + 1) * 128, :]),
                        "l1A" if li % 2 == 0 else "l1B", 16)
                    li += 1
            for wi, (src_, dst_, nch, wid, cm, cs_) in enumerate(wspecs):
                e.wait_ge(b.s("scD"), wi + 1)
                b.inc(e.dma_start(
                    out=cm[:, :],
                    in_=ap(c_wsq[2 * wi], 0, [[1, 1], [0, 128], [1, 1]])),
                    "bc%d" % wi, 16)
                b.inc(e.dma_start(
                    out=cs_[:, :],
                    in_=ap(c_wsq[2 * wi + 1], 0, [[1, 1], [0, 128], [1, 1]])),
                    "bc%d" % wi, 16)

        @blk.scalar
        def _(e):
            qi = 0
            for wi, (src_, dst_, nch, wid, cm, cs_) in enumerate(wspecs):
                e.wait_ge(b.s("bc%d" % wi), 32)
                for ch in range(nch):
                    e.wait_ge(b.s("l2%d" % (qi % 4)), (qi // 4 + 1) * 16)
                    buf = t_wq4[qi % 4]
                    ins = e.activation(buf[:, :wid], buf[:, :wid], AF.Identity,
                                       scale=cs_[:, :], bias=c_magic[:, :])
                    b.inc(ins, "qi", 1)
                    qi += 1

    sC.close()

    ftiles = []
    c = 0
    while c < PF:
        w = min(512, PF - c)
        ftiles.append((c, w))
        c += w
    HHW = S // 2 + 2
    PFP = 17 * 128

    sD = ExitStack()
    sbD = mksb(sD)
    a_q1 = sbD("f_q1", [128, 8 * PF], BF16)
    a_h1 = sbD("f_h1", [128, 32 * HHW], BF16)
    a_mx = sbD("f_mx", [128, PFP])
    a_mu2 = sbD("f_mu2", [128, PF])
    a_f1 = sbD("f_f1", [128, PF])
    a_al1 = sbD("f_al1", [128, PF])

    sDi = ExitStack()
    sbDi = mksb(sDi)
    t_xf = sbDi("f_xf", [128, 8 * 512])
    t_sqf = sbDi("f_sqf", [128, 8 * 512], BF16)
    t_mxc = sbDi("f_mxc", [128, 32])
    t_mxr = sbDi("f_mxr", [32, 128])

    _close_semctrs()
    b = SemCtr(nc)
    with nc.Block() as blk, ExitStack() as ps:
        p_s1f = ps.enter_context(nc.psum_tensor("f_s1", [128, 512], F32))
        p_s2f = ps.enter_context(nc.psum_tensor("f_s2", [128, 512], F32))
        p_tf = ps.enter_context(nc.psum_tensor("f_tf", [128, 128], F32))
        nft = len(ftiles)

        @blk.sync
        def _(e):
            for i, (c0, w) in enumerate(ftiles):
                if i >= 1:
                    e.wait_ge(b.s("ftd"), i)
                b.inc(e.dma_start(
                    out=t_xf[:, :8 * w],
                    in_=ap(x1s.tensor, c0, [[PF, 128], [128 * PF, 8], [1, w]])),
                    "hx", 16)

        @blk.gpsimd
        def _(e):
            for i, (c0, w) in enumerate(ftiles):
                if i >= 1:
                    e.wait_ge(b.s("ftd"), i)
                b.inc(e.dma_start(
                    out=ap(a_q1, c0, [[8 * PF, 128], [PF, 8], [1, w]]),
                    in_=ap(x1s.tensor, c0, [[PF, 128], [128 * PF, 8], [1, w]])),
                    "gx", 16)
            e.wait_ge(b.s("mxT"), 1)
            b.inc(e.dma_start(out=fl_sc[0:1, :PFP], in_=t_mxr[:17, :]),
                  "fm", 16)
            e.wait_ge(b.s("fm"), 16)
            b.inc(e.dma_start(
                out=a_mx[:, :],
                in_=ap(fl_sc.tensor, 0, [[1, 1], [0, 128], [1, PFP]])),
                "fm", 16)

        @blk.vector
        def _(e):
            e.memset(a_mx[:, PF:], 0.0)
            for i, (c0, w) in enumerate(ftiles):
                e.wait_ge(b.s("gx"), (i + 1) * 16)
                if i >= 1:
                    e.wait_ge(b.s("mmS"), i)
                q1ap = ap(a_q1, c0, [[8 * PF, 128], [PF, 8], [1, w]])
                e.tensor_tensor(t_sqf[:, :8 * w], q1ap, q1ap, ALU.mult)
                b.inc(e.engine_nop(), "sqD", 1)
                e.wait_ge(b.s("mmS"), i + 1)
                e.tensor_scalar(a_mu2[:, c0:c0 + w], p_s1f[:, :w], 1.0 / D,
                                None, ALU.mult)
                nfn = b.inc(
                    e.tensor_scalar(a_f1[:, c0:c0 + w], p_s2f[:, :w],
                                    1.0 / D, None, ALU.mult), "stE", 1)
                e.wait_ge(b.s("stE"), nfn)
                nfn = b.inc(e.tensor_tensor(t_sqf[:, :w], a_mu2[:, c0:c0 + w],
                                            a_mu2[:, c0:c0 + w], ALU.mult),
                            "fn", 1)
                e.wait_ge(b.s("fn"), nfn)
                e.tensor_tensor(a_f1[:, c0:c0 + w], a_f1[:, c0:c0 + w],
                                t_sqf[:, :w], ALU.subtract)
                b.inc(e.engine_nop(), "vok", 1)
                e.wait_ge(b.s("sqt"), i + 1)
                e.reciprocal(a_f1[:, c0:c0 + w], a_f1[:, c0:c0 + w])
                e.wait_ge(b.s("hx"), (i + 1) * 16)
                nfn = b.inc(
                    e.tensor_tensor(t_xf[:, :8 * w], t_xf[:, :8 * w],
                                    ap(a_mu2, c0, [[PF, 128], [0, 8], [1, w]]),
                                    ALU.subtract), "fn", 1)
                e.wait_ge(b.s("fn"), nfn)
                e.scalar_tensor_tensor(a_mx[:, c0:c0 + w], t_xf[:, :w], -1.0,
                                       t_xf[:, :w], ALU.mult, ALU.max)
                for k in range(1, 8):
                    sl = t_xf[:, k * w:(k + 1) * w]
                    e.scalar_tensor_tensor(t_sqf[:, :w], sl, -1.0,
                                           sl, ALU.mult, ALU.max)
                    e.tensor_tensor(a_mx[:, c0:c0 + w], a_mx[:, c0:c0 + w],
                                    t_sqf[:, :w], ALU.max)
                b.inc(e.engine_nop(), "mxD", 1)
                b.inc(e.engine_nop(), "ftd", 1)
            for g in range(17):
                e.wait_ge(b.s("mtr"), g + 1)
                if g == 0:
                    e.tensor_scalar(t_sqf[:, :512], t_sqf[:, :512], 1.0,
                                    None, ALU.mult)
                e.reduce_max(t_mxc[:, g:g + 1], p_tf[:, :],
                             mybir.AxisListType.X)
                b.inc(e.engine_nop(), "mxg", 1)
            e.wait_ge(b.s("mtc"), 1)
            e.tensor_scalar(t_sqf[:, :512], t_sqf[:, :512], 1.0, None,
                            ALU.mult)
            e.tensor_copy(t_mxr[:17, :], p_tf[:17, :])
            b.inc(e.engine_nop(), "mxT", 1)
            def vfence(ins, tag="fn"):
                nfn = b.inc(ins, tag, 1)
                e.wait_ge(b.s(tag), nfn)
            e.wait_ge(b.s("fm"), 32)
            vfence(e.tensor_tensor(a_mx[:, 0:PF], a_mx[:, 0:PF], a_f1[:, :],
                                   ALU.mult))
            vfence(e.tensor_scalar(a_mx[:, 0:PF], a_mx[:, 0:PF], EPS, None,
                                   ALU.max))
            e.wait_ge(b.s("w1m"), 1)
            vfence(e.tensor_scalar(a_al1[:, :], a_mx[:, 0:PF], c_w1m[:, :],
                                   1.0 / 127.0, ALU.mult, ALU.mult))
            vfence(e.tensor_scalar(a_al1[:, 0:1], a_al1[:, 0:1], c_hm[:, 0:1],
                                   None, ALU.mult))
            e.tensor_scalar(a_al1[:, PF - 1:PF], a_al1[:, PF - 1:PF],
                            c_hm[:, 1:2], None, ALU.mult)
            vfence(e.reciprocal(a_mx[:, 0:PF], a_mx[:, 0:PF]))
            vfence(e.tensor_tensor(a_f1[:, :], a_f1[:, :], a_mx[:, 0:PF],
                                   ALU.mult))
            e.tensor_scalar(a_f1[:, :], a_f1[:, :], 127.0, None, ALU.mult)
            b.inc(e.engine_nop(), "f1D", 1)
            for i, (c0, w) in enumerate(ftiles):
                e.wait_ge(b.s("hx2"), (i + 1) * 16)
                vfence(e.tensor_tensor(
                    t_xf[:, :8 * w], t_xf[:, :8 * w],
                    ap(a_mu2, c0, [[PF, 128], [0, 8], [1, w]]),
                    ALU.subtract))
                vfence(e.tensor_tensor(
                    t_xf[:, :8 * w], t_xf[:, :8 * w],
                    ap(a_f1, c0, [[PF, 128], [0, 8], [1, w]]),
                    ALU.mult))
                e.tensor_scalar(ap(a_q1, c0, [[8 * PF, 128], [PF, 8], [1, w]]),
                                t_xf[:, :8 * w], MAGIC, MAGIC,
                                ALU.add, ALU.subtract)
                b.inc(e.engine_nop(), "q1D", 1)

        @blk.tensor
        def _(e):
            for i, (c0, w) in enumerate(ftiles):
                e.wait_ge(b.s("gx"), (i + 1) * 16)
                e.wait_ge(b.s("sqD"), i + 1)
                if i >= 1:
                    e.wait_ge(b.s("stE"), i)
                q1ap3 = ap(a_q1, c0, [[8 * PF, 128], [PF, 8], [1, w]])
                for k in range(8):
                    e.matmul(p_s1f[:, :w], c_ones[:, :],
                             ap(a_q1, c0 + k * PF, [[8 * PF, 128], [1, w]]),
                             start=(k == 0), stop=(k == 7))
                for k in range(8):
                    ins = e.matmul(p_s2f[:, :w], c_ones[:, :],
                                   t_sqf[:, k * w:(k + 1) * w],
                                   start=(k == 0), stop=(k == 7))
                b.inc(ins, "mmS", 1)
            e.wait_ge(b.s("mxD"), nft)
            for g in range(17):
                if g >= 1:
                    e.wait_ge(b.s("mxg"), g)
                ins = e.transpose(p_tf[:, :], a_mx[:, g * 128:(g + 1) * 128],
                                  c_id32[:, :])
                b.inc(ins, "mtr", 1)
            e.wait_ge(b.s("mxg"), 17)
            ins = e.transpose(p_tf[:17, :], t_mxc[:, 0:17], c_id32[:, :])
            b.inc(ins, "mtc", 1)

        @blk.scalar
        def _(e):
            for i, (c0, w) in enumerate(ftiles):
                e.wait_ge(b.s("vok"), i + 1)
                ins = e.activation(a_f1[:, c0:c0 + w], a_f1[:, c0:c0 + w],
                                   AF.Sqrt, bias=c_eps1[:, 0:1])
                b.inc(ins, "sqt", 1)

        @blk.sync
        def _(e):
            for i, (c0, w) in enumerate(ftiles):
                if i == 0:
                    e.wait_ge(b.s("f1D"), 1)
                else:
                    e.wait_ge(b.s("q1D"), i)
                b.inc(e.dma_start(
                    out=t_xf[:, :8 * w],
                    in_=ap(x1s.tensor, c0, [[PF, 128], [128 * PF, 8], [1, w]])),
                    "hx2", 16)

        @blk.gpsimd
        def _(e):
            b.inc(e.engine_nop(), "w1m", 1)

    sDi.close()

    sDm = ExitStack()
    sbDm = mksb(sDm)
    t_r5A = sbDm("f_r5A", [128, 8 * 128], BF16)
    t_r5B = sbDm("f_r5B", [128, 8 * 128], BF16)
    t_r6A = sbDm("f_r6A", [128, 32 * 128], BF16)
    t_r6B = sbDm("f_r6B", [128, 32 * 128], BF16)
    HT = S // 2
    NOT = HT // 512
    t_cv5 = sbDm("f_cv5", [128, HT])
    t_mx2 = sbDm("f_mx2", [128, HT])
    t_g2 = sbDm("f_g2", [128, HT])
    t_al2 = sbDm("f_al2", [128, HT])
    t_x2 = sbDm("f_x2", [128, 8 * 512])
    t_xr = sbDm("f_xr", [128, 8 * 512])
    t_m2c = sbDm("f_m2c", [128, 16])
    t_m2r = sbDm("f_m2r", [16, 128])
    hts = [(o, min(512, HHW - o)) for o in range(0, HHW, 512)]
    NJ = len(hts)
    NG2 = HT // 128

    _close_semctrs()
    b = SemCtr(nc)
    with nc.Block() as blk, ExitStack() as ps:
        p_a = ps.enter_context(nc.psum_tensor("f_a", [128, 512], F32))
        p_b = ps.enter_context(nc.psum_tensor("f_b", [128, 512], F32))
        p_t6 = ps.enter_context(nc.psum_tensor("f_t6", [128, 128], F32))

        @blk.sync
        def _(e):
            for hl in range(2):
                for dc in range(32):
                    idx = hl * 32 + dc
                    if idx >= 2:
                        e.wait_ge(b.s("m5"), (idx - 1) * NJ)
                    buf = t_r5A if dc % 2 == 0 else t_r5B
                    b.inc(e.dma_start(
                        out=buf[:, :],
                        in_=ap(r1s.tensor, dc * 128,
                               [[FH, 128], [128 * FH, 8], [1, 128]])),
                        "g5A" if idx % 2 == 0 else "g5B", 16)
                for ot in range(NOT):
                    if hl * NOT + ot >= 1:
                        e.wait_ge(b.s("xru"), hl * NOT + ot)
                    b.inc(e.dma_start(
                        out=t_xr[:, :],
                        in_=ap(x1s.tensor, 1 + hl * HT + ot * 512,
                               [[PF, 128], [128 * PF, 8], [1, 512]])), "xr", 16)
                    for dc6 in range(8):
                        idx = (hl * NOT + ot) * 8 + dc6
                        if idx >= 2:
                            e.wait_ge(b.s("m6"), idx - 1)
                        buf = t_r6A if dc6 % 2 == 0 else t_r6B
                        b.inc(e.dma_start(
                            out=buf[:, :],
                            in_=ap(r2s.tensor, dc6 * 128,
                                   [[D, 128], [128 * D, 32], [1, 128]])),
                            "g6A" if idx % 2 == 0 else "g6B", 16)
                        e.wait_ge(b.s("x2e"), idx + 1)
                        b.inc(e.dma_start(
                            out=ap(y_out.tensor,
                                   dc6 * 128 * S + hl * HT + ot * 512,
                                   [[S, 128], [1, 512]]),
                            in_=t_x2[:, dc6 * 512:(dc6 + 1) * 512]), "yo", 16)
            e.wait_ge(b.s("yo"), b.n("yo"))

        @blk.vector
        def _(e):
            def vfence(ins, tag="fn"):
                nfn = b.inc(ins, tag, 1)
                e.wait_ge(b.s(tag), nfn)
            for hl in range(2):
                for dc in range(32):
                    for j, (o5, w5) in enumerate(hts):
                        e.wait_ge(b.s("m5"), (hl * 32 + dc) * NJ + j + 1)
                        pm = p_a if ((hl * 32 + dc) * NJ + j) % 2 == 0 else p_b
                        ins = e.tensor_tensor(
                            a_h1[:, dc * HHW + o5:dc * HHW + o5 + w5],
                            pm[:, :w5],
                            ap(a_al1, hl * HT + o5, [[PF, 128], [1, w5]]),
                            ALU.mult)
                        b.inc(ins, "h1e", 1)
                        if j == NJ - 1:
                            vfence(ins if False else e.engine_nop())
                    base = dc * HHW
                    e.tensor_scalar(t_cv5[:, :], a_h1[:, base:base + HT],
                                    c_fcw[:, 3 * dc:3 * dc + 1],
                                    None, ALU.mult)
                    e.scalar_tensor_tensor(
                        t_cv5[:, :], a_h1[:, base + 1:base + 1 + HT],
                        c_fcw[:, 3 * dc + 1:3 * dc + 2], t_cv5[:, :],
                        ALU.mult, ALU.add)
                    e.scalar_tensor_tensor(
                        t_cv5[:, :], a_h1[:, base + 2:base + 2 + HT],
                        c_fcw[:, 3 * dc + 2:3 * dc + 3], t_cv5[:, :],
                        ALU.mult, ALU.add)
                    b.inc(e.engine_nop(), "cv5", 1)
                    e.wait_ge(b.s("cs5"), hl * 32 + dc + 1)
                    if dc == 0:
                        e.scalar_tensor_tensor(
                            t_mx2[:, :], a_h1[:, base:base + HT], -1.0,
                            a_h1[:, base:base + HT], ALU.mult, ALU.max)
                    else:
                        e.scalar_tensor_tensor(
                            t_cv5[:, :], a_h1[:, base:base + HT], -1.0,
                            a_h1[:, base:base + HT], ALU.mult, ALU.max)
                        e.tensor_tensor(t_mx2[:, :], t_mx2[:, :],
                                        t_cv5[:, :], ALU.max)
                    b.inc(e.engine_nop(), "mx2D", 1)
                for g in range(NG2):
                    e.wait_ge(b.s("t6"), hl * (NG2 + 1) + g + 1)
                    if g == 0:
                        vfence(e.tensor_scalar(t_cv5[:, :], t_cv5[:, :], 1.0,
                                               None, ALU.mult))
                    e.reduce_max(t_m2c[:, g:g + 1], p_t6[:, :],
                                 mybir.AxisListType.X)
                    b.inc(e.engine_nop(), "m2g", 1)
                e.wait_ge(b.s("t6"), hl * (NG2 + 1) + NG2 + 1)
                vfence(e.tensor_scalar(t_cv5[:, :], t_cv5[:, :], 1.0, None,
                                       ALU.mult))
                e.tensor_copy(t_m2r[:NG2, :], p_t6[:NG2, :])
                b.inc(e.engine_nop(), "m2rD", 1)
                e.wait_ge(b.s("fb2"), (hl + 1) * 16)
                e.tensor_scalar(t_g2[:, :], t_g2[:, :], EPS, None, ALU.max)
                e.tensor_scalar(t_al2[:, :], t_g2[:, :], c_w2m[:, :],
                                1.0 / 127.0, ALU.mult, ALU.mult)
                e.reciprocal(t_g2[:, :], t_g2[:, :])
                e.tensor_scalar(t_g2[:, :], t_g2[:, :], 127.0, None,
                                ALU.mult)
                b.inc(e.engine_nop(), "g2D", 1)
                for dc in range(32):
                    base = dc * HHW
                    e.tensor_tensor(t_cv5[:, :], a_h1[:, base:base + HT],
                                    t_g2[:, :], ALU.mult)
                    e.tensor_scalar(a_h1[:, base:base + HT],
                                    t_cv5[:, :], MAGIC, MAGIC,
                                    ALU.add, ALU.subtract)
                    b.inc(e.engine_nop(), "q2D", 1)
                for ot in range(NOT):
                    e.wait_ge(b.s("xr"), (hl * NOT + ot + 1) * 16)
                    for dc6 in range(8):
                        idx6 = (hl * NOT + ot) * 8 + dc6
                        e.wait_ge(b.s("m6"), idx6 + 1)
                        pm = p_a if idx6 % 2 == 0 else p_b
                        sl = slice(dc6 * 512, (dc6 + 1) * 512)
                        e.tensor_tensor(
                            t_x2[:, sl], pm[:, :],
                            ap(t_al2, ot * 512, [[HT, 128], [1, 512]]),
                            ALU.mult)
                        e.tensor_tensor(t_x2[:, sl], t_x2[:, sl],
                                        t_xr[:, sl], ALU.add)
                        b.inc(e.engine_nop(), "x2e", 1)
                    e.wait_ge(b.s("yo"), (hl * NOT + ot + 1) * 128)
                    b.inc(e.engine_nop(), "xru", 1)

        @blk.scalar
        def _(e):
            for hl in range(2):
                for dc in range(32):
                    e.wait_ge(b.s("cv5"), hl * 32 + dc + 1)
                    ins = e.activation(
                        a_h1[:, dc * HHW:dc * HHW + HT], t_cv5[:, :],
                        AF.Silu, bias=c_fcb[:, dc:dc + 1])
                    b.inc(ins, "cs5", 1)

        @blk.tensor
        def _(e):
            for hl in range(2):
                for dc in range(32):
                    for j, (o5, w5) in enumerate(hts):
                        idx5 = (hl * 32 + dc) * NJ + j
                        idxg = hl * 32 + dc
                        e.wait_ge(b.s("g5A" if idxg % 2 == 0 else "g5B"),
                                  (idxg // 2 + 1) * 16)
                        if hl == 0 and dc == 0 and j == 0:
                            e.wait_ge(b.s("q1ok"), 1)
                        if idx5 >= 2:
                            e.wait_ge(b.s("h1e"), idx5 - 1)
                        pm = p_a if idx5 % 2 == 0 else p_b
                        buf = t_r5A if dc % 2 == 0 else t_r5B
                        for k in range(8):
                            ins = e.matmul(
                                pm[:, :w5], buf[:, k * 128:(k + 1) * 128],
                                ap(a_q1, hl * HT + o5 + k * PF,
                                   [[8 * PF, 128], [1, w5]]),
                                start=(k == 0), stop=(k == 7))
                        b.inc(ins, "m5", 1)
                e.wait_ge(b.s("mx2D"), (hl + 1) * 32)
                for g in range(NG2):
                    if g == 0 and hl >= 1:
                        e.wait_ge(b.s("m2rD"), hl)
                    if hl * NG2 + g >= 1:
                        e.wait_ge(b.s("m2g"), hl * NG2 + g)
                    ins = e.transpose(p_t6[:, :],
                                      t_mx2[:, g * 128:(g + 1) * 128],
                                      c_id32[:, :])
                    b.inc(ins, "t6", 1)
                e.wait_ge(b.s("m2g"), (hl + 1) * NG2)
                ins = e.transpose(p_t6[:NG2, :], t_m2c[:, 0:NG2],
                                  c_id32[:, :])
                b.inc(ins, "t6", 1)
                for ot in range(NOT):
                    for dc6 in range(8):
                        idx6 = (hl * NOT + ot) * 8 + dc6
                        e.wait_ge(b.s("g6A" if idx6 % 2 == 0 else "g6B"),
                                  (idx6 // 2 + 1) * 16)
                        if idx6 >= 2:
                            e.wait_ge(b.s("x2e"), idx6 - 1)
                        pm = p_a if idx6 % 2 == 0 else p_b
                        buf = t_r6A if dc6 % 2 == 0 else t_r6B
                        for k2 in range(32):
                            if ot == 0 and dc6 == 0:
                                e.wait_ge(b.s("q2D"), hl * 32 + k2 + 1)
                            ins = e.matmul(
                                pm[:, :], buf[:, k2 * 128:(k2 + 1) * 128],
                                a_h1[:, k2 * HHW + ot * 512:
                                     k2 * HHW + ot * 512 + 512],
                                start=(k2 == 0), stop=(k2 == 31))
                        b.inc(ins, "m6", 1)

        @blk.gpsimd
        def _(e):
            b.inc(e.engine_nop(), "q1ok", 1)
            for hl in range(2):
                e.wait_ge(b.s("m2rD"), hl + 1)
                b.inc(e.dma_start(out=fl_sc[0:1, :HT], in_=t_m2r[:NG2, :]),
                      "fb1", 16)
                e.wait_ge(b.s("fb1"), (hl + 1) * 16)
                b.inc(e.dma_start(
                    out=t_g2[:, :],
                    in_=ap(fl_sc.tensor, 0, [[1, 1], [0, 128], [1, HT]])),
                    "fb2", 16)

    _close_semctrs()
    sDm.close()
    sD.close()
    es.close()
    return nc


def make_in_maps(inputs, S, n_cores):
    x = np.asarray(inputs["L0_attn_hidden_tokens"])[0]
    xT = np.ascontiguousarray(x.T, dtype=np.float32)
    base = {
        "w_in": np.ascontiguousarray(inputs["L0_attn_W_in"], np.float32),
        "conv_w": np.ascontiguousarray(inputs["L0_attn_conv_w"], np.float32),
        "conv_b": np.ascontiguousarray(inputs["L0_attn_conv_b"], np.float32).reshape(CONV, 1),
        "dt_bias": np.ascontiguousarray(inputs["L0_attn_dt_bias"], np.float32).reshape(NH, 1),
        "a_log": np.ascontiguousarray(inputs["L0_attn_A_log"], np.float32).reshape(NH, 1),
        "d_rep": np.ascontiguousarray(
            np.asarray(inputs["L0_attn_D_param"], np.float32)
            [2 * np.arange(16)[None, :] + (np.arange(128)[:, None] // 64)]),
        "norm_w": np.ascontiguousarray(inputs["L0_attn_norm_w"], np.float32).reshape(DIN, 1),
        "w_out": np.ascontiguousarray(inputs["L0_attn_W_out"], np.float32),
        "rms_w": np.ascontiguousarray(inputs["L0_attn_rms_w"], np.float32).reshape(D, 1),
        "mlp_w1": np.ascontiguousarray(inputs["L0_attn_mlp_w1"], np.float32),
        "mlp_b1": np.ascontiguousarray(inputs["L0_attn_mlp_b1"], np.float32).reshape(MLP, 1),
        "mlp_w2": np.ascontiguousarray(inputs["L0_attn_mlp_w2"], np.float32),
        "mlp_b2": np.ascontiguousarray(inputs["L0_attn_mlp_b2"], np.float32).reshape(D, 1),
        "w1t": np.ascontiguousarray(np.asarray(inputs["L0_ffn_w1"]).T, np.float32),
        "fconv_w": np.ascontiguousarray(
            np.asarray(inputs["L0_ffn_conv_w"]).reshape(FH, 3), np.float32),
        "fconv_b": np.ascontiguousarray(inputs["L0_ffn_conv_b"], np.float32).reshape(FH, 1),
        "w2t": np.ascontiguousarray(np.asarray(inputs["L0_ffn_w2"]).T, np.float32),
        "identity": np.eye(128, dtype=np.float32),
        "tri01": np.ascontiguousarray(
            np.tril(np.ones((128, 128), np.float32)).T),
    }
    maps = []
    for c in range(n_cores):
        lo = c * S
        halo = np.zeros((D, 3), np.float32) if c == 0 else xT[:, lo - 3:lo]
        x0 = np.ascontiguousarray(np.concatenate([halo, xT[:, lo:lo + S]], 1))
        kvec = np.zeros((n_cores, 1), np.float32)
        if c > 0:
            kvec[c - 1, 0] = 1.0
        mcol = np.full((n_cores, 1), NEG, np.float32)
        mcol[:c, 0] = 0.0
        selL = np.zeros((2 * n_cores, 1), np.float32)
        selR = np.zeros((2 * n_cores, 1), np.float32)
        if c > 0:
            selL[2 * (c - 1) + 1, 0] = 1.0
        if c < n_cores - 1:
            selR[2 * (c + 1), 0] = 1.0
        hmask = np.array([[0.0 if c == 0 else 1.0,
                           0.0 if c == n_cores - 1 else 1.0]], np.float32)
        m = dict(base)
        m.update({"x0": x0, "kvec": kvec, "maskcol": mcol,
                  "selL": selL, "selR": selR, "halo_mask": hmask})
        maps.append(m)
    return maps


_PROG = {}


def kernel(**inputs):
    S = S_FULL // N_CORES
    key = (S, N_CORES)
    if key not in _PROG:
        _PROG[key] = build_program(S, N_CORES)
    maps = make_in_maps(inputs, S, N_CORES)
    res = run_bass_kernel_spmd(_PROG[key], maps, core_ids=list(range(N_CORES)))
    parts = [np.asarray(r["y"]).T for r in res.results]
    return np.ascontiguousarray(np.concatenate(parts, 0)[None], np.float32)


# version 13 (final)
# speedup vs baseline: 1.0653x; 1.0025x over previous
from contextlib import ExitStack

import numpy as np

import concourse.bass as bass
import concourse.mybir as mybir
from concourse.bass_utils import run_bass_kernel_spmd

F32 = mybir.dt.float32
BF16 = mybir.dt.bfloat16
AF = mybir.ActivationFunctionType
ALU = mybir.AluOpType

D = 1024
DPROJ = 4256
DIN = 2048
DSTATE = 64
NH = 32
HD = 64
CONV = 2176
MLP = 4096
FH = 4096
QC = 128
EPS = 1e-5
MAGIC = float(np.float32(1.5 * 2 ** 23))
NEG = -1.0e30
N_CORES = 8
S_FULL = 16384

STAGE = "full"


def ap(t, off, dims):
    return bass.AP(t, off, [list(d) for d in dims])


_SEM_SEQ = [0]
_OPEN_CTRS = []


def _close_semctrs():
    while _OPEN_CTRS:
        _OPEN_CTRS.pop().close()


class SemCtr:

    def __init__(self, nc, stack=None):
        self.nc = nc
        self.cms = []
        self.sems = {}
        self.cnt = {}
        _SEM_SEQ[0] += 1
        self.pfx = f"b{_SEM_SEQ[0]}_"
        _OPEN_CTRS.append(self)

    def s(self, name):
        if name not in self.sems:
            cm = self.nc.semaphore(self.pfx + name)
            self.sems[name] = cm.__enter__()
            self.cms.append(cm)
            self.cnt[name] = 0
        return self.sems[name]

    def inc(self, ins, name, k=1):
        ins.then_inc(self.s(name), k)
        self.cnt[name] += k
        return self.cnt[name]

    def n(self, name):
        return self.cnt.get(name, 0)

    def close(self):
        if self.sems:
            nc = self.nc
            items = [(self.sems[n], self.cnt[n]) for n in self.sems]
            with nc.Block() as wb:
                @wb.gpsimd
                def _(e):
                    for h, cnt in items:
                        if cnt > 0:
                            e.wait_ge(h, cnt)
            with nc.Block() as cb:
                @cb.gpsimd
                def _(e):
                    for h, cnt in items:
                        e.dma_reset(range(h.num, h.num + 1))
                        e.sem_clear(h)
        for cm in reversed(self.cms):
            cm.__exit__(None, None, None)


def build_program(S, n_cores, stage=STAGE, dbg=False, rd=False):
    _SEM_SEQ[0] = 0
    _OPEN_CTRS.clear()
    NC = S // QC
    P = S + 3
    PF = S + 2
    NT = S // 512
    tiles = []
    c = 0
    while c < P:
        w = min(512, P - c)
        tiles.append((c, w))
        c += w

    nc = bass.Bass("TRN2", target_bir_lowering=False,
                   detect_race_conditions=rd)
    di = {}

    def din(name, shape, dt=F32):
        di[name] = nc.dram_tensor(name, shape, dt, kind="ExternalInput").ap()

    din("x0", [D, P])
    din("w_in", [D, DPROJ])
    din("conv_w", [CONV, 4])
    din("conv_b", [CONV, 1])
    din("dt_bias", [NH, 1])
    din("a_log", [NH, 1])
    din("d_rep", [128, 16])
    din("norm_w", [DIN, 1])
    din("w_out", [DIN, D])
    din("rms_w", [D, 1])
    din("mlp_w1", [D, MLP])
    din("mlp_b1", [MLP, 1])
    din("mlp_w2", [MLP, D])
    din("mlp_b2", [D, 1])
    din("w1t", [D, FH])
    din("fconv_w", [FH, 3])
    din("fconv_b", [FH, 1])
    din("w2t", [FH, D])
    din("identity", [128, 128])
    din("tri01", [128, 128])
    din("kvec", [n_cores, 1])
    din("maskcol", [n_cores, 1])
    din("selL", [2 * n_cores, 1])
    din("selR", [2 * n_cores, 1])
    din("halo_mask", [1, 2])

    y_out = nc.dram_tensor("y", [D, S], F32, kind="ExternalOutput").ap()

    dk = dict(kind="ExternalOutput") if dbg else {}
    x1s = nc.dram_tensor("x1s", [D, PF], F32, **dk).ap()
    h_sc = nc.dram_tensor("h_sc", [D, S], F32, **dk).ap()
    st_sc = nc.dram_tensor("st_sc", [128, 3 * S], F32, **dk).ap()
    fl_sc = nc.dram_tensor("fl_sc", [1, 8192], F32).ap()
    dkr = dk if dbg == "w" else {}
    r1s = nc.dram_tensor("r1s", [D, FH], BF16, **dkr).ap()
    r2s = nc.dram_tensor("r2s", [FH, D], BF16, **dkr).ap()
    CCW = NH * HD + 32
    cc1i = nc.dram_tensor("cc1i", [DSTATE, CCW], F32)
    cc1o = nc.dram_tensor("cc1o", [DSTATE * n_cores, CCW], F32)
    cc2i = nc.dram_tensor("cc2i", [2, D], F32)
    cc2o = nc.dram_tensor("cc2o", [2 * n_cores, D], F32)

    es = ExitStack()

    def mksb(stk):
        def f(name, shape, dt=F32):
            return stk.enter_context(nc.sbuf_tensor(name, shape, dt))
        return f

    sb = mksb(es)

    c_convw = sb("c_convw", [128, 17 * 4])
    c_convb = sb("c_convb", [128, 17])
    c_dtb = sb("c_dtb", [NH, 1])
    c_nA = sb("c_nA", [NH, 1])
    c_drep = sb("c_drep", [128, 16])
    c_nrmw = sb("c_nrmw", [128, 16])
    c_rmsw = sb("c_rmsw", [128, 8])
    c_b1 = sb("c_b1", [128, 32])
    c_b2 = sb("c_b2", [128, 8])
    c_fcw = sb("c_fcw", [128, 32 * 3])
    c_fcb = sb("c_fcb", [128, 32])
    c_id32 = sb("c_id32", [128, 128])
    c_idb = sb("c_idb", [128, 128], BF16)
    c_tri = sb("c_tri", [128, 128])
    c_ones = sb("c_ones", [128, 128], BF16)
    c_kv = sb("c_kv", [n_cores, 1])
    c_mc = sb("c_mc", [n_cores, 1])
    c_selL = sb("c_selL", [2 * n_cores, 1])
    c_selR = sb("c_selR", [2 * n_cores, 1])
    c_hm = sb("c_hm", [128, 2])
    c_zero = sb("c_zero", [64, max(QC, n_cores)])
    c_eps1 = sb("c_eps1", [128, 2])
    c_magic = sb("c_magic", [128, 1])
    c_wsq = [sb(f"c_wsq{i}", [1, 1]) for i in range(4)]
    c_w1m = sb("c_w1m", [128, 1])
    c_w2m = sb("c_w2m", [128, 1])
    c_w1s = sb("c_w1s", [128, 1])
    c_w2s = sb("c_w2s", [128, 1])

    _close_semctrs()
    b = SemCtr(nc)
    with nc.Block() as blk:

        @blk.sync
        def _(e):
            def d(o, i):
                b.inc(e.dma_start(out=o, in_=i,
                                  allow_slow_non_contiguous=True), "h", 16)
            d(c_convw[:, :], ap(di["conv_w"].tensor, 0, [[4, 128], [512, 17], [1, 4]]))
            d(c_convb[:, :], ap(di["conv_b"].tensor, 0, [[1, 128], [128, 17]]))
            d(c_dtb[:, :], di["dt_bias"])
            d(c_drep[:, :], di["d_rep"])
            d(c_nrmw[:, :], ap(di["norm_w"].tensor, 0, [[1, 128], [128, 16]]))
            d(c_rmsw[:, :], ap(di["rms_w"].tensor, 0, [[1, 128], [128, 8]]))
            d(c_b1[:, :], ap(di["mlp_b1"].tensor, 0, [[1, 128], [128, 32]]))
            d(c_b2[:, :], ap(di["mlp_b2"].tensor, 0, [[1, 128], [128, 8]]))
            d(c_fcw[:, :], ap(di["fconv_w"].tensor, 0, [[3, 128], [384, 32], [1, 3]]))
            d(c_fcb[:, :], ap(di["fconv_b"].tensor, 0, [[1, 128], [128, 32]]))
            d(c_id32[:, :], di["identity"])
            d(c_tri[:, :], di["tri01"])
            d(c_kv[:, :], di["kvec"])
            d(c_mc[:, :], di["maskcol"])
            d(c_selL[:, :], di["selL"])
            d(c_selR[:, :], di["selR"])
            d(c_hm[:, :], ap(di["halo_mask"].tensor, 0, [[1, 1], [0, 128], [1, 2]]))

        @blk.gpsimd
        def _(e):
            b.inc(e.dma_start(out=c_idb[:, :], in_=di["identity"]), "g", 16)
            b.inc(e.dma_start(out=c_nA[:, :], in_=di["a_log"]), "g", 16)

        @blk.vector
        def _(e):
            e.memset(c_ones[:, :], 1.0)
            e.memset(c_zero[:, :], 0.0)
            e.memset(c_eps1[:, 0:1], EPS)
            e.memset(c_eps1[:, 1:2], 1.0)
            e.memset(c_magic[:, :], MAGIC)

        @blk.scalar
        def _(e):
            e.wait_ge(b.s("g"), 32)
            e.activation(c_nA[:, :], c_nA[:, :], AF.Exp)

    sa = ExitStack()
    sba = mksb(sa)
    a_vf = sba("a_vf", [128, 16 * S], BF16)
    a_xs = sba("a_xs", [128, 16 * S], BF16)
    a_B = sba("a_B", [64, S], BF16)
    a_C = sba("a_C", [64, S], BF16)
    a_dt = sba("a_dt", [NH, S])
    a_lc = sba("a_lc", [NH, 80])

    s1 = ExitStack()
    sb1 = mksb(s1)
    t_st1 = sb1("t_st1", [128, 512])
    t_st2 = sb1("t_st2", [128, 512])
    t_x0b = sb1("t_x0b", [128, 8 * 512], BF16)
    t_wA = sb1("t_wA", [128, 8 * 128], BF16)
    t_wB = sb1("t_wB", [128, 8 * 128], BF16)
    t_pre = sb1("t_pre", [128, 17 * 515], BF16)
    t_cv = sb1("t_cv", [128, 512])
    t_cv2 = sb1("t_cv2", [128, 512])
    t_bcs = sb1("t_bcs", [128, 512], BF16)

    _close_semctrs()
    b = SemCtr(nc)
    with nc.Block() as blk, ExitStack() as ps:
        p_mm = [ps.enter_context(nc.psum_tensor(f"p_mm{i}", [128, 512], F32))
                for i in range(2)]
        p_s1 = ps.enter_context(nc.psum_tensor("p_s1", [128, 512], F32))
        p_s2 = ps.enter_context(nc.psum_tensor("p_s2", [128, 512], F32))
        p_dt = ps.enter_context(nc.psum_tensor("p_dt", [NH, 512], F32))
        ntl = len(tiles)

        @blk.gpsimd
        def _(e):
            for i, (c0, w) in enumerate(tiles):
                if i >= 1:
                    e.wait_ge(b.s("tdn"), i)
                b.inc(e.dma_start(
                    out=t_x0b[:, :8 * w],
                    in_=ap(di["x0"].tensor, c0, [[P, 128], [128 * P, 8], [1, w]])),
                    "gx", 16)
                for dc in range(34):
                    wdc = 128 if dc < 33 else 32
                    idx = i * 34 + dc
                    if idx >= 2:
                        e.wait_ge(b.s("mmW"), idx - 1)
                    buf = t_wA if dc % 2 == 0 else t_wB
                    b.inc(e.dma_start(
                        out=buf[:, :8 * wdc],
                        in_=ap(di["w_in"].tensor, dc * 128,
                               [[DPROJ, 128], [128 * DPROJ, 8], [1, wdc]])),
                        "gwA" if idx % 2 == 0 else "gwB", 16)
                r0 = max(c0 - 3, 0)
                wc = (c0 + w - 3) - r0
                e.wait_ge(b.s("cvS"), (i + 1) * 17)
                b.inc(e.dma_start(out=a_B[:, r0:r0 + wc], in_=t_bcs[0:64, :wc]),
                      "gb", 16)
                b.inc(e.dma_start(out=a_C[:, r0:r0 + wc], in_=t_bcs[64:128, :wc]),
                      "gb", 16)
                e.wait_ge(b.s("gb"), b.n("gb"))
                b.inc(e.engine_nop(), "bcD", 1)

        @blk.sync
        def _(e):
            for i, (c0, w) in enumerate(tiles):
                r0 = max(c0 - 3, 0)
                o = r0 - (c0 - 3)
                wr = w - o
                e.wait_ge(b.s("stR"), i + 1)
                b.inc(e.dma_start(out=st_sc[:, r0:r0 + wr],
                                  in_=t_st1[:, o:o + wr]), "sp", 16)
                b.inc(e.dma_start(out=st_sc[:, S + r0:S + r0 + wr],
                                  in_=t_st2[:, o:o + wr]), "sp", 16)

        @blk.vector
        def _(e):
            for i, (c0, w) in enumerate(tiles):
                e.wait_ge(b.s("gx"), (i + 1) * 16)
                if i >= 1:
                    e.wait_ge(b.s("bcD"), i)
                    e.wait_ge(b.s("sp"), i * 32)
                for k in range(8):
                    if k == 0:
                        if i >= 1:
                            e.wait_ge(b.s("mmS"), i)
                    else:
                        e.wait_ge(b.s("mmQ"), i * 7 + k)
                    e.tensor_tensor(t_bcs[:, :w], t_x0b[:, k * w:(k + 1) * w],
                                    t_x0b[:, k * w:(k + 1) * w], ALU.mult)
                    b.inc(e.engine_nop(), "sqk", 1)
                e.wait_ge(b.s("mmS"), i + 1)
                e.tensor_scalar(t_st1[:, :], p_s1[:, :], 1.0 / D, None, ALU.mult)
                nfn = b.inc(
                    e.tensor_scalar(t_st2[:, :], p_s2[:, :], 1.0 / D, None,
                                    ALU.mult), "stE", 1)
                e.wait_ge(b.s("stE"), nfn)
                nfn = b.inc(e.tensor_tensor(t_cv[:, :], t_st1[:, :],
                                            t_st1[:, :], ALU.mult), "fn", 1)
                e.wait_ge(b.s("fn"), nfn)
                e.tensor_tensor(t_st2[:, :], t_st2[:, :], t_cv[:, :],
                                ALU.subtract)
                b.inc(e.engine_nop(), "vok", 1)
                e.wait_ge(b.s("sqt"), i + 1)
                e.reciprocal(t_st2[:, :], t_st2[:, :])
                b.inc(e.engine_nop(), "stR", 1)
                e.wait_ge(b.s("mmX"), i + 1)
                nfn = b.inc(
                    e.tensor_tensor(t_x0b[:, :8 * w], t_x0b[:, :8 * w],
                                    ap(t_st1, 0, [[512, 128], [0, 8], [1, w]]),
                                    ALU.subtract), "fn", 1)
                e.wait_ge(b.s("fn"), nfn)
                e.tensor_tensor(t_x0b[:, :8 * w], t_x0b[:, :8 * w],
                                ap(t_st2, 0, [[512, 128], [0, 8], [1, w]]),
                                ALU.mult)
                b.inc(e.engine_nop(), "lnR", 1)
                r0 = max(c0 - 3, 0)
                wcv = (c0 + w - 3) - r0
                wpd = wcv if wcv >= 128 else 509
                for cc in range(17):
                    e.wait_ge(b.s("evD"), i * 34 + 17 + cc)
                    base = cc * 515
                    j0 = r0 - (c0 - 3)
                    cvb = t_cv if cc % 2 == 0 else t_cv2
                    if cc >= 2:
                        e.wait_ge(b.s("cvS"), i * 17 + cc - 1)
                    e.tensor_scalar(cvb[:, :wpd],
                                    t_pre[:, base + j0:base + j0 + wpd],
                                    c_convw[:, 4 * cc:4 * cc + 1],
                                    None, ALU.mult)
                    for tp in range(1, 4):
                        e.scalar_tensor_tensor(
                            cvb[:, :wpd],
                            t_pre[:, base + j0 + tp:base + j0 + tp + wpd],
                            c_convw[:, 4 * cc + tp:4 * cc + tp + 1],
                            cvb[:, :wpd], ALU.mult, ALU.add)
                    b.inc(e.engine_nop(), "cv", 1)
                if i + 1 < ntl:
                    e.wait_ge(b.s("cvS"), (i + 1) * 17)
                    for cc in range(17):
                        base = cc * 515
                        e.tensor_copy(t_pre[:, base:base + 3],
                                      t_pre[:, base + w:base + w + 3])
                e.wait_ge(b.s("evD"), (i + 1) * 34)
                b.inc(e.engine_nop(), "tdn", 1)

        @blk.tensor
        def _(e):
            for i, (c0, w) in enumerate(tiles):
                e.wait_ge(b.s("gx"), (i + 1) * 16)
                if i >= 1:
                    e.wait_ge(b.s("stE"), i)
                for k in range(8):
                    ins = e.matmul(p_s1[:, :w], c_ones[:, :],
                                   t_x0b[:, k * w:(k + 1) * w],
                                   start=(k == 0), stop=(k == 7))
                b.inc(ins, "mmX", 1)
                for k in range(8):
                    e.wait_ge(b.s("sqk"), i * 8 + k + 1)
                    ins = e.matmul(p_s2[:, :w], c_ones[:, :], t_bcs[:, :w],
                                   start=(k == 0), stop=(k == 7))
                    if k == 7:
                        b.inc(ins, "mmS", 1)
                    else:
                        b.inc(ins, "mmQ", 1)
                e.wait_ge(b.s("lnR"), i + 1)
                for dc in range(34):
                    wdc = 128 if dc < 33 else 32
                    idx = i * 34 + dc
                    e.wait_ge(b.s("gwA" if idx % 2 == 0 else "gwB"),
                              (idx // 2 + 1) * 16)
                    if idx >= 2:
                        e.wait_ge(b.s("evD"), idx - 1)
                    buf = t_wA if dc % 2 == 0 else t_wB
                    pm = p_dt if dc == 33 else p_mm[dc % 2]
                    for k in range(8):
                        ins = e.matmul(pm[:wdc, :w],
                                       buf[:, k * wdc:(k + 1) * wdc],
                                       t_x0b[:, k * w:(k + 1) * w],
                                       start=(k == 0), stop=(k == 7))
                    b.inc(ins, "mmW", 1)

        @blk.scalar
        def _(e):
            for i, (c0, w) in enumerate(tiles):
                e.wait_ge(b.s("vok"), i + 1)
                ins = e.activation(t_st2[:, :w], t_st2[:, :w], AF.Sqrt,
                                   bias=c_eps1[:, 0:1])
                b.inc(ins, "sqt", 1)
                r0 = max(c0 - 3, 0)
                o = r0 - (c0 - 3)
                wr = w - o
                for dc in range(34):
                    e.wait_ge(b.s("mmW"), i * 34 + dc + 1)
                    pm = p_dt if dc == 33 else p_mm[dc % 2]
                    if dc < 16:
                        ins = e.activation(
                            a_vf[:, dc * S + r0:dc * S + r0 + wr],
                            pm[:, o:o + wr], AF.Silu)
                    elif dc < 33:
                        cc = dc - 16
                        ins = e.copy(t_pre[:, cc * 515 + 3:cc * 515 + 3 + w],
                                     pm[:, :w])
                    else:
                        ins = e.copy(a_dt[:, r0:r0 + wr], pm[:NH, o:o + wr])
                    b.inc(ins, "evD", 1)
                wcv = (c0 + w - 3) - r0
                for cc in range(17):
                    e.wait_ge(b.s("cv"), i * 17 + cc + 1)
                    cvb = t_cv if cc % 2 == 0 else t_cv2
                    if cc < 16:
                        ins = e.activation(
                            a_xs[:, cc * S + r0:cc * S + r0 + wcv],
                            cvb[:, :wcv], AF.Silu, bias=c_convb[:, cc:cc + 1])
                    else:
                        if i >= 1:
                            e.wait_ge(b.s("bcD"), i)
                        ins = e.activation(t_bcs[:, :wcv], cvb[:, :wcv],
                                           AF.Silu, bias=c_convb[:, cc:cc + 1])
                    b.inc(ins, "cvS", 1)

    s1.close()

    s1b = ExitStack()
    sb1b = mksb(s1b)
    a_s = sb1b("a_s", [NH, S])
    a_st = sb1b("a_st", [64, NH * HD])
    a_dc = sb1b("a_dc", [64, NC * NH])
    t_xtm = sb1b("t_xtm", [128, NH * HD], BF16)
    t_bt = sb1b("t_bt", [128, DSTATE], BF16)
    t_bw = sb1b("t_bw", [128, NH * DSTATE], BF16)
    t_dwT = sb1b("t_dwT", [128, NH])
    s1w = ExitStack()
    a_w = mksb(s1w)("a_w", [NH, S])
    s1s = ExitStack()
    t_sp = mksb(s1s)("t_sp", [NH, S])

    _close_semctrs()
    b = SemCtr(nc)
    with nc.Block() as blk, ExitStack() as ps:
        p_f = ps.enter_context(nc.psum_tensor("p_f", [128, 128], F32))

        @blk.scalar
        def _(e):
            nfn = b.inc(e.activation(t_sp[:, 0:S], a_dt[:, 0:S], AF.Abs,
                                     bias=c_dtb[:, :]), "sf", 1)
            e.wait_ge(b.s("sf"), nfn)
            nfn = b.inc(e.activation(t_sp[:, 0:S], t_sp[:, 0:S], AF.Exp,
                                     scale=-1.0), "sf", 1)
            e.wait_ge(b.s("sf"), nfn)
            ins = e.activation(t_sp[:, 0:S], t_sp[:, 0:S], AF.Ln,
                               bias=c_eps1[:NH, 1:2])
            b.inc(ins, "sp", 1)
            e.wait_ge(b.s("cs"), 1)
            ins = e.activation(a_lc[:, 48:48 + NC], a_lc[:, 0:NC], AF.Exp)
            b.inc(ins, "elc", 1)
            e.wait_ge(b.s("dcol"), 1)
            for cc in range(NC):
                ins = e.activation(a_w[:, cc * QC:(cc + 1) * QC],
                                   a_s[:, cc * QC:(cc + 1) * QC],
                                   AF.Exp, scale=-1.0,
                                   bias=a_lc[:, 32 + cc:33 + cc])
            b.inc(ins, "wx", 1)

        @blk.vector
        def _(e):
            def vfence(ins):
                nfn = b.inc(ins, "fn", 1)
                e.wait_ge(b.s("fn"), nfn)
            e.wait_ge(b.s("sp"), 1)
            vfence(e.tensor_scalar(a_dt[:, 0:S], a_dt[:, 0:S], c_dtb[:, :],
                                   0.0, ALU.add, ALU.max))
            vfence(e.tensor_tensor(a_dt[:, 0:S], a_dt[:, 0:S], t_sp[:, 0:S],
                                   ALU.add))
            vfence(e.tensor_scalar(t_sp[:, 0:S], a_dt[:, 0:S], c_nA[:, :],
                                   -1.0, ALU.mult, ALU.mult))
            for cc in range(NC):
                ins = e.tensor_tensor_scan(
                    a_s[:, cc * QC:(cc + 1) * QC],
                    t_sp[:, cc * QC:(cc + 1) * QC],
                    c_zero[0:NH, :QC], 0.0, ALU.add, ALU.add)
            vfence(ins)
            vfence(e.tensor_copy(a_lc[:, 0:NC],
                                 ap(a_s, QC - 1, [[S, NH], [QC, NC]])))
            vfence(e.tensor_tensor_scan(a_lc[:, 16:16 + NC], a_lc[:, 0:NC],
                                        c_zero[0:NH, :NC], 0.0,
                                        ALU.add, ALU.add))
            b.inc(e.engine_nop(), "cs", 1)
            e.memset(a_lc[:, 32:33], 0.0)
            if NC > 1:
                ins = e.tensor_copy(a_lc[:, 33:32 + NC], a_lc[:, 16:15 + NC])
            else:
                ins = e.engine_nop()
            vfence(ins)
            e.tensor_scalar(a_lc[:, 32:32 + NC], a_lc[:, 32:32 + NC],
                            a_lc[:, 15 + NC:16 + NC], -1.0,
                            ALU.subtract, ALU.mult)
            b.inc(e.engine_nop(), "dcol", 1)
            e.wait_ge(b.s("wx"), 1)
            e.tensor_tensor(a_w[:, :], a_w[:, :], a_dt[:, 0:S], ALU.mult)
            b.inc(e.engine_nop(), "dtw", 1)
            e.wait_ge(b.s("tlc"), 1)
            e.tensor_copy(t_sp[:NC, :NH], p_f[:NC, :NH])
            b.inc(e.engine_nop(), "tlcE", 1)

        @blk.tensor
        def _(e):
            e.wait_ge(b.s("elc"), 1)
            ins = e.transpose(p_f[:NC, :NH], a_lc[:, 48:48 + NC],
                              c_id32[:NH, :NH])
            b.inc(ins, "tlc", 1)

        @blk.sync
        def _(e):
            e.wait_ge(b.s("tlcE"), 1)
            b.inc(e.dma_start(out=fl_sc[0:1, :NC * NH], in_=t_sp[:NC, :NH]),
                  "fl", 16)
            e.wait_ge(b.s("fl"), 16)
            b.inc(e.dma_start(
                out=a_dc[:, :],
                in_=ap(fl_sc.tensor, 0, [[1, 1], [0, 64], [1, NC * NH]])),
                "fl", 16)
            e.wait_ge(b.s("fl"), 32)

    if dbg and dbg != "d3":
        dbg1t = nc.dram_tensor("dbg1", [128, S], F32, kind="ExternalOutput")
        dbg1 = dbg1t.ap()
        _close_semctrs()
        b = SemCtr(nc)
        with nc.Block() as blk:

            @blk.sync
            def _(e):
                b.inc(e.dma_start(out=dbg1t.ap()[0:NH, :], in_=a_dt[:, :]),
                      "d", 16)
                b.inc(e.dma_start(out=dbg1t.ap()[NH:2 * NH, :], in_=a_s[:, :]),
                      "d", 16)
                b.inc(e.dma_start(out=dbg1t.ap()[2 * NH:3 * NH, :],
                                  in_=a_w[:, :]), "d", 16)
                for gi in range(4):
                    b.inc(e.dma_start(
                        out=dbg1t.ap()[3 * NH:4 * NH, gi * 16:gi * 16 + NC],
                        in_=a_lc[:, gi * 16:gi * 16 + NC]), "d", 16)
                e.wait_ge(b.s("d"), 112)

    s1s.close()

    s2 = ExitStack()
    sb2 = mksb(s2)
    t_sg = sb2("t_sg", [64, NH * HD])
    t_p8 = sb2("t_p8", [NH, n_cores])
    t_p8b = sb2("t_p8b", [n_cores, NH])
    t_cb = sb2("t_cb", [n_cores, NH])
    t_wbx = sb2("t_wbx", [64, n_cores * NH])

    _close_semctrs()
    b = SemCtr(nc)
    with nc.Block() as blk, ExitStack() as ps:
        p_tr = ps.enter_context(nc.psum_tensor("p_tr", [128, NH * HD], BF16))
        p_tb = ps.enter_context(nc.psum_tensor("p_tb", [128, DSTATE], BF16))
        p_tw = ps.enter_context(nc.psum_tensor("p_tw", [128, NH], F32))
        p_st = ps.enter_context(nc.psum_tensor("p_st", [64, NH * HD], F32))

        @blk.tensor
        def _(e):
            for cc in range(NC):
                if cc >= 1:
                    e.wait_ge(b.s("bw"), cc)
                for fc in range(16):
                    e.transpose(p_tr[:, fc * QC:(fc + 1) * QC],
                                a_xs[:, fc * S + cc * QC:fc * S + (cc + 1) * QC],
                                c_idb[:, :])
                e.transpose(p_tb[:, :], a_B[:, cc * QC:(cc + 1) * QC],
                            c_idb[:64, :64])
                ins = e.transpose(p_tw[:, :NH], a_w[:, cc * QC:(cc + 1) * QC],
                                  c_id32[:NH, :NH])
                b.inc(ins, "tA", 1)
                e.wait_ge(b.s("bw"), cc + 1)
                for h in range(NH):
                    ins = e.matmul(p_st[:, h * HD:(h + 1) * HD],
                                   t_bw[:, h * DSTATE:(h + 1) * DSTATE],
                                   t_xtm[:, h * HD:(h + 1) * HD],
                                   start=(cc == 0 and h % 8 == 0),
                                   stop=(cc == NC - 1 and h % 8 == 7))
                b.inc(ins, "smm", 1)
            e.wait_ge(b.s("p8s"), 1)
            ins = e.transpose(p_tw[:n_cores, :NH], t_p8[:, :n_cores],
                              c_id32[:NH, :NH])
            b.inc(ins, "tp8", 1)
            e.wait_ge(b.s("p8T"), 1)
            ins = e.matmul(p_tw[0:1, :NH], c_kv[:, :], t_p8b[:, :],
                           start=True, stop=True)
            b.inc(ins, "cmm", 1)

        @blk.vector
        def _(e):
            for cc in range(NC):
                e.wait_ge(b.s("tA"), cc + 1)
                e.tensor_copy(t_xtm[:, :], p_tr[:, :])
                e.tensor_copy(t_bt[:, :], p_tb[:, :])
                nfn = b.inc(e.tensor_copy(t_dwT[:, :NH], p_tw[:, :NH]),
                            "fn", 1)
                e.wait_ge(b.s("fn"), nfn)
                e.tensor_tensor(
                    t_bw[:, :],
                    ap(t_bt, 0, [[DSTATE, 128], [0, NH], [1, DSTATE]]),
                    ap(t_dwT, 0, [[NH, 128], [1, NH], [0, DSTATE]]),
                    ALU.mult)
                b.inc(e.engine_nop(), "bw", 1)
            e.wait_ge(b.s("lt"), 16)
            e.tensor_tensor_scan(t_p8[:, :], t_p8[:, :],
                                 c_zero[0:NH, :n_cores], 0.0, ALU.add, ALU.add)
            b.inc(e.engine_nop(), "p8s", 1)
            e.wait_ge(b.s("tp8"), 1)
            e.tensor_copy(t_p8b[:, :], p_tw[:n_cores, :NH])
            b.inc(e.engine_nop(), "p8T", 1)
            e.wait_ge(b.s("cmm"), 1)
            e.tensor_copy(t_cb[0:1, :], p_tw[0:1, :NH])
            b.inc(e.engine_nop(), "crow", 1)
            e.wait_ge(b.s("cb"), 16)
            e.scalar_tensor_tensor(t_cb[:, :], t_cb[:, :], c_mc[:, :],
                                   t_p8b[:, :], ALU.add, ALU.subtract)
            b.inc(e.engine_nop(), "warg", 1)
            nfn = b.inc(e.memset(a_st[:, :], 0.0), "fn", 1)
            e.wait_ge(b.s("fn"), nfn)
            e.wait_ge(b.s("wf"), 32)
            for j in range(n_cores):
                e.wait_ge(b.s("sgl"), (j + 1) * 16)
                nfn = b.inc(e.tensor_tensor(
                    t_sg[:, :], t_sg[:, :],
                    ap(t_wbx, j * NH, [[n_cores * NH, 64], [1, NH], [0, HD]]),
                    ALU.mult), "fn", 1)
                e.wait_ge(b.s("fn"), nfn)
                e.tensor_tensor(a_st[:, :], a_st[:, :], t_sg[:, :], ALU.add)
                b.inc(e.engine_nop(), "acc", 1)

        @blk.scalar
        def _(e):
            e.wait_ge(b.s("smm"), NC)
            ins = e.copy(t_sg[:, :], p_st[:, :])
            b.inc(ins, "stEv", 1)
            e.wait_ge(b.s("warg"), 1)
            ins = e.activation(t_cb[:, :], t_cb[:, :], AF.Exp)
            b.inc(ins, "wexp", 1)

        @blk.gpsimd
        def _(e):
            e.wait_ge(b.s("ci"), 32)
            ins = e.collective_compute(
                "AllGather", ALU.bypass,
                replica_groups=[list(range(n_cores))],
                ins=[cc1i.ap().opt()], outs=[cc1o.ap().opt()])
            b.inc(ins, "cco", 1)

        @blk.sync
        def _(e):
            b.inc(e.dma_start(out=cc1i.ap()[:, NH * HD:NH * HD + 32],
                              in_=c_zero[0:64, 0:32]), "cz", 16)
            e.wait_ge(b.s("cz"), 16)
            e.wait_ge(b.s("stEv"), 1)
            b.inc(e.dma_start(out=cc1i.ap()[:, 0:NH * HD], in_=t_sg[:, :]),
                  "ci", 16)
            b.inc(e.dma_start(out=cc1i.ap()[0:NH, NH * HD:NH * HD + 1],
                              in_=a_lc[:, 15 + NC:16 + NC],
                              allow_slow_non_contiguous=True), "ci", 16)
            e.wait_ge(b.s("cco"), 1)
            b.inc(e.dma_start(
                out=t_p8[:, :],
                in_=ap(cc1o.ap().tensor, NH * HD,
                       [[CCW, NH], [64 * CCW, n_cores]]),
                allow_slow_non_contiguous=True), "lt", 16)
            e.wait_ge(b.s("crow"), 1)
            b.inc(e.dma_start(
                out=t_cb[:, :],
                in_=ap(t_cb, 0, [[1, 1], [0, n_cores], [1, NH]])), "cb", 16)
            e.wait_ge(b.s("wexp"), 1)
            b.inc(e.dma_start(out=fl_sc[0:1, :n_cores * NH], in_=t_cb[:, :]),
                  "wf", 16)
            e.wait_ge(b.s("wf"), 16)
            b.inc(e.dma_start(
                out=t_wbx[:, :],
                in_=ap(fl_sc.tensor, 0, [[1, 1], [0, 64], [1, n_cores * NH]])),
                "wf", 16)
            for j in range(n_cores):
                if j >= 1:
                    e.wait_ge(b.s("acc"), j)
                b.inc(e.dma_start(
                    out=t_sg[:, :],
                    in_=cc1o.ap()[j * 64:(j + 1) * 64, 0:NH * HD]), "sgl", 16)

    if dbg == "d2":
        dbg2 = nc.dram_tensor("dbg2", [128, 2048 + 512], F32,
                              kind="ExternalOutput")
        _close_semctrs()
        b = SemCtr(nc)
        with nc.Block() as blk:

            @blk.sync
            def _(e):
                b.inc(e.dma_start(out=dbg2.ap()[0:64, 0:2048],
                                  in_=a_st[:, :]), "d", 16)
                b.inc(e.dma_start(out=dbg2.ap()[64:128, 2048:2048 + 256],
                                  in_=t_wbx[:, :]), "d", 16)
                b.inc(e.dma_start(out=dbg2.ap()[0:8, 2048 + 256:2048 + 288],
                                  in_=t_cb[:, :]), "d", 16)
                b.inc(e.dma_start(out=dbg2.ap()[0:32, 2048 + 288:2048 + 296],
                                  in_=t_p8[:, :]), "d", 16)
                b.inc(e.dma_start(out=dbg2.ap()[0:8, 2048 + 296:2048 + 328],
                                  in_=t_p8b[:, :]), "d", 16)
                e.wait_ge(b.s("d"), 80)

    s2.close()
    s1w.close()

    s3 = ExitStack()
    sb3 = mksb(s3)
    t_shb = sb3("t_shb", [128, 16 * QC])
    t_eA = sb3("t_eA", [128, 16 * QC], BF16)
    t_M = sb3("t_M", [128, 16 * QC], BF16)
    t_gm = sb3("t_gm", [128, QC], BF16)
    t_nsh = sb3("t_nsh", [NH, QC])
    t_se = sb3("t_se", [NH, QC])
    t_nshT = sb3("t_nshT", [128, NH])
    t_eST = sb3("t_eST", [128, NH])
    t_ye = sb3("t_ye", [128, NH * HD // 4])
    t_ytm = sb3("t_ytm", [128, NH * HD], BF16)
    t_q = sb3("t_q", [128, 16 * QC], BF16)
    t_stb = sb3("t_stb", [64, NH * HD], BF16)
    t_r1c = sb3("t_r1c", [128, QC])
    t_wc = sb3("t_wc", [NH, QC])

    _close_semctrs()
    b = SemCtr(nc)
    with nc.Block() as blk, ExitStack() as ps:
        p_tr = ps.enter_context(nc.psum_tensor("q_tr", [128, NH * HD + DSTATE], BF16))
        p_ms = ps.enter_context(nc.psum_tensor("q_ms", [128, 512], F32))
        p_sh = ps.enter_context(nc.psum_tensor("q_sh", [64, NH * HD // 2], F32))
        p_ya = ps.enter_context(nc.psum_tensor("q_ya", [128, NH * HD // 4], F32))
        p_yb = ps.enter_context(nc.psum_tensor("q_yb", [128, NH * HD // 4], F32))
        NQ = 4
        HQ = NH // NQ

        @blk.scalar
        def _(e):
            for cc in range(NC):
                if cc >= 1:
                    e.wait_ge(b.s("nsh"), cc)
                ins = e.activation(t_nsh[:, :], a_dt[:, cc * QC:(cc + 1) * QC],
                                   AF.Ln)
                b.inc(ins, "lnc", 1)
                if cc >= 1:
                    e.wait_ge(b.s("tf"), cc)
                ins = e.activation(t_se[:, :], a_s[:, cc * QC:(cc + 1) * QC],
                                   AF.Exp)
                b.inc(ins, "sec", 1)
                ins = e.activation(t_wc[:, :], a_s[:, cc * QC:(cc + 1) * QC],
                                   AF.Exp, scale=-1.0, bias=a_lc[:, cc:cc + 1])
                b.inc(ins, "wxc", 1)
                for hf in range(2):
                    e.wait_ge(b.s("t1"), 2 * cc + hf + 1)
                    if 2 * cc + hf >= 1:
                        e.wait_ge(b.s("mM"), 2 * cc + hf)
                    ins = e.activation(t_eA[:, :], t_shb[:, :], AF.Exp)
                    b.inc(ins, "eA", 1)
                e.wait_ge(b.s("rs"), cc + 1)
                if cc >= 1:
                    e.wait_ge(b.s("vfD"), cc)
                    e.wait_ge(b.s("r1o"), cc * 16)
                ins = e.activation(t_r1c[:, :], p_ms[:, 224:224 + QC], AF.Sqrt,
                                   scale=1.0 / DIN, bias=c_eps1[:, 0:1])
                b.inc(ins, "rsq", 1)

        @blk.vector
        def _(e):
            e.memset(p_ms[:, 0:96], 0.0)
            for cc in range(NC):
                q0 = cc * QC
                if cc >= 1:
                    e.wait_ge(b.s("ymm"), cc * NQ)
                e.tensor_copy(t_stb[:, :], a_st[:, :])
                b.inc(e.engine_nop(), "stb", 1)
                e.wait_ge(b.s("lnc"), cc + 1)
                e.tensor_tensor(t_nsh[:, :], t_nsh[:, :],
                                a_s[:, q0:q0 + QC], ALU.subtract)
                b.inc(e.engine_nop(), "nsh", 1)
                e.wait_ge(b.s("wxc"), cc + 1)
                e.tensor_tensor(t_wc[:, :], t_wc[:, :],
                                a_dt[:, q0:q0 + QC], ALU.mult)
                b.inc(e.engine_nop(), "dtwc", 1)
                e.wait_ge(b.s("tx"), cc + 1)
                if cc >= 1:
                    e.wait_ge(b.s("smm"), 2 * cc)
                e.tensor_copy(t_xtm[:, :], p_tr[:, :NH * HD])
                nfn = b.inc(
                    e.tensor_copy(t_bt[:, :],
                                  p_tr[:, NH * HD:NH * HD + DSTATE]),
                    "fn", 1)
                e.wait_ge(b.s("fn"), nfn)
                b.inc(e.engine_nop(), "trE", 1)
                e.wait_ge(b.s("tf"), cc + 1)
                e.tensor_copy(t_nshT[:, :], p_ms[:, 0:NH])
                e.tensor_copy(t_dwT[:, :NH], p_ms[:, 32:32 + NH])
                nfn = b.inc(e.tensor_copy(t_eST[:, :], p_ms[:, 64:64 + NH]),
                            "fn", 1)
                e.wait_ge(b.s("fn"), nfn)
                b.inc(e.engine_nop(), "tfE", 1)
                e.tensor_tensor(
                    t_bw[:, :],
                    ap(t_bt, 0, [[DSTATE, 128], [0, NH], [1, DSTATE]]),
                    ap(t_dwT, 0, [[NH, 128], [1, NH], [0, DSTATE]]),
                    ALU.mult)
                b.inc(e.engine_nop(), "bw", 1)
                e.wait_ge(b.s("g"), cc + 1)
                if cc >= 1:
                    e.wait_ge(b.s("mM"), 2 * cc)
                e.scalar_tensor_tensor(t_gm[:, :], p_ms[:, 96:96 + QC], 1.0,
                                       c_tri[:, :], ALU.mult, ALU.mult)
                b.inc(e.engine_nop(), "gmE", 1)
                for hf in range(2):
                    e.wait_ge(b.s("f2"), (2 * cc + hf + 1) * 16)
                    e.tensor_tensor(
                        t_shb[:, :], t_shb[:, :],
                        ap(t_nshT, hf * 16, [[NH, 128], [1, 16], [0, QC]]),
                        ALU.add)
                    e.tensor_scalar(t_shb[:, :], t_shb[:, :], 16.0, None,
                                    ALU.min)
                    b.inc(e.engine_nop(), "t1", 1)
                    e.wait_ge(b.s("eA"), 2 * cc + hf + 1)
                    b.inc(e.engine_nop(), "shU", 1)
                for qg in range(NQ):
                    h0 = qg * HQ
                    e.wait_ge(b.s("ymm"), cc * NQ + qg + 1)
                    e.tensor_tensor(
                        t_ye[:, :HQ * HD], p_yb[:, :],
                        ap(t_eST, h0, [[NH, 128], [1, HQ], [0, HD]]),
                        ALU.mult)
                    e.tensor_tensor(t_ytm[:, h0 * HD:(h0 + HQ) * HD],
                                    t_ye[:, :HQ * HD], p_ya[:, :], ALU.add)
                    b.inc(e.engine_nop(), "yev", 1)
                b.inc(e.engine_nop(), "ytm", 1)
                e.tensor_tensor(
                    a_st[:, :], a_st[:, :],
                    ap(a_dc, cc * NH, [[NC * NH, 64], [1, NH], [0, HD]]),
                    ALU.mult)
                e.wait_ge(b.s("smm"), 2 * cc + 1)
                b.inc(e.tensor_tensor(
                    a_st[:, 0:NH * HD // 2], a_st[:, 0:NH * HD // 2],
                    p_sh[:, :], ALU.add), "stU", 1)
                e.wait_ge(b.s("smm"), 2 * cc + 2)
                e.tensor_tensor(a_st[:, NH * HD // 2:], a_st[:, NH * HD // 2:],
                                p_sh[:, :], ALU.add)
                b.inc(e.engine_nop(), "stU", 1)
                e.wait_ge(b.s("ty"), cc + 1)
                if cc >= 1:
                    e.wait_ge(b.s("rs"), cc)
                vf_ap = ap(a_vf, q0, [[16 * S, 128], [S, 16], [1, QC]])
                e.tensor_tensor(
                    t_q[:, :],
                    ap(a_xs, q0, [[16 * S, 128], [S, 16], [1, QC]]),
                    ap(c_drep, 0, [[16, 128], [1, 16], [0, QC]]), ALU.mult)
                b.inc(e.tensor_tensor(t_q[:, :], t_q[:, :],
                                      p_tr[:, :NH * HD], ALU.add), "vE", 1)
                e.tensor_tensor(vf_ap, vf_ap, t_q[:, :], ALU.mult)
                e.tensor_tensor(t_q[:, :], vf_ap, vf_ap, ALU.mult)
                b.inc(e.engine_nop(), "sq", 1)
                e.wait_ge(b.s("rsq"), cc + 1)
                e.reciprocal(t_r1c[:, :], t_r1c[:, :])
                b.inc(e.engine_nop(), "r1D", 1)
                e.tensor_tensor(
                    vf_ap, vf_ap,
                    ap(c_nrmw, 0, [[16, 128], [1, 16], [0, QC]]), ALU.mult)
                b.inc(e.engine_nop(), "vfD", 1)

        @blk.tensor
        def _(e):
            for cc in range(NC):
                q0 = cc * QC
                e.wait_ge(b.s("nsh"), cc + 1)
                e.wait_ge(b.s("dtwc"), cc + 1)
                e.wait_ge(b.s("sec"), cc + 1)
                if cc >= 1:
                    e.wait_ge(b.s("tfE"), cc)
                e.transpose(p_ms[:, 0:NH], t_nsh[:, :], c_id32[:NH, :NH])
                e.transpose(p_ms[:, 32:32 + NH], t_wc[:, :],
                            c_id32[:NH, :NH])
                ins = e.transpose(p_ms[:, 64:64 + NH], t_se[:, :],
                                  c_id32[:NH, :NH])
                b.inc(ins, "tf", 1)
                if cc >= 1:
                    e.wait_ge(b.s("gmE"), cc)
                ins = e.matmul(p_ms[:, 96:96 + QC], a_B[:, q0:q0 + QC],
                               a_C[:, q0:q0 + QC], start=True, stop=True)
                b.inc(ins, "g", 1)
                e.wait_ge(b.s("vE"), cc)
                for fc in range(16):
                    e.transpose(p_tr[:, fc * QC:(fc + 1) * QC],
                                a_xs[:, fc * S + q0:fc * S + q0 + QC],
                                c_idb[:, :])
                ins = e.transpose(p_tr[:, NH * HD:NH * HD + DSTATE],
                                  a_B[:, q0:q0 + QC], c_idb[:64, :64])
                b.inc(ins, "tx", 1)
                e.wait_ge(b.s("stb"), cc + 1)
                for qg in range(NQ):
                    h0 = qg * HQ
                    hf = qg // 2
                    e.wait_ge(b.s("mM"), 2 * cc + hf + 1)
                    if cc * NQ + qg >= 1:
                        e.wait_ge(b.s("yev"), cc * NQ + qg)
                    for h in range(h0, h0 + HQ):
                        e.matmul(p_ya[:, (h - h0) * HD:(h - h0 + 1) * HD],
                                 t_M[:, (h - hf * 16) * QC:(h - hf * 16 + 1) * QC],
                                 t_xtm[:, h * HD:(h + 1) * HD],
                                 start=(h == h0), stop=(h == h0 + HQ - 1))
                    for h in range(h0, h0 + HQ):
                        ins = e.matmul(p_yb[:, (h - h0) * HD:(h - h0 + 1) * HD],
                                       a_C[:, q0:q0 + QC],
                                       t_stb[:, h * HD:(h + 1) * HD],
                                       start=(h == h0), stop=(h == h0 + HQ - 1))
                    b.inc(ins, "ymm", 1)
                e.wait_ge(b.s("ytm"), cc + 1)
                for fc in range(16):
                    ins = e.transpose(p_tr[:, fc * QC:(fc + 1) * QC],
                                      t_ytm[:, fc * QC:(fc + 1) * QC],
                                      c_idb[:, :])
                b.inc(ins, "ty", 1)
                e.wait_ge(b.s("bw"), cc + 1)
                for hf in range(2):
                    h0 = hf * (NH // 2)
                    e.wait_ge(b.s("stU"), 2 * cc + hf)
                    for h in range(h0, h0 + NH // 2):
                        ins = e.matmul(
                            p_sh[:, (h - h0) * HD:(h - h0 + 1) * HD],
                            t_bw[:, h * DSTATE:(h + 1) * DSTATE],
                            t_xtm[:, h * HD:(h + 1) * HD],
                            start=((h - h0) % 8 == 0),
                            stop=((h - h0) % 8 == 7))
                    b.inc(ins, "smm", 1)
                e.wait_ge(b.s("sq"), cc + 1)
                if cc >= 1:
                    e.wait_ge(b.s("rsq"), cc)
                for fc in range(16):
                    ins = e.matmul(p_ms[:, 224:224 + QC], c_ones[:, :],
                                   t_q[:, fc * QC:(fc + 1) * QC],
                                   start=(fc == 0), stop=(fc == 15))
                b.inc(ins, "rs", 1)

        @blk.sync
        def _(e):
            for cc in range(NC):
                for hf in range(2):
                    ix = 2 * cc + hf
                    if ix >= 1:
                        e.wait_ge(b.s("shU"), ix)
                        if dbg == "d5" and ix == 1:
                            e.wait_ge(b.s("dd"), 32)
                    b.inc(e.dma_start(
                        out=fl_sc[0:1, :16 * QC],
                        in_=a_s[hf * 16:hf * 16 + 16, cc * QC:(cc + 1) * QC]),
                        "f1", 16)
                    e.wait_ge(b.s("f1"), (ix + 1) * 16)
                    b.inc(e.dma_start(
                        out=t_shb[:, :],
                        in_=ap(fl_sc.tensor, 0, [[1, 1], [0, 128], [1, 16 * QC]])),
                        "f2", 16)

        if dbg == "d5":
            dbg5 = nc.dram_tensor("dbg5", [128, 6 * 2048], F32,
                                  kind="ExternalOutput")

        @blk.gpsimd
        def _(e):
            if dbg == "d5":
                e.wait_ge(b.s("f1"), 16)
                b.inc(e.dma_start(
                    out=dbg5.ap()[0:1, 4 * 2048 + 128:4 * 2048 + 128 + 2048],
                    in_=fl_sc[0:1, 0:2048]), "dd", 16)
                e.wait_ge(b.s("f2"), 16)
                e.wait_ge(b.s("tfE"), 1)
                b.inc(e.dma_start(
                    out=dbg5.ap()[:, 4 * 2048 + 128 + 2048:
                                  4 * 2048 + 128 + 2048 + 32],
                    in_=t_nshT[:, :]), "dd", 16)
            for cc in range(NC):
                e.wait_ge(b.s("gmE"), cc + 1)
                for hf in range(2):
                    ix = 2 * cc + hf
                    e.wait_ge(b.s("eA"), ix + 1)
                    if 4 * cc + 2 * hf >= 1:
                        e.wait_ge(b.s("ymm"), 4 * cc + 2 * hf)
                    e.tensor_tensor(
                        t_M[:, :], t_eA[:, :],
                        ap(t_gm, 0, [[QC, 128], [0, 16], [1, QC]]),
                        ALU.mult)
                    b.inc(e.engine_nop(), "mM", 1)
                    if dbg == "d5" and cc == 0:
                        b.inc(e.dma_start(
                            out=dbg5.ap()[:, hf * 2048:(hf + 1) * 2048],
                            in_=t_M[:, :]), "dd", 16)
                        b.inc(e.dma_start(
                            out=dbg5.ap()[:, (2 + hf) * 2048:(3 + hf) * 2048],
                            in_=t_eA[:, :]), "dd", 16)
                        if hf == 0:
                            b.inc(e.dma_start(
                                out=dbg5.ap()[:, 4 * 2048:4 * 2048 + 128],
                                in_=t_gm[:, :]), "dd", 16)
                        e.wait_ge(b.s("dd"), b.n("dd"))
                e.wait_ge(b.s("r1D"), cc + 1)
                b.inc(e.dma_start(
                    out=st_sc[:, 2 * S + cc * QC:2 * S + (cc + 1) * QC],
                    in_=t_r1c[:, :]), "r1o", 16)
            e.wait_ge(b.s("r1o"), NC * 16)

    if dbg and dbg != "d1":
        dbg3 = nc.dram_tensor("dbg3", [128, 16 * S + 1024], F32,
                              kind="ExternalOutput")
        _close_semctrs()
        b = SemCtr(nc)
        with nc.Block() as blk:

            @blk.gpsimd
            def _(e):
                for fc in range(16):
                    b.inc(e.dma_start(out=dbg3.ap()[:, fc * S:(fc + 1) * S],
                                      in_=a_vf[:, fc * S:(fc + 1) * S]),
                          "d", 16)
                b.inc(e.dma_start(out=dbg3.ap()[0:64, 16 * S:16 * S + 1024],
                                  in_=a_st[:, 0:1024]), "d", 16)
                e.wait_ge(b.s("d"), 17 * 16)

    s3.close()
    s1b.close()

    s4 = ExitStack()
    sb4 = mksb(s4)
    t_x0 = sb4("m_x0", [128, 8 * 512])
    t_wA4 = sb4("m_wA", [128, 16 * 128], BF16)
    t_wB4 = sb4("m_wB", [128, 16 * 128], BF16)
    t_mu = sb4("m_mu", [128, 512])
    t_r0 = sb4("m_r0", [128, 512])
    t_r1t = sb4("m_r1", [128, 512])
    t_h = sb4("m_h", [128, 8 * 512])
    t_t1 = sb4("m_t1", [128, 512])

    _close_semctrs()
    b = SemCtr(nc)
    with nc.Block() as blk, ExitStack() as ps:
        p_a = ps.enter_context(nc.psum_tensor("m_a", [128, 512], F32))
        p_b = ps.enter_context(nc.psum_tensor("m_b", [128, 512], F32))

        @blk.gpsimd
        def _(e):
            for tt in range(NT):
                for dc in range(8):
                    idx = tt * 8 + dc
                    if idx >= 2:
                        e.wait_ge(b.s("m2"), idx - 1)
                    buf = t_wA4 if dc % 2 == 0 else t_wB4
                    b.inc(e.dma_start(
                        out=buf[:, :],
                        in_=ap(di["w_out"].tensor, dc * 128,
                               [[D, 128], [128 * D, 16], [1, 128]])),
                        "gwA" if idx % 2 == 0 else "gwB", 16)

        @blk.sync
        def _(e):
            for tt in range(NT):
                if tt >= 1:
                    e.wait_ge(b.s("x0u"), tt)
                b.inc(e.dma_start(
                    out=t_x0[:, :],
                    in_=ap(di["x0"].tensor, 3 + tt * 512,
                           [[P, 128], [128 * P, 8], [1, 512]])), "hx", 16)
                b.inc(e.dma_start(out=t_mu[:, :],
                                  in_=st_sc[:, tt * 512:tt * 512 + 512]),
                      "hx", 16)
                b.inc(e.dma_start(out=t_r0[:, :],
                                  in_=st_sc[:, S + tt * 512:S + tt * 512 + 512]),
                      "hx", 16)
                b.inc(e.dma_start(
                    out=t_r1t[:, :],
                    in_=st_sc[:, 2 * S + tt * 512:2 * S + tt * 512 + 512]),
                    "hx", 16)
                for dc in range(8):
                    e.wait_ge(b.s("hE"), tt * 8 + dc + 1)
                    b.inc(e.dma_start(
                        out=ap(h_sc.tensor, dc * 128 * S + tt * 512,
                               [[S, 128], [1, 512]]),
                        in_=t_h[:, dc * 512:(dc + 1) * 512]), "ho", 16)

        @blk.tensor
        def _(e):
            for tt in range(NT):
                for dc in range(8):
                    idx = tt * 8 + dc
                    e.wait_ge(b.s("gwA" if idx % 2 == 0 else "gwB"),
                              (idx // 2 + 1) * 16)
                    if idx >= 2:
                        e.wait_ge(b.s("m2e"), idx - 1)
                    buf = t_wA4 if dc % 2 == 0 else t_wB4
                    pm = p_a if dc % 2 == 0 else p_b
                    for k in range(16):
                        ins = e.matmul(pm[:, :], buf[:, k * 128:(k + 1) * 128],
                                       a_vf[:, k * S + tt * 512:k * S + tt * 512 + 512],
                                       start=(k == 0), stop=(k == 15))
                    b.inc(ins, "m2", 1)

        @blk.vector
        def _(e):
            for tt in range(NT):
                e.wait_ge(b.s("hx"), (tt + 1) * 64)
                if tt >= 1:
                    e.wait_ge(b.s("ho"), tt * 128)
                for dc in range(8):
                    idx = tt * 8 + dc
                    e.wait_ge(b.s("m2"), idx + 1)
                    if idx >= 1:
                        e.wait_ge(b.s("hE"), idx)
                    pm = p_a if dc % 2 == 0 else p_b
                    nfn = b.inc(e.tensor_tensor(t_t1[:, :], pm[:, :],
                                                t_r1t[:, :], ALU.mult),
                                "m2e", 1)
                    sl = slice(dc * 512, (dc + 1) * 512)
                    e.tensor_tensor(t_h[:, sl], t_x0[:, sl],
                                    t_mu[:, :], ALU.subtract)
                    e.tensor_tensor(t_h[:, sl], t_h[:, sl],
                                    t_r0[:, :], ALU.mult)
                    b.inc(e.tensor_tensor(t_h[:, sl], t_h[:, sl], t_t1[:, :],
                                          ALU.add), "hE", 1)
                b.inc(e.engine_nop(), "x0u", 1)

        @blk.scalar
        def _(e):
            e.wait_ge(b.s("ho"), NT * 128)

    s4.close()
    sa.close()

    if stage == "p2":
        _close_semctrs()
        b = SemCtr(nc)
        with nc.Block() as blk:

            @blk.gpsimd
            def _(e):
                b.inc(e.dma_start(out=y_out, in_=h_sc), "o", 16)
                e.wait_ge(b.s("o"), 16)
        _close_semctrs()
        es.close()
        return nc

    sB = ExitStack()
    sbB = mksb(sB)
    t_h5 = sbB("n_h", [128, 8 * 512])
    t_x05 = sbB("n_x0", [128, 8 * 512])
    t_hn = sbB("n_hn", [128, 8 * 512], BF16)
    t_sq5 = sbB("n_sq", [128, 8 * 512], BF16)
    t_w5A = sbB("n_w5A", [128, 8 * 128], BF16)
    t_w5B = sbB("n_w5B", [128, 8 * 128], BF16)
    t_w6A = sbB("n_w6A", [128, 32 * 128], BF16)
    t_w6B = sbB("n_w6B", [128, 32 * 128], BF16)
    t_mid = sbB("n_mid", [128, 32 * 512], BF16)
    t_st5 = sbB("n_st5", [128, 512])
    t_x1 = sbB("n_x1", [128, 8 * 512])

    _close_semctrs()
    b = SemCtr(nc)
    with nc.Block() as blk, ExitStack() as ps:
        p_a = ps.enter_context(nc.psum_tensor("n_a", [128, 512], F32))
        p_b = ps.enter_context(nc.psum_tensor("n_b", [128, 512], F32))
        p_s = ps.enter_context(nc.psum_tensor("n_s", [128, 512], F32))
        p_c = ps.enter_context(nc.psum_tensor("n_c", [128, 512], F32))
        p_d = ps.enter_context(nc.psum_tensor("n_d", [128, 512], F32))

        @blk.sync
        def _(e):
            for tt in range(NT):
                if tt >= 1:
                    e.wait_ge(b.s("ttd"), tt)
                b.inc(e.dma_start(
                    out=t_h5[:, :],
                    in_=ap(h_sc.tensor, tt * 512,
                           [[S, 128], [128 * S, 8], [1, 512]])), "hx", 16)
                b.inc(e.dma_start(
                    out=t_x05[:, :],
                    in_=ap(di["x0"].tensor, 3 + tt * 512,
                           [[P, 128], [128 * P, 8], [1, 512]])), "hx", 16)
                for dc in range(8):
                    e.wait_ge(b.s("x1e"), tt * 8 + dc + 1)
                    b.inc(e.dma_start(
                        out=ap(x1s.tensor, 1 + dc * 128 * PF + tt * 512,
                               [[PF, 128], [1, 512]]),
                        in_=t_x1[:, dc * 512:(dc + 1) * 512]), "xo", 16)
                if tt == 0:
                    e.wait_ge(b.s("x1e"), 8)
                    b.inc(e.dma_start(
                        out=ap(cc2i.ap().tensor, 0, [[1, 1], [1, 128], [128, 8]]),
                        in_=ap(t_x1, 0, [[8 * 512, 128], [512, 8]]),
                        allow_slow_non_contiguous=True), "c2", 16)
                if tt == NT - 1:
                    e.wait_ge(b.s("x1e"), NT * 8)
                    b.inc(e.dma_start(
                        out=ap(cc2i.ap().tensor, D, [[1, 1], [1, 128], [128, 8]]),
                        in_=ap(t_x1, 511, [[8 * 512, 128], [512, 8]]),
                        allow_slow_non_contiguous=True), "c2", 16)

        @blk.gpsimd
        def _(e):
            for tt in range(NT):
                for k2 in range(32):
                    idx = tt * 32 + k2
                    if idx >= 2:
                        e.wait_ge(b.s("m3"), idx - 1)
                    buf = t_w5A if k2 % 2 == 0 else t_w5B
                    b.inc(e.dma_start(
                        out=buf[:, :],
                        in_=ap(di["mlp_w1"].tensor, k2 * 128,
                               [[MLP, 128], [128 * MLP, 8], [1, 128]])),
                        "g1A" if idx % 2 == 0 else "g1B", 16)
                for dc in range(8):
                    idx = tt * 8 + dc
                    if idx >= 2:
                        e.wait_ge(b.s("m4"), idx - 1)
                    buf = t_w6A if dc % 2 == 0 else t_w6B
                    b.inc(e.dma_start(
                        out=buf[:, :],
                        in_=ap(di["mlp_w2"].tensor, dc * 128,
                               [[D, 128], [128 * D, 32], [1, 128]])),
                        "g2A" if idx % 2 == 0 else "g2B", 16)

        @blk.vector
        def _(e):
            def vfence(ins, tag="fn"):
                nfn = b.inc(ins, tag, 1)
                e.wait_ge(b.s(tag), nfn)
            for tt in range(NT):
                e.wait_ge(b.s("hx"), (tt + 1) * 32)
                vfence(e.tensor_copy(t_hn[:, :], t_h5[:, :]))
                e.tensor_tensor(t_sq5[:, :], t_hn[:, :], t_hn[:, :], ALU.mult)
                b.inc(e.engine_nop(), "hsq", 1)
                e.wait_ge(b.s("sq2"), tt + 1)
                vfence(e.reciprocal(t_st5[:, :], t_st5[:, :]))
                for dc in range(8):
                    sl = slice(dc * 512, (dc + 1) * 512)
                    ins = e.tensor_tensor(t_hn[:, sl], t_hn[:, sl],
                                          t_st5[:, :], ALU.mult)
                vfence(ins)
                for dc in range(8):
                    sl = slice(dc * 512, (dc + 1) * 512)
                    e.tensor_tensor(
                        t_hn[:, sl], t_hn[:, sl],
                        ap(c_rmsw, dc, [[8, 128], [0, 512]]), ALU.mult)
                b.inc(e.engine_nop(), "hnD", 1)
                for dc in range(8):
                    e.wait_ge(b.s("m4"), tt * 8 + dc + 1)
                    pm = p_c if dc % 2 == 0 else p_d
                    sl = slice(dc * 512, (dc + 1) * 512)
                    vfence(e.scalar_tensor_tensor(
                        t_x1[:, sl], pm[:, :], c_b2[:, dc:dc + 1],
                        t_h5[:, sl], ALU.add, ALU.add), "m4e")
                    e.tensor_tensor(t_x1[:, sl], t_x1[:, sl],
                                    t_x05[:, sl], ALU.add)
                    b.inc(e.engine_nop(), "x1e", 1)
                e.wait_ge(b.s("xo"), (tt + 1) * 128)
                if tt == 0:
                    e.wait_ge(b.s("c2"), 16)
                b.inc(e.engine_nop(), "ttd", 1)

        @blk.tensor
        def _(e):
            for tt in range(NT):
                e.wait_ge(b.s("hsq"), tt + 1)
                if tt >= 1:
                    e.wait_ge(b.s("sq2"), tt)
                for k in range(8):
                    ins = e.matmul(p_s[:, :], c_ones[:, :],
                                   t_sq5[:, k * 512:(k + 1) * 512],
                                   start=(k == 0), stop=(k == 7))
                b.inc(ins, "mmS", 1)
                e.wait_ge(b.s("hnD"), tt + 1)
                for k2 in range(32):
                    idx = tt * 32 + k2
                    e.wait_ge(b.s("g1A" if idx % 2 == 0 else "g1B"),
                              (idx // 2 + 1) * 16)
                    if idx >= 2:
                        e.wait_ge(b.s("midE"), idx - 1)
                    buf = t_w5A if k2 % 2 == 0 else t_w5B
                    pm = p_a if k2 % 2 == 0 else p_b
                    for k in range(8):
                        ins = e.matmul(pm[:, :], buf[:, k * 128:(k + 1) * 128],
                                       t_hn[:, k * 512:(k + 1) * 512],
                                       start=(k == 0), stop=(k == 7))
                    b.inc(ins, "m3", 1)
                for dc in range(8):
                    idx = tt * 8 + dc
                    e.wait_ge(b.s("g2A" if idx % 2 == 0 else "g2B"),
                              (idx // 2 + 1) * 16)
                    pm = p_c if dc % 2 == 0 else p_d
                    if dc >= 2 or tt >= 1:
                        e.wait_ge(b.s("m4e"), idx - 1)
                    buf = t_w6A if dc % 2 == 0 else t_w6B
                    for k2 in range(32):
                        if dc == 0:
                            e.wait_ge(b.s("midE"), tt * 32 + k2 + 1)
                        ins = e.matmul(pm[:, :], buf[:, k2 * 128:(k2 + 1) * 128],
                                       t_mid[:, k2 * 512:(k2 + 1) * 512],
                                       start=(k2 == 0), stop=(k2 == 31))
                    b.inc(ins, "m4", 1)

        @blk.scalar
        def _(e):
            for tt in range(NT):
                e.wait_ge(b.s("mmS"), tt + 1)
                ins = e.activation(t_st5[:, :], p_s[:, :], AF.Sqrt,
                                   scale=1.0 / D, bias=c_eps1[:, 0:1])
                b.inc(ins, "sq2", 1)
                if tt >= 1:
                    e.wait_ge(b.s("m4"), tt * 8)
                for k2 in range(32):
                    e.wait_ge(b.s("m3"), tt * 32 + k2 + 1)
                    pm = p_a if k2 % 2 == 0 else p_b
                    ins = e.activation(t_mid[:, k2 * 512:(k2 + 1) * 512],
                                       pm[:, :], AF.Silu,
                                       bias=c_b1[:, k2:k2 + 1])
                    b.inc(ins, "midE", 1)

    sB.close()

    sE = ExitStack()
    sbE = mksb(sE)
    t_sel = sbE("e_sel", [2 * n_cores, D])
    t_rL = sbE("e_rL", [1, D])
    t_rR = sbE("e_rR", [1, D])
    t_hc = sbE("e_hc", [128, 16])

    _close_semctrs()
    b = SemCtr(nc)
    with nc.Block() as blk, ExitStack() as ps:
        p_hr = ps.enter_context(nc.psum_tensor("e_hr", [1, 512], F32))
        p_hc = ps.enter_context(nc.psum_tensor("e_phc", [128, 16], F32))

        @blk.gpsimd
        def _(e):
            ins = e.collective_compute(
                "AllGather", ALU.bypass,
                replica_groups=[list(range(n_cores))],
                ins=[cc2i.ap().opt()], outs=[cc2o.ap().opt()])
            b.inc(ins, "cc", 1)

        @blk.sync
        def _(e):
            e.wait_ge(b.s("cc"), 1)
            b.inc(e.dma_start(out=t_sel[:, :], in_=cc2o.ap()[:, :]), "ld", 16)
            e.wait_ge(b.s("hcE"), 1)
            b.inc(e.dma_start(
                out=ap(x1s.tensor, 0, [[1, 1], [PF, 128], [128 * PF, 8]]),
                in_=t_hc[:, 0:8], allow_slow_non_contiguous=True), "hw", 16)
            b.inc(e.dma_start(
                out=ap(x1s.tensor, PF - 1, [[1, 1], [PF, 128], [128 * PF, 8]]),
                in_=t_hc[:, 8:16], allow_slow_non_contiguous=True), "hw", 16)
            e.wait_ge(b.s("hw"), 32)

        @blk.tensor
        def _(e):
            e.wait_ge(b.s("ld"), 16)
            for side, sel in ((0, c_selL), (1, c_selR)):
                for half in range(2):
                    if side + half >= 1:
                        e.wait_ge(b.s("rw"), 2 * side + half)
                    ins = e.matmul(p_hr[:, :], sel[:, :],
                                   t_sel[:, half * 512:(half + 1) * 512],
                                   start=True, stop=True)
                    b.inc(ins, "rmm", 1)
            e.wait_ge(b.s("rw"), 4)
            for fc in range(8):
                e.transpose(p_hc[:, fc:fc + 1],
                            t_rL[0:1, fc * 128:(fc + 1) * 128],
                            c_id32[0:1, 0:1])
            for fc in range(8):
                ins = e.transpose(p_hc[:, 8 + fc:9 + fc],
                                  t_rR[0:1, fc * 128:(fc + 1) * 128],
                                  c_id32[0:1, 0:1])
            b.inc(ins, "tr", 1)

        @blk.vector
        def _(e):
            for side in range(2):
                for half in range(2):
                    e.wait_ge(b.s("rmm"), 2 * side + half + 1)
                    e.tensor_scalar(t_sel[0:1, :], t_sel[0:1, :], 1.0,
                                    None, ALU.mult)
                    dst = t_rL if side == 0 else t_rR
                    e.tensor_copy(dst[0:1, half * 512:(half + 1) * 512],
                                  p_hr[:, :])
                    b.inc(e.engine_nop(), "rw", 1)
            e.wait_ge(b.s("tr"), 1)
            e.tensor_scalar(t_sel[0:1, :], t_sel[0:1, :], 1.0, None,
                            ALU.mult)
            nfn = b.inc(e.tensor_copy(t_hc[:, :], p_hc[:, :]), "fn", 1)
            e.wait_ge(b.s("fn"), nfn)
            e.tensor_scalar(t_hc[:, 0:8], t_hc[:, 0:8], c_hm[:, 0:1], None,
                            ALU.mult)
            e.tensor_scalar(t_hc[:, 8:16], t_hc[:, 8:16], c_hm[:, 1:2], None,
                            ALU.mult)
            b.inc(e.engine_nop(), "hcE", 1)

    sE.close()

    if stage == "x1":
        _close_semctrs()
        b = SemCtr(nc)
        with nc.Block() as blk:

            @blk.gpsimd
            def _(e):
                b.inc(e.dma_start(out=y_out, in_=x1s.tensor.ap()[:, 1:S + 1]),
                      "o", 16)
                e.wait_ge(b.s("o"), 16)
        _close_semctrs()
        es.close()
        return nc

    sC = ExitStack()
    sbC = mksb(sC)
    t_wq4 = [sbC("w_q%d" % i, [128, 4096]) for i in range(4)]
    t_wb2 = [sbC("w_b%d" % i, [128, 4096], BF16) for i in range(2)]
    t_q1A = sbC("w_1A", [128, 4096], BF16)
    t_q1B = sbC("w_1B", [128, 4096], BF16)
    t_acc = sbC("w_acc", [128, 1])
    t_ac2 = sbC("w_ac2", [128, 1])

    _close_semctrs()
    b = SemCtr(nc)
    with nc.Block() as blk, ExitStack() as ps:
        p_m = ps.enter_context(nc.psum_tensor("w_m", [1, 4], F32))
        wspecs = [(di["w1t"], r1s, D // 128, FH, c_w1m, c_w1s),
                  (di["w2t"], r2s, FH // 128, D, c_w2m, c_w2s)]
        NCH = sum(sp[2] for sp in wspecs)

        chunks = []
        for wi, (src_, dst_, nch, wid, cm, cs_) in enumerate(wspecs):
            for ch in range(nch):
                chunks.append((src_, dst_, ch, wid))

        @blk.sync
        def _(e):
            for qi, (src_, dst_, ch, wid) in enumerate(chunks):
                if qi >= 4:
                    e.wait_ge(b.s("qd"), qi - 3)
                b.inc(e.dma_start(
                    out=t_wq4[qi % 4][:, :wid],
                    in_=src_.tensor.ap()[ch * 128:(ch + 1) * 128, :]),
                    "l2%d" % (qi % 4), 16)
                if qi >= 1:
                    sj = qi - 1
                    _, dstj, chj, widj = chunks[sj]
                    e.wait_ge(b.s("qd"), sj + 1)
                    b.inc(e.dma_start(
                        out=dstj.tensor.ap()[chj * 128:(chj + 1) * 128, :],
                        in_=t_wb2[sj % 2][:, :widj]),
                        "qo%d" % (sj % 2), 16)
            sj = len(chunks) - 1
            _, dstj, chj, widj = chunks[sj]
            e.wait_ge(b.s("qd"), sj + 1)
            b.inc(e.dma_start(
                out=dstj.tensor.ap()[chj * 128:(chj + 1) * 128, :],
                in_=t_wb2[sj % 2][:, :widj]),
                "qo%d" % (sj % 2), 16)

        @blk.vector
        def _(e):
            li = 0
            for wi, (src_, dst_, nch, wid, cm, cs_) in enumerate(wspecs):
                e.memset(t_acc[:, :], 0.0)
                for ch in range(nch):
                    e.wait_ge(b.s("l1A" if li % 2 == 0 else "l1B"),
                              (li // 2 + 1) * 16)
                    buf = t_q1A if li % 2 == 0 else t_q1B
                    nfn = b.inc(
                        e.scalar_tensor_tensor(buf[:, :wid], buf[:, :wid],
                                               -1.0, buf[:, :wid],
                                               ALU.mult, ALU.max), "fn", 1)
                    e.wait_ge(b.s("fn"), nfn)
                    nfn = b.inc(e.reduce_sum(t_ac2[:, :], buf[:, :wid],
                                             mybir.AxisListType.X), "fn", 1)
                    e.wait_ge(b.s("fn"), nfn)
                    e.tensor_tensor(t_acc[:, :], t_acc[:, :], t_ac2[:, :],
                                    ALU.add)
                    b.inc(e.engine_nop(), "rd", 1)
                    li += 1
                b.inc(e.engine_nop(), "accD", 1)
                e.wait_ge(b.s("msum"), wi + 1)
                nfn = b.inc(e.tensor_copy(c_wsq[2 * wi][0:1, 0:1],
                                          p_m[0:1, 0:1]), "fn", 1)
                e.wait_ge(b.s("fn"), nfn)
                nfn = b.inc(
                    e.tensor_scalar(c_wsq[2 * wi][0:1, 0:1],
                                    c_wsq[2 * wi][0:1, 0:1],
                                    1.0 / (nch * 128 * wid), EPS,
                                    ALU.mult, ALU.max), "fn", 1)
                e.wait_ge(b.s("fn"), nfn)
                e.reciprocal(c_wsq[2 * wi + 1][0:1, 0:1], c_wsq[2 * wi][0:1, 0:1])
                b.inc(e.engine_nop(), "scD", 1)
            for qi, (src_, dst_, ch, wid) in enumerate(chunks):
                e.wait_ge(b.s("qi"), qi + 1)
                if qi >= 2:
                    e.wait_ge(b.s("qo%d" % (qi % 2)), (qi // 2) * 16)
                buf = t_wq4[qi % 4]
                nfn = b.inc(
                    e.tensor_scalar(buf[:, :wid], buf[:, :wid], MAGIC,
                                    -1.0, ALU.subtract, ALU.max), "fn", 1)
                e.wait_ge(b.s("fn"), nfn)
                e.tensor_scalar(t_wb2[qi % 2][:, :wid], buf[:, :wid], 1.0,
                                None, ALU.min)
                b.inc(e.engine_nop(), "qd", 1)

        @blk.tensor
        def _(e):
            for wi in range(2):
                e.wait_ge(b.s("accD"), wi + 1)
                if wi >= 1:
                    e.wait_ge(b.s("scD"), wi)
                ins = e.matmul(p_m[0:1, 0:1], t_acc[:, :], c_eps1[:, 1:2],
                               start=True, stop=True)
                b.inc(ins, "msum", 1)

        @blk.gpsimd
        def _(e):
            li = 0
            for wi, (src_, dst_, nch, wid, cm, cs_) in enumerate(wspecs):
                for ch in range(nch):
                    if li >= 2:
                        e.wait_ge(b.s("rd"), li - 1)
                    b.inc(e.dma_start(
                        out=(t_q1A if li % 2 == 0 else t_q1B)[:, :wid],
                        in_=src_.tensor.ap()[ch * 128:(ch + 1) * 128, :]),
                        "l1A" if li % 2 == 0 else "l1B", 16)
                    li += 1
            for wi, (src_, dst_, nch, wid, cm, cs_) in enumerate(wspecs):
                e.wait_ge(b.s("scD"), wi + 1)
                b.inc(e.dma_start(
                    out=cm[:, :],
                    in_=ap(c_wsq[2 * wi], 0, [[1, 1], [0, 128], [1, 1]])),
                    "bc%d" % wi, 16)
                b.inc(e.dma_start(
                    out=cs_[:, :],
                    in_=ap(c_wsq[2 * wi + 1], 0, [[1, 1], [0, 128], [1, 1]])),
                    "bc%d" % wi, 16)

        @blk.scalar
        def _(e):
            qi = 0
            for wi, (src_, dst_, nch, wid, cm, cs_) in enumerate(wspecs):
                e.wait_ge(b.s("bc%d" % wi), 32)
                for ch in range(nch):
                    e.wait_ge(b.s("l2%d" % (qi % 4)), (qi // 4 + 1) * 16)
                    buf = t_wq4[qi % 4]
                    ins = e.activation(buf[:, :wid], buf[:, :wid], AF.Identity,
                                       scale=cs_[:, :], bias=c_magic[:, :])
                    b.inc(ins, "qi", 1)
                    qi += 1

    sC.close()

    ftiles = []
    c = 0
    while c < PF:
        w = min(512, PF - c)
        ftiles.append((c, w))
        c += w
    HHW = S // 2 + 2
    PFP = 17 * 128

    sD = ExitStack()
    sbD = mksb(sD)
    a_q1 = sbD("f_q1", [128, 8 * PF], BF16)
    a_h1 = sbD("f_h1", [128, 32 * HHW], BF16)
    a_mx = sbD("f_mx", [128, PFP])
    a_mu2 = sbD("f_mu2", [128, PF])
    a_f1 = sbD("f_f1", [128, PF])
    a_al1 = sbD("f_al1", [128, PF])

    sDi = ExitStack()
    sbDi = mksb(sDi)
    t_xf = sbDi("f_xf", [128, 8 * 512])
    t_sqf = sbDi("f_sqf", [128, 8 * 512], BF16)
    t_mxc = sbDi("f_mxc", [128, 32])
    t_mxr = sbDi("f_mxr", [32, 128])

    _close_semctrs()
    b = SemCtr(nc)
    with nc.Block() as blk, ExitStack() as ps:
        p_s1f = ps.enter_context(nc.psum_tensor("f_s1", [128, 512], F32))
        p_s2f = ps.enter_context(nc.psum_tensor("f_s2", [128, 512], F32))
        p_tf = ps.enter_context(nc.psum_tensor("f_tf", [128, 128], F32))
        nft = len(ftiles)

        @blk.sync
        def _(e):
            for i, (c0, w) in enumerate(ftiles):
                if i >= 1:
                    e.wait_ge(b.s("ftd"), i)
                b.inc(e.dma_start(
                    out=t_xf[:, :8 * w],
                    in_=ap(x1s.tensor, c0, [[PF, 128], [128 * PF, 8], [1, w]])),
                    "hx", 16)

        @blk.gpsimd
        def _(e):
            for i, (c0, w) in enumerate(ftiles):
                if i >= 1:
                    e.wait_ge(b.s("ftd"), i)
                b.inc(e.dma_start(
                    out=ap(a_q1, c0, [[8 * PF, 128], [PF, 8], [1, w]]),
                    in_=ap(x1s.tensor, c0, [[PF, 128], [128 * PF, 8], [1, w]])),
                    "gx", 16)
            e.wait_ge(b.s("mxT"), 1)
            b.inc(e.dma_start(out=fl_sc[0:1, :PFP], in_=t_mxr[:17, :]),
                  "fm", 16)
            e.wait_ge(b.s("fm"), 16)
            b.inc(e.dma_start(
                out=a_mx[:, :],
                in_=ap(fl_sc.tensor, 0, [[1, 1], [0, 128], [1, PFP]])),
                "fm", 16)

        @blk.vector
        def _(e):
            e.memset(a_mx[:, PF:], 0.0)
            for i, (c0, w) in enumerate(ftiles):
                e.wait_ge(b.s("gx"), (i + 1) * 16)
                if i >= 1:
                    e.wait_ge(b.s("mmS"), i)
                q1ap = ap(a_q1, c0, [[8 * PF, 128], [PF, 8], [1, w]])
                e.tensor_tensor(t_sqf[:, :8 * w], q1ap, q1ap, ALU.mult)
                b.inc(e.engine_nop(), "sqD", 1)
                e.wait_ge(b.s("mmS"), i + 1)
                e.tensor_scalar(a_mu2[:, c0:c0 + w], p_s1f[:, :w], 1.0 / D,
                                None, ALU.mult)
                nfn = b.inc(
                    e.tensor_scalar(a_f1[:, c0:c0 + w], p_s2f[:, :w],
                                    1.0 / D, None, ALU.mult), "stE", 1)
                e.wait_ge(b.s("stE"), nfn)
                nfn = b.inc(e.tensor_tensor(t_sqf[:, :w], a_mu2[:, c0:c0 + w],
                                            a_mu2[:, c0:c0 + w], ALU.mult),
                            "fn", 1)
                e.wait_ge(b.s("fn"), nfn)
                e.tensor_tensor(a_f1[:, c0:c0 + w], a_f1[:, c0:c0 + w],
                                t_sqf[:, :w], ALU.subtract)
                b.inc(e.engine_nop(), "vok", 1)
                e.wait_ge(b.s("sqt"), i + 1)
                e.reciprocal(a_f1[:, c0:c0 + w], a_f1[:, c0:c0 + w])
                e.wait_ge(b.s("hx"), (i + 1) * 16)
                nfn = b.inc(
                    e.tensor_tensor(t_xf[:, :8 * w], t_xf[:, :8 * w],
                                    ap(a_mu2, c0, [[PF, 128], [0, 8], [1, w]]),
                                    ALU.subtract), "fn", 1)
                e.wait_ge(b.s("fn"), nfn)
                e.scalar_tensor_tensor(a_mx[:, c0:c0 + w], t_xf[:, :w], -1.0,
                                       t_xf[:, :w], ALU.mult, ALU.max)
                for k in range(1, 8):
                    sl = t_xf[:, k * w:(k + 1) * w]
                    e.scalar_tensor_tensor(t_sqf[:, :w], sl, -1.0,
                                           sl, ALU.mult, ALU.max)
                    e.tensor_tensor(a_mx[:, c0:c0 + w], a_mx[:, c0:c0 + w],
                                    t_sqf[:, :w], ALU.max)
                b.inc(e.engine_nop(), "mxD", 1)
                b.inc(e.engine_nop(), "ftd", 1)
            for g in range(17):
                e.wait_ge(b.s("mtr"), g + 1)
                if g == 0:
                    e.tensor_scalar(t_sqf[:, :512], t_sqf[:, :512], 1.0,
                                    None, ALU.mult)
                e.reduce_max(t_mxc[:, g:g + 1], p_tf[:, :],
                             mybir.AxisListType.X)
                b.inc(e.engine_nop(), "mxg", 1)
            e.wait_ge(b.s("mtc"), 1)
            e.tensor_scalar(t_sqf[:, :512], t_sqf[:, :512], 1.0, None,
                            ALU.mult)
            e.tensor_copy(t_mxr[:17, :], p_tf[:17, :])
            b.inc(e.engine_nop(), "mxT", 1)
            def vfence(ins, tag="fn"):
                nfn = b.inc(ins, tag, 1)
                e.wait_ge(b.s(tag), nfn)
            e.wait_ge(b.s("fm"), 32)
            vfence(e.tensor_tensor(a_mx[:, 0:PF], a_mx[:, 0:PF], a_f1[:, :],
                                   ALU.mult))
            vfence(e.tensor_scalar(a_mx[:, 0:PF], a_mx[:, 0:PF], EPS, None,
                                   ALU.max))
            e.wait_ge(b.s("w1m"), 1)
            vfence(e.tensor_scalar(a_al1[:, :], a_mx[:, 0:PF], c_w1m[:, :],
                                   1.0 / 127.0, ALU.mult, ALU.mult))
            vfence(e.tensor_scalar(a_al1[:, 0:1], a_al1[:, 0:1], c_hm[:, 0:1],
                                   None, ALU.mult))
            e.tensor_scalar(a_al1[:, PF - 1:PF], a_al1[:, PF - 1:PF],
                            c_hm[:, 1:2], None, ALU.mult)
            vfence(e.reciprocal(a_mx[:, 0:PF], a_mx[:, 0:PF]))
            vfence(e.tensor_tensor(a_f1[:, :], a_f1[:, :], a_mx[:, 0:PF],
                                   ALU.mult))
            e.tensor_scalar(a_f1[:, :], a_f1[:, :], 127.0, None, ALU.mult)
            b.inc(e.engine_nop(), "f1D", 1)
            for i, (c0, w) in enumerate(ftiles):
                e.wait_ge(b.s("hx2"), (i + 1) * 16)
                vfence(e.tensor_tensor(
                    t_xf[:, :8 * w], t_xf[:, :8 * w],
                    ap(a_mu2, c0, [[PF, 128], [0, 8], [1, w]]),
                    ALU.subtract))
                vfence(e.tensor_tensor(
                    t_xf[:, :8 * w], t_xf[:, :8 * w],
                    ap(a_f1, c0, [[PF, 128], [0, 8], [1, w]]),
                    ALU.mult))
                e.tensor_scalar(ap(a_q1, c0, [[8 * PF, 128], [PF, 8], [1, w]]),
                                t_xf[:, :8 * w], MAGIC, MAGIC,
                                ALU.add, ALU.subtract)
                b.inc(e.engine_nop(), "q1D", 1)

        @blk.tensor
        def _(e):
            for i, (c0, w) in enumerate(ftiles):
                e.wait_ge(b.s("gx"), (i + 1) * 16)
                e.wait_ge(b.s("sqD"), i + 1)
                if i >= 1:
                    e.wait_ge(b.s("stE"), i)
                q1ap3 = ap(a_q1, c0, [[8 * PF, 128], [PF, 8], [1, w]])
                for k in range(8):
                    e.matmul(p_s1f[:, :w], c_ones[:, :],
                             ap(a_q1, c0 + k * PF, [[8 * PF, 128], [1, w]]),
                             start=(k == 0), stop=(k == 7))
                for k in range(8):
                    ins = e.matmul(p_s2f[:, :w], c_ones[:, :],
                                   t_sqf[:, k * w:(k + 1) * w],
                                   start=(k == 0), stop=(k == 7))
                b.inc(ins, "mmS", 1)
            e.wait_ge(b.s("mxD"), nft)
            for g in range(17):
                if g >= 1:
                    e.wait_ge(b.s("mxg"), g)
                ins = e.transpose(p_tf[:, :], a_mx[:, g * 128:(g + 1) * 128],
                                  c_id32[:, :])
                b.inc(ins, "mtr", 1)
            e.wait_ge(b.s("mxg"), 17)
            ins = e.transpose(p_tf[:17, :], t_mxc[:, 0:17], c_id32[:, :])
            b.inc(ins, "mtc", 1)

        @blk.scalar
        def _(e):
            for i, (c0, w) in enumerate(ftiles):
                e.wait_ge(b.s("vok"), i + 1)
                ins = e.activation(a_f1[:, c0:c0 + w], a_f1[:, c0:c0 + w],
                                   AF.Sqrt, bias=c_eps1[:, 0:1])
                b.inc(ins, "sqt", 1)

        @blk.sync
        def _(e):
            for i, (c0, w) in enumerate(ftiles):
                if i == 0:
                    e.wait_ge(b.s("f1D"), 1)
                else:
                    e.wait_ge(b.s("q1D"), i)
                b.inc(e.dma_start(
                    out=t_xf[:, :8 * w],
                    in_=ap(x1s.tensor, c0, [[PF, 128], [128 * PF, 8], [1, w]])),
                    "hx2", 16)

        @blk.gpsimd
        def _(e):
            b.inc(e.engine_nop(), "w1m", 1)

    sDi.close()

    sDm = ExitStack()
    sbDm = mksb(sDm)
    t_r5A = sbDm("f_r5A", [128, 8 * 128], BF16)
    t_r5B = sbDm("f_r5B", [128, 8 * 128], BF16)
    t_r6A = sbDm("f_r6A", [128, 32 * 128], BF16)
    t_r6B = sbDm("f_r6B", [128, 32 * 128], BF16)
    HT = S // 2
    NOT = HT // 512
    t_cv5 = sbDm("f_cv5", [128, HT])
    t_mx2 = sbDm("f_mx2", [128, HT])
    t_g2 = sbDm("f_g2", [128, HT])
    t_al2 = sbDm("f_al2", [128, HT])
    t_x2 = sbDm("f_x2", [128, 8 * 512])
    t_xr = sbDm("f_xr", [128, 8 * 512])
    t_m2c = sbDm("f_m2c", [128, 16])
    t_m2r = sbDm("f_m2r", [16, 128])
    hts = [(o, min(512, HHW - o)) for o in range(0, HHW, 512)]
    NJ = len(hts)
    NG2 = HT // 128

    _close_semctrs()
    b = SemCtr(nc)
    with nc.Block() as blk, ExitStack() as ps:
        p_a = ps.enter_context(nc.psum_tensor("f_a", [128, 512], F32))
        p_b = ps.enter_context(nc.psum_tensor("f_b", [128, 512], F32))
        p_t6 = ps.enter_context(nc.psum_tensor("f_t6", [128, 128], F32))

        @blk.sync
        def _(e):
            for hl in range(2):
                for dc in range(32):
                    idx = hl * 32 + dc
                    if idx >= 2:
                        e.wait_ge(b.s("m5"), (idx - 1) * NJ)
                    buf = t_r5A if dc % 2 == 0 else t_r5B
                    b.inc(e.dma_start(
                        out=buf[:, :],
                        in_=ap(r1s.tensor, dc * 128,
                               [[FH, 128], [128 * FH, 8], [1, 128]])),
                        "g5A" if idx % 2 == 0 else "g5B", 16)
                for ot in range(NOT):
                    if hl * NOT + ot >= 1:
                        e.wait_ge(b.s("xru"), hl * NOT + ot)
                    b.inc(e.dma_start(
                        out=t_xr[:, :],
                        in_=ap(x1s.tensor, 1 + hl * HT + ot * 512,
                               [[PF, 128], [128 * PF, 8], [1, 512]])), "xr", 16)
                    for dc6 in range(8):
                        idx = (hl * NOT + ot) * 8 + dc6
                        if idx >= 2:
                            e.wait_ge(b.s("m6"), idx - 1)
                        buf = t_r6A if dc6 % 2 == 0 else t_r6B
                        b.inc(e.dma_start(
                            out=buf[:, :],
                            in_=ap(r2s.tensor, dc6 * 128,
                                   [[D, 128], [128 * D, 32], [1, 128]])),
                            "g6A" if idx % 2 == 0 else "g6B", 16)
                        e.wait_ge(b.s("x2e"), idx + 1)
                        b.inc(e.dma_start(
                            out=ap(y_out.tensor,
                                   dc6 * 128 * S + hl * HT + ot * 512,
                                   [[S, 128], [1, 512]]),
                            in_=t_x2[:, dc6 * 512:(dc6 + 1) * 512]), "yo", 16)
            e.wait_ge(b.s("yo"), b.n("yo"))

        @blk.vector
        def _(e):
            def vfence(ins, tag="fn"):
                nfn = b.inc(ins, tag, 1)
                e.wait_ge(b.s(tag), nfn)
            for hl in range(2):
                for dc in range(32):
                    for j, (o5, w5) in enumerate(hts):
                        e.wait_ge(b.s("m5"), (hl * 32 + dc) * NJ + j + 1)
                        pm = p_a if ((hl * 32 + dc) * NJ + j) % 2 == 0 else p_b
                        ins = e.tensor_tensor(
                            a_h1[:, dc * HHW + o5:dc * HHW + o5 + w5],
                            pm[:, :w5],
                            ap(a_al1, hl * HT + o5, [[PF, 128], [1, w5]]),
                            ALU.mult)
                        b.inc(ins, "h1e", 1)
                        if j == NJ - 1:
                            vfence(ins if False else e.engine_nop())
                    base = dc * HHW
                    e.tensor_scalar(t_cv5[:, :], a_h1[:, base:base + HT],
                                    c_fcw[:, 3 * dc:3 * dc + 1],
                                    None, ALU.mult)
                    e.scalar_tensor_tensor(
                        t_cv5[:, :], a_h1[:, base + 1:base + 1 + HT],
                        c_fcw[:, 3 * dc + 1:3 * dc + 2], t_cv5[:, :],
                        ALU.mult, ALU.add)
                    e.scalar_tensor_tensor(
                        t_cv5[:, :], a_h1[:, base + 2:base + 2 + HT],
                        c_fcw[:, 3 * dc + 2:3 * dc + 3], t_cv5[:, :],
                        ALU.mult, ALU.add)
                    b.inc(e.engine_nop(), "cv5", 1)
                    e.wait_ge(b.s("cs5"), hl * 32 + dc + 1)
                    if dc == 0:
                        e.scalar_tensor_tensor(
                            t_mx2[:, :], a_h1[:, base:base + HT], -1.0,
                            a_h1[:, base:base + HT], ALU.mult, ALU.max)
                    else:
                        e.scalar_tensor_tensor(
                            t_cv5[:, :], a_h1[:, base:base + HT], -1.0,
                            a_h1[:, base:base + HT], ALU.mult, ALU.max)
                        e.tensor_tensor(t_mx2[:, :], t_mx2[:, :],
                                        t_cv5[:, :], ALU.max)
                    b.inc(e.engine_nop(), "mx2D", 1)
                for g in range(NG2):
                    e.wait_ge(b.s("t6"), hl * (NG2 + 1) + g + 1)
                    if g == 0:
                        vfence(e.tensor_scalar(t_cv5[:, :], t_cv5[:, :], 1.0,
                                               None, ALU.mult))
                    e.reduce_max(t_m2c[:, g:g + 1], p_t6[:, :],
                                 mybir.AxisListType.X)
                    b.inc(e.engine_nop(), "m2g", 1)
                e.wait_ge(b.s("t6"), hl * (NG2 + 1) + NG2 + 1)
                vfence(e.tensor_scalar(t_cv5[:, :], t_cv5[:, :], 1.0, None,
                                       ALU.mult))
                e.tensor_copy(t_m2r[:NG2, :], p_t6[:NG2, :])
                b.inc(e.engine_nop(), "m2rD", 1)
                e.wait_ge(b.s("fb2"), (hl + 1) * 16)
                e.tensor_scalar(t_g2[:, :], t_g2[:, :], EPS, None, ALU.max)
                e.tensor_scalar(t_al2[:, :], t_g2[:, :], c_w2m[:, :],
                                1.0 / 127.0, ALU.mult, ALU.mult)
                e.reciprocal(t_g2[:, :], t_g2[:, :])
                e.tensor_scalar(t_g2[:, :], t_g2[:, :], 127.0, None,
                                ALU.mult)
                b.inc(e.engine_nop(), "g2D", 1)
                for dc in range(32):
                    base = dc * HHW
                    e.tensor_tensor(t_cv5[:, :], a_h1[:, base:base + HT],
                                    t_g2[:, :], ALU.mult)
                    e.tensor_scalar(a_h1[:, base:base + HT],
                                    t_cv5[:, :], MAGIC, MAGIC,
                                    ALU.add, ALU.subtract)
                    b.inc(e.engine_nop(), "q2D", 1)
                for ot in range(NOT):
                    e.wait_ge(b.s("xr"), (hl * NOT + ot + 1) * 16)
                    for dc6 in range(8):
                        idx6 = (hl * NOT + ot) * 8 + dc6
                        e.wait_ge(b.s("m6"), idx6 + 1)
                        pm = p_a if idx6 % 2 == 0 else p_b
                        sl = slice(dc6 * 512, (dc6 + 1) * 512)
                        e.tensor_tensor(
                            t_x2[:, sl], pm[:, :],
                            ap(t_al2, ot * 512, [[HT, 128], [1, 512]]),
                            ALU.mult)
                        e.tensor_tensor(t_x2[:, sl], t_x2[:, sl],
                                        t_xr[:, sl], ALU.add)
                        b.inc(e.engine_nop(), "x2e", 1)
                    e.wait_ge(b.s("yo"), (hl * NOT + ot + 1) * 128)
                    b.inc(e.engine_nop(), "xru", 1)

        @blk.scalar
        def _(e):
            for hl in range(2):
                for dc in range(32):
                    e.wait_ge(b.s("cv5"), hl * 32 + dc + 1)
                    ins = e.activation(
                        a_h1[:, dc * HHW:dc * HHW + HT], t_cv5[:, :],
                        AF.Silu, bias=c_fcb[:, dc:dc + 1])
                    b.inc(ins, "cs5", 1)

        @blk.tensor
        def _(e):
            for hl in range(2):
                for dc in range(32):
                    for j, (o5, w5) in enumerate(hts):
                        idx5 = (hl * 32 + dc) * NJ + j
                        idxg = hl * 32 + dc
                        e.wait_ge(b.s("g5A" if idxg % 2 == 0 else "g5B"),
                                  (idxg // 2 + 1) * 16)
                        if hl == 0 and dc == 0 and j == 0:
                            e.wait_ge(b.s("q1ok"), 1)
                        if idx5 >= 2:
                            e.wait_ge(b.s("h1e"), idx5 - 1)
                        pm = p_a if idx5 % 2 == 0 else p_b
                        buf = t_r5A if dc % 2 == 0 else t_r5B
                        for k in range(8):
                            ins = e.matmul(
                                pm[:, :w5], buf[:, k * 128:(k + 1) * 128],
                                ap(a_q1, hl * HT + o5 + k * PF,
                                   [[8 * PF, 128], [1, w5]]),
                                start=(k == 0), stop=(k == 7))
                        b.inc(ins, "m5", 1)
                e.wait_ge(b.s("mx2D"), (hl + 1) * 32)
                for g in range(NG2):
                    if g == 0 and hl >= 1:
                        e.wait_ge(b.s("m2rD"), hl)
                    if hl * NG2 + g >= 1:
                        e.wait_ge(b.s("m2g"), hl * NG2 + g)
                    ins = e.transpose(p_t6[:, :],
                                      t_mx2[:, g * 128:(g + 1) * 128],
                                      c_id32[:, :])
                    b.inc(ins, "t6", 1)
                e.wait_ge(b.s("m2g"), (hl + 1) * NG2)
                ins = e.transpose(p_t6[:NG2, :], t_m2c[:, 0:NG2],
                                  c_id32[:, :])
                b.inc(ins, "t6", 1)
                for ot in range(NOT):
                    for dc6 in range(8):
                        idx6 = (hl * NOT + ot) * 8 + dc6
                        e.wait_ge(b.s("g6A" if idx6 % 2 == 0 else "g6B"),
                                  (idx6 // 2 + 1) * 16)
                        if idx6 >= 2:
                            e.wait_ge(b.s("x2e"), idx6 - 1)
                        pm = p_a if idx6 % 2 == 0 else p_b
                        buf = t_r6A if dc6 % 2 == 0 else t_r6B
                        for k2 in range(32):
                            if ot == 0 and dc6 == 0:
                                e.wait_ge(b.s("q2D"), hl * 32 + k2 + 1)
                            ins = e.matmul(
                                pm[:, :], buf[:, k2 * 128:(k2 + 1) * 128],
                                a_h1[:, k2 * HHW + ot * 512:
                                     k2 * HHW + ot * 512 + 512],
                                start=(k2 == 0), stop=(k2 == 31))
                        b.inc(ins, "m6", 1)

        @blk.gpsimd
        def _(e):
            b.inc(e.engine_nop(), "q1ok", 1)
            for hl in range(2):
                e.wait_ge(b.s("m2rD"), hl + 1)
                b.inc(e.dma_start(out=fl_sc[0:1, :HT], in_=t_m2r[:NG2, :]),
                      "fb1", 16)
                e.wait_ge(b.s("fb1"), (hl + 1) * 16)
                b.inc(e.dma_start(
                    out=t_g2[:, :],
                    in_=ap(fl_sc.tensor, 0, [[1, 1], [0, 128], [1, HT]])),
                    "fb2", 16)

    _close_semctrs()
    sDm.close()
    sD.close()
    es.close()
    return nc


def make_in_maps(inputs, S, n_cores):
    x = np.asarray(inputs["L0_attn_hidden_tokens"])[0]
    xT = np.ascontiguousarray(x.T, dtype=np.float32)
    base = {
        "w_in": np.ascontiguousarray(inputs["L0_attn_W_in"], np.float32),
        "conv_w": np.ascontiguousarray(inputs["L0_attn_conv_w"], np.float32),
        "conv_b": np.ascontiguousarray(inputs["L0_attn_conv_b"], np.float32).reshape(CONV, 1),
        "dt_bias": np.ascontiguousarray(inputs["L0_attn_dt_bias"], np.float32).reshape(NH, 1),
        "a_log": np.ascontiguousarray(inputs["L0_attn_A_log"], np.float32).reshape(NH, 1),
        "d_rep": np.ascontiguousarray(
            np.asarray(inputs["L0_attn_D_param"], np.float32)
            [2 * np.arange(16)[None, :] + (np.arange(128)[:, None] // 64)]),
        "norm_w": np.ascontiguousarray(inputs["L0_attn_norm_w"], np.float32).reshape(DIN, 1),
        "w_out": np.ascontiguousarray(inputs["L0_attn_W_out"], np.float32),
        "rms_w": np.ascontiguousarray(inputs["L0_attn_rms_w"], np.float32).reshape(D, 1),
        "mlp_w1": np.ascontiguousarray(inputs["L0_attn_mlp_w1"], np.float32),
        "mlp_b1": np.ascontiguousarray(inputs["L0_attn_mlp_b1"], np.float32).reshape(MLP, 1),
        "mlp_w2": np.ascontiguousarray(inputs["L0_attn_mlp_w2"], np.float32),
        "mlp_b2": np.ascontiguousarray(inputs["L0_attn_mlp_b2"], np.float32).reshape(D, 1),
        "w1t": np.ascontiguousarray(np.asarray(inputs["L0_ffn_w1"]).T, np.float32),
        "fconv_w": np.ascontiguousarray(
            np.asarray(inputs["L0_ffn_conv_w"]).reshape(FH, 3), np.float32),
        "fconv_b": np.ascontiguousarray(inputs["L0_ffn_conv_b"], np.float32).reshape(FH, 1),
        "w2t": np.ascontiguousarray(np.asarray(inputs["L0_ffn_w2"]).T, np.float32),
        "identity": np.eye(128, dtype=np.float32),
        "tri01": np.ascontiguousarray(
            np.tril(np.ones((128, 128), np.float32)).T),
    }
    maps = []
    for c in range(n_cores):
        lo = c * S
        halo = np.zeros((D, 3), np.float32) if c == 0 else xT[:, lo - 3:lo]
        x0 = np.ascontiguousarray(np.concatenate([halo, xT[:, lo:lo + S]], 1))
        kvec = np.zeros((n_cores, 1), np.float32)
        if c > 0:
            kvec[c - 1, 0] = 1.0
        mcol = np.full((n_cores, 1), NEG, np.float32)
        mcol[:c, 0] = 0.0
        selL = np.zeros((2 * n_cores, 1), np.float32)
        selR = np.zeros((2 * n_cores, 1), np.float32)
        if c > 0:
            selL[2 * (c - 1) + 1, 0] = 1.0
        if c < n_cores - 1:
            selR[2 * (c + 1), 0] = 1.0
        hmask = np.array([[0.0 if c == 0 else 1.0,
                           0.0 if c == n_cores - 1 else 1.0]], np.float32)
        m = dict(base)
        m.update({"x0": x0, "kvec": kvec, "maskcol": mcol,
                  "selL": selL, "selR": selR, "halo_mask": hmask})
        maps.append(m)
    return maps


_PROG = {}


def kernel(**inputs):
    S = S_FULL // N_CORES
    key = (S, N_CORES)
    if key not in _PROG:
        _PROG[key] = build_program(S, N_CORES)
    maps = make_in_maps(inputs, S, N_CORES)
    res = run_bass_kernel_spmd(_PROG[key], maps, core_ids=list(range(N_CORES)))
    parts = [np.asarray(r["y"]).T for r in res.results]
    return np.ascontiguousarray(np.concatenate(parts, 0)[None], np.float32)
